# version 05
# baseline (speedup 1.0000x reference)
import numpy as np
import ml_dtypes
from contextlib import ExitStack
import concourse.bass as bass
import concourse.mybir as mybir
from concourse.bass_utils import run_bass_kernel_spmd

F32, BF16 = mybir.dt.float32, mybir.dt.bfloat16
AF = mybir.ActivationFunctionType
ALU = mybir.AluOpType
AX = mybir.AxisListType.X

S, D = 4096, 1024
NH, HD = 16, 64
NE, FF = 8, 4096
DIN, DST = 2048, 64
CONVD = DIN + 2 * DST
DPROJ = 2 * DIN + 2 * DST + NH
CH, NCH = 256, 16
EPS = 1e-5

BUFS = {
    "xn0t_f32": ((S, D), F32),
    "xn0t_bf": ((S, D), BF16),
    "xn0c_bf": ((D, S), BF16),
    "ln2t_bf": ((S, D), BF16),
    "qkT": ((2 * D, S), BF16),
    "qn2d": ((NH, S), F32),
    "kn2d": ((NH, S), F32),
    "crow": ((NH, S), BF16),
    "vtok": ((NH, S, HD), BF16),
    "ctxc": ((D, S), BF16),
    "s0t": ((S, D), F32),
    "msb": ((S, 8), F32),
    "hbuf": ((NE, FF, S), BF16),
    "selt": ((S, D), F32),
    "x1t": ((S, D), F32),
    "xn1t_bf": ((S, D), BF16),
    "zc": ((DIN, S), F32),
    "xbc": ((CONVD, S), F32),
    "dtraw": ((NH, S), F32),
    "dtv": ((NH, S), F32),
    "acumd": ((NH, S), F32),
    "xs_bf": ((DIN, S), BF16),
    "xs_f32": ((DIN, S), F32),
    "bcbf": ((2 * DST, S), BF16),
    "ybuf": ((DIN, S), F32),
    "s1c": ((D, S), F32),
    "outt": ((S, D), F32),
    "junk": ((128, 4096), BF16),
}

INPUT_SHAPES = {
    "x_tok": (S, D), "gamr": (1, D), "betr": (1, D), "wqkvT": (D, 3 * D),
    "bqkvr": (1, 3 * D), "woutT": (D, D), "boutr": (1, D), "wred": (D, 16),
    "wg8": (8, 16), "w1e": (NE, D, FF), "b1c": (NE, 128, 32), "w2e": (NE, FF, D),
    "b2r": (NE, D), "winT": (D, DPROJ), "convw": (128, 17, 4), "convb": (128, 17),
    "dtb": (NH, 1), "alog": (NH, 1), "dpr": (1, NH), "normw": (128, 16),
    "woprojT": (DIN, D), "cbT": (D, D), "scbr": (1, D), "fbiasr": (1, D),
    "eyeb": (128, 128), "eyef": (128, 128), "hsel3": (16, 16 * 128),
    "trimask": (128, 2 * CH),
}
INPUT_DTYPES = {nm: (BF16 if nm == "eyeb" else F32) for nm in INPUT_SHAPES}


def prep_inputs(ins):
    f32 = np.float32
    A = np.ascontiguousarray
    sh = {}
    sh["gamr"] = A(ins["L0_attn_ln_gamma"].reshape(1, D), f32)
    sh["betr"] = A(ins["L0_attn_ln_beta"].reshape(1, D), f32)
    sh["wqkvT"] = A(np.transpose(ins["L0_attn_W_qkv"]), f32)
    sh["bqkvr"] = A(ins["L0_attn_b_qkv"].reshape(1, 3 * D), f32)
    sh["woutT"] = A(np.transpose(ins["L0_attn_W_out"]), f32)
    sh["boutr"] = A(ins["L0_attn_b_out"].reshape(1, D), f32)
    sh["wred"] = A(ins["L0_ffn_w_red"], f32)
    sh["wg8"] = A(ins["L0_ffn_wg"], f32)
    sh["w1e"] = A(ins["L0_ffn_w1"], f32)
    sh["b1c"] = A(ins["L0_ffn_b1"].reshape(NE, 32, 128).transpose(0, 2, 1), f32)
    sh["w2e"] = A(ins["L0_ffn_w2"], f32)
    sh["b2r"] = A(ins["L0_ffn_b2"], f32)
    sh["winT"] = A(np.transpose(ins["L1_attn_in_proj_w"]), f32)
    sh["convw"] = A(ins["L1_attn_conv_w"].reshape(17, 128, 4).transpose(1, 0, 2), f32)
    sh["convb"] = A(ins["L1_attn_conv_b"].reshape(17, 128).T, f32)
    sh["dtb"] = A(ins["L1_attn_dt_bias"].reshape(NH, 1), f32)
    sh["alog"] = A(ins["L1_attn_A_log"].reshape(NH, 1), f32)
    sh["dpr"] = A(ins["L1_attn_Dp"].reshape(1, NH), f32)
    sh["normw"] = A(ins["L1_attn_norm_w"].reshape(16, 128).T, f32)
    sh["woprojT"] = A(np.transpose(ins["L1_attn_out_proj_w"]), f32)
    sh["cbT"] = A(np.transpose(ins["L1_ffn_CB"]), f32)
    sh["scbr"] = A(ins["L1_ffn_SCB"].reshape(1, D), f32)
    sh["fbiasr"] = A(ins["L1_ffn_bias"].reshape(1, D), f32)
    sh["eyeb"] = np.eye(128).astype(ml_dtypes.bfloat16)
    sh["eyef"] = np.eye(128).astype(f32)
    hs = np.zeros((16, 16, 128), f32)
    for h in range(16):
        hs[h, h, :] = 1.0
    sh["hsel3"] = A(hs.transpose(1, 0, 2).reshape(16, 16 * 128))
    tm = np.zeros((2, 128, CH), f32)
    for j in range(2):
        for p in range(128):
            tm[j, p, : j * 128 + p] = -1e9
    sh["trimask"] = A(tm.transpose(1, 0, 2).reshape(128, 2 * CH))
    xs = [A(ins["L0_attn_hidden_states"][c], f32) for c in range(8)]
    return sh, xs


class Ctx:
    def __init__(self, nc, es):
        self.nc = nc
        self.es = es
        self.marks = {}
        self.b = {}
        self.i = {}
        self.sem = {}
        self.n = {}
        for s_ in ("qs", "qg", "qo", "mm", "ac", "ve", "gp"):
            self.sem[s_] = es.enter_context(nc.semaphore(s_))
            self.n[s_] = 0
        self.C = {}

    def snap(self):
        return dict(self.n)

    def wait_snap(self, eng, snap):
        for s_, v in snap.items():
            if v > 0:
                eng.wait_ge(self.sem[s_], v)

    def sb(self, es, name, shape, dtype, side=None):
        if side:
            return es.enter_context(self.nc.sbuf_tensor(name, shape, dtype, side=side))
        return es.enter_context(self.nc.sbuf_tensor(name, shape, dtype))

    def ps(self, es, name, shape, dtype=F32):
        return es.enter_context(self.nc.psum_tensor(name, shape, dtype))

    def sdma(self, eng, out, in_, qoc, **kw):
        eng.dma_start(out=out, in_=in_, **kw).then_inc(self.sem["qo"], 16)
        qoc += 16
        eng.wait_ge(self.sem["qo"], qoc)
        return qoc


def declare(nc, cx, io_kinds):
    for nm, shp in INPUT_SHAPES.items():
        cx.i[nm] = nc.dram_tensor(nm, list(shp), INPUT_DTYPES[nm], kind="ExternalInput").ap()
    for nm, (shp, dt) in BUFS.items():
        kind = io_kinds.get(nm, "Internal")
        cx.b[nm] = nc.dram_tensor(nm, list(shp), dt, kind=kind).ap()


def s0_prelude(cx):
    nc = cx.nc
    C = cx.C
    es = cx.es
    C["eyeb"] = cx.sb(es, "c_eyeb", [128, 128], BF16)
    C["eyef"] = cx.sb(es, "c_eyef", [128, 128], F32)
    C["hsel"] = cx.sb(es, "c_hsel", [16, 16 * 128], F32)
    C["trim"] = cx.sb(es, "c_trim", [128, 2 * CH], F32)
    C["ones_l"] = cx.sb(es, "c_onl", [1, 128], F32)
    C["ones_lb"] = cx.sb(es, "c_onlb", [1, 128], BF16)
    C["ones_c"] = cx.sb(es, "c_onc", [128, 1], F32)
    C["ones_cb"] = cx.sb(es, "c_oncb", [128, 1], BF16)
    C["ones_rb"] = cx.sb(es, "c_onrb", [1, 512], BF16)
    C["ones2"] = cx.sb(es, "c_on2", [128, 2], F32)
    C["ones2b"] = cx.sb(es, "c_on2b", [128, 2], BF16)
    C["dpb"] = cx.sb(es, "c_dpb", [128, NH], F32)
    C["cv"] = cx.sb(es, "c_cv", [128, 8], F32)
    C["dbgf"] = cx.sb(es, "c_dbgf", [128, 8], F32)
    C["dprow"] = cx.sb(es, "c_dprow", [1, NH], F32)
    pes = ExitStack()
    pb = cx.ps(pes, "p_pre", [128, 512])

    bar = cx.snap()
    with nc.Block() as blk:
        @blk.sync
        def _(s):
            qoc = cx.n["qo"]
            for dst, src in ((C["eyeb"][:], cx.i["eyeb"]), (C["eyef"][:], cx.i["eyef"]),
                             (C["hsel"][:], cx.i["hsel3"]), (C["trim"][:], cx.i["trimask"]),
                             (C["dprow"][:], cx.i["dpr"])):
                qoc = cx.sdma(s, dst, src, qoc)
        cx.n["qo"] += 16 * 5

        @blk.vector
        def _(v):
            v.memset(C["ones_l"][:], 1.0)
            v.memset(C["ones_lb"][:], 1.0)
            v.memset(C["ones_c"][:], 1.0)
            v.memset(C["ones_cb"][:], 1.0)
            v.memset(C["ones_rb"][:], 1.0)
            v.memset(C["cv"][:, 0:1], EPS)
            v.memset(C["cv"][:, 1:2], -0.5)
            v.memset(C["cv"][:, 2:3], 0.125)
            v.memset(C["cv"][:, 3:4], -1.0)
            v.memset(C["cv"][:, 4:5], 1.0 / 2048.0)
            v.memset(C["cv"][:, 5:6], 1.0)
            v.memset(C["ones2"][:], 0.0)
            v.memset(C["ones2"][0:64, 0:1], 1.0)
            v.memset(C["ones2"][64:128, 1:2], 1.0)
            v.memset(C["ones2b"][:], 0.0)
            v.memset(C["ones2b"][0:64, 0:1], 1.0)
            v.memset(C["ones2b"][64:128, 1:2], 1.0).then_inc(cx.sem["ve"], 1)
        cx.n["ve"] += 1

        @blk.tensor
        def _(t):
            t.wait_ge(cx.sem["qo"], cx.n["qo"])
            t.wait_ge(cx.sem["ve"], cx.n["ve"])
            t.matmul(pb[:, 0:NH], C["ones_l"][:], C["dprow"][:],
                     start=True, stop=True).then_inc(cx.sem["mm"], 1)
        cx.n["mm"] += 1

        @blk.vector
        def _(v):
            v.wait_ge(cx.sem["mm"], cx.n["mm"])
            v.tensor_copy(C["dpb"][:], pb[:, 0:NH]).then_inc(cx.sem["ve"], 1)
        cx.n["ve"] += 1
    pes.close()


def bcast_row_f32(cx, blk_t, blk_v, row_ap, out_sb, psum, ncols, m0, v0):
    raise NotImplementedError


def s1_ln0(cx):
    nc = cx.nc
    C = cx.C
    bar = cx.snap()
    es = ExitStack()
    NV = 14
    with es, nc.Block() as blk:
        xt = [cx.sb(es, f"s1x{i}", [128, D], F32) for i in range(2)]
        xn = [cx.sb(es, f"s1n{i}", [128, D], F32) for i in range(2)]
        l2 = [cx.sb(es, f"s1l{i}", [128, D], BF16) for i in range(2)]
        sq = [cx.sb(es, f"s1q{i}", [128, D], F32) for i in range(2)]
        st = cx.sb(es, "s1st", [128, 16], F32)
        t1 = cx.sb(es, "s1t1", [128, D], F32)
        cgam = cx.sb(es, "s1cg", [128, D], F32)
        cbet = cx.sb(es, "s1cb", [128, D], F32)
        grow = cx.sb(es, "s1gr", [1, D], F32)
        brow = cx.sb(es, "s1br", [1, D], F32)
        pb = cx.ps(es, "s1pb", [128, 1024])

        q0, g0, m0, a0, v0 = (cx.n[k] for k in ("qs", "qg", "mm", "ac", "ve"))

        o0 = cx.n["qo"]

        @blk.sync
        def _(s):
            cx.wait_snap(s, bar)
            qoc = o0
            qoc = cx.sdma(s, grow[:], cx.i["gamr"], qoc)
            qoc = cx.sdma(s, brow[:], cx.i["betr"], qoc)
            for tc in range(32):
                if tc >= 2:
                    s.wait_ge(cx.sem["ve"], v0 + 4 + (tc - 2) * NV + NV)
                s.dma_start(out=xt[tc % 2][:], in_=cx.i["x_tok"][tc * 128:(tc + 1) * 128, :]
                            ).then_inc(cx.sem["qs"], 16)

        @blk.tensor
        def _(t):
            cx.wait_snap(t, bar)
            t.wait_ge(cx.sem["qo"], o0 + 32)
            for half in range(2):
                t.matmul(pb[:, half * 512:(half + 1) * 512], C["ones_l"][:],
                         grow[:, half * 512:(half + 1) * 512], start=True, stop=True
                         ).then_inc(cx.sem["mm"], 1)
            t.wait_ge(cx.sem["ve"], v0 + 2)
            for half in range(2):
                t.matmul(pb[:, half * 512:(half + 1) * 512], C["ones_l"][:],
                         brow[:, half * 512:(half + 1) * 512], start=True, stop=True
                         ).then_inc(cx.sem["mm"], 1)

        @blk.vector
        def _(v):
            cx.wait_snap(v, bar)
            for half in range(2):
                v.wait_ge(cx.sem["mm"], m0 + half + 1)
                v.tensor_copy(cgam[:, half * 512:(half + 1) * 512],
                              pb[:, half * 512:(half + 1) * 512]).then_inc(cx.sem["ve"], 1)
            for half in range(2):
                v.wait_ge(cx.sem["mm"], m0 + 2 + half + 1)
                v.tensor_copy(cbet[:, half * 512:(half + 1) * 512],
                              pb[:, half * 512:(half + 1) * 512]).then_inc(cx.sem["ve"], 1)
            for tc in range(32):
                b = tc % 2
                vb = v0 + 4 + tc * NV
                ab = a0 + tc * 6
                v.wait_ge(cx.sem["qs"], q0 + (tc + 1) * 16)
                v.reduce_sum(st[:, 0:1], xt[b][:], axis=AX).then_inc(cx.sem["ve"], 1)
                v.wait_ge(cx.sem["ac"], ab + 1)
                v.reduce_sum(st[:, 1:2], sq[b][:], axis=AX).then_inc(cx.sem["ve"], 1)
                v.wait_ge(cx.sem["ve"], vb + 2)
                v.tensor_scalar(st[:, 2:4], st[:, 0:2], 1.0 / D, None, ALU.mult
                                ).then_inc(cx.sem["ve"], 1)
                v.wait_ge(cx.sem["ve"], vb + 3)
                v.tensor_mul(st[:, 4:5], st[:, 2:3], st[:, 2:3]).then_inc(cx.sem["ve"], 1)
                v.wait_ge(cx.sem["ve"], vb + 4)
                v.tensor_sub(st[:, 5:6], st[:, 3:4], st[:, 4:5]).then_inc(cx.sem["ve"], 1)
                v.wait_ge(cx.sem["ac"], ab + 3)
                if tc >= 2:
                    v.wait_ge(cx.sem["qg"], g0 + 16 * ((tc - 2) * 3 + 2))
                v.tensor_scalar(xn[b][:], xt[b][:], st[:, 2:3], st[:, 7:8],
                                ALU.subtract, ALU.mult).then_inc(cx.sem["ve"], 1)
                v.wait_ge(cx.sem["ve"], vb + 6)
                v.reduce_sum(st[:, 8:9], xn[b][:], axis=AX).then_inc(cx.sem["ve"], 1)
                v.wait_ge(cx.sem["ac"], ab + 4)
                v.reduce_sum(st[:, 9:10], sq[b][:], axis=AX).then_inc(cx.sem["ve"], 1)
                v.wait_ge(cx.sem["ve"], vb + 8)
                v.tensor_scalar(st[:, 10:12], st[:, 8:10], 1.0 / D, None, ALU.mult
                                ).then_inc(cx.sem["ve"], 1)
                v.wait_ge(cx.sem["ve"], vb + 9)
                v.tensor_mul(st[:, 12:13], st[:, 10:11], st[:, 10:11]).then_inc(cx.sem["ve"], 1)
                v.wait_ge(cx.sem["ve"], vb + 10)
                v.tensor_sub(st[:, 13:14], st[:, 11:12], st[:, 12:13]).then_inc(cx.sem["ve"], 1)
                v.wait_ge(cx.sem["ac"], ab + 6)
                v.tensor_scalar(t1[:], xn[b][:], st[:, 10:11], st[:, 15:16],
                                ALU.subtract, ALU.mult).then_inc(cx.sem["ve"], 1)
                v.wait_ge(cx.sem["ve"], vb + 12)
                v.tensor_mul(t1[:], t1[:], cgam[:]).then_inc(cx.sem["ve"], 1)
                v.wait_ge(cx.sem["ve"], vb + 13)
                if tc >= 2:
                    v.wait_ge(cx.sem["qg"], g0 + 16 * ((tc - 2) * 3 + 3))
                v.tensor_add(l2[b][:], t1[:], cbet[:]).then_inc(cx.sem["ve"], 1)

        @blk.scalar
        def _(a):
            cx.wait_snap(a, bar)
            for tc in range(32):
                b = tc % 2
                vb = v0 + 4 + tc * NV
                ab = a0 + tc * 6
                a.wait_ge(cx.sem["qs"], q0 + (tc + 1) * 16)
                if tc >= 2:
                    a.wait_ge(cx.sem["ve"], v0 + 4 + (tc - 2) * NV + 8)
                a.activation(sq[b][:], xt[b][:], AF.Square).then_inc(cx.sem["ac"], 1)
                a.wait_ge(cx.sem["ve"], vb + 5)
                a.activation(st[:, 6:7], st[:, 5:6], AF.Ln, bias=C["cv"][:, 0:1]).then_inc(cx.sem["ac"], 1)
                a.wait_ge(cx.sem["ac"], ab + 2)
                a.activation(st[:, 7:8], st[:, 6:7], AF.Exp, scale=C["cv"][:, 1:2]).then_inc(cx.sem["ac"], 1)
                a.wait_ge(cx.sem["ve"], vb + 6)
                a.activation(sq[b][:], xn[b][:], AF.Square).then_inc(cx.sem["ac"], 1)
                a.wait_ge(cx.sem["ve"], vb + 11)
                a.activation(st[:, 14:15], st[:, 13:14], AF.Ln, bias=C["cv"][:, 0:1]).then_inc(cx.sem["ac"], 1)
                a.wait_ge(cx.sem["ac"], ab + 5)
                a.activation(st[:, 15:16], st[:, 14:15], AF.Exp, scale=C["cv"][:, 1:2]).then_inc(cx.sem["ac"], 1)

        @blk.gpsimd
        def _(g):
            cx.wait_snap(g, bar)
            for tc in range(32):
                b = tc % 2
                vb = v0 + 4 + tc * NV
                rows = slice(tc * 128, (tc + 1) * 128)
                g.wait_ge(cx.sem["ve"], vb + 6)
                g.dma_start(out=cx.b["xn0t_f32"][rows, :], in_=xn[b][:]).then_inc(cx.sem["qg"], 16)
                g.dma_start(out=cx.b["xn0t_bf"][rows, :], in_=xn[b][:]).then_inc(cx.sem["qg"], 16)
                g.wait_ge(cx.sem["ve"], vb + NV)
                g.dma_start(out=cx.b["ln2t_bf"][rows, :], in_=l2[b][:]).then_inc(cx.sem["qg"], 16)

        cx.n["qs"] = q0 + 32 * 16
        cx.n["qo"] = o0 + 32
        cx.n["qg"] = g0 + 96 * 16
        cx.n["mm"] = m0 + 4
        cx.n["ac"] = a0 + 32 * 6
        cx.n["ve"] = v0 + 4 + 32 * NV


def s2_qkv(cx):
    nc = cx.nc
    C = cx.C
    bar = cx.snap()
    es = ExitStack()
    with es, nc.Block() as blk:
        ln2T = cx.sb(es, "s2ln", [128, 8 * S], BF16)
        wsb = cx.sb(es, "s2w", [128, 8 * 3 * D], BF16)
        bqb = cx.sb(es, "s2bq", [1, 3 * D], BF16)
        qk = [cx.sb(es, f"s2qk{i}", [128, 512], BF16) for i in range(2)]
        vst = [cx.sb(es, f"s2v{i}", [128, 512], BF16) for i in range(2)]
        p2 = [cx.ps(es, f"s2p{i}", [128, 512]) for i in range(2)]
        pv = [cx.ps(es, f"s2pv{i}", [128, 512]) for i in range(2)]

        q0, g0, m0, a0, v0 = (cx.n[k] for k in ("qs", "qg", "mm", "ac", "ve"))
        NG = 16 * 8
        NV = 32 * 2

        @blk.sync
        def _(s):
            cx.wait_snap(s, bar)
            for dc in range(8):
                s.dma_start(out=ln2T[:, dc * S:(dc + 1) * S],
                            in_=cx.b["ln2t_bf"][:, dc * 128:(dc + 1) * 128],
                            transpose=True).then_inc(cx.sem["qs"], 16)

        o0 = cx.n["qo"]

        @blk.gpsimd
        def _(g):
            cx.wait_snap(g, bar)
            g.dma_start(out=wsb.rearrange("p (c o) -> p c o", c=8),
                        in_=cx.i["wqkvT"].rearrange("(c p) o -> p c o", c=8)
                        ).then_inc(cx.sem["qg"], 16)
            cx.sdma(g, bqb[:], cx.i["bqkvr"], o0)
            for g_i in range(NG):
                g.wait_ge(cx.sem["ve"], v0 + g_i + 1)
                oc, tt = divmod(g_i, 8)
                g.dma_start(out=cx.b["qkT"][oc * 128:(oc + 1) * 128, tt * 512:(tt + 1) * 512],
                            in_=qk[g_i % 2][:]).then_inc(cx.sem["qg"], 16)
            for gv in range(NV):
                g.wait_ge(cx.sem["ve"], v0 + NG + gv + 1)
                tc, oh = divmod(gv, 2)
                g.dma_start(
                    out=cx.b["vtok"].rearrange("h t d -> t h d")[
                        tc * 128:(tc + 1) * 128, oh * 8:(oh + 1) * 8, :],
                    in_=vst[gv % 2].rearrange("p (h d) -> p h d", h=8)
                ).then_inc(cx.sem["qg"], 16)

        @blk.tensor
        def _(t):
            cx.wait_snap(t, bar)
            t.wait_ge(cx.sem["qs"], q0 + 8 * 16)
            t.wait_ge(cx.sem["qg"], g0 + 16)
            t.wait_ge(cx.sem["qo"], o0 + 16)
            for g_i in range(NG):
                oc, tt = divmod(g_i, 8)
                if g_i >= 2:
                    t.wait_ge(cx.sem["ve"], v0 + (g_i - 2) + 1)
                for dc in range(8):
                    t.matmul(p2[g_i % 2][:],
                             wsb[:, dc * 3072 + oc * 128:dc * 3072 + (oc + 1) * 128],
                             ln2T[:, dc * S + tt * 512:dc * S + (tt + 1) * 512],
                             start=(dc == 0), stop=False)
                t.matmul(p2[g_i % 2][:], bqb[0:1, oc * 128:(oc + 1) * 128],
                         C["ones_rb"][:], start=False, stop=True).then_inc(cx.sem["mm"], 1)
            for gv in range(NV):
                tc, oh = divmod(gv, 2)
                if gv >= 2:
                    t.wait_ge(cx.sem["ve"], v0 + NG + (gv - 2) + 1)
                for dc in range(8):
                    t.matmul(pv[gv % 2][:],
                             ln2T[:, dc * S + tc * 128:dc * S + (tc + 1) * 128],
                             wsb[:, dc * 3072 + 2048 + oh * 512:dc * 3072 + 2048 + (oh + 1) * 512],
                             start=(dc == 0), stop=False)
                t.matmul(pv[gv % 2][:], C["ones_lb"][:],
                         bqb[0:1, 2048 + oh * 512:2048 + (oh + 1) * 512],
                         start=False, stop=True).then_inc(cx.sem["mm"], 1)

        @blk.vector
        def _(v):
            cx.wait_snap(v, bar)
            for g_i in range(NG):
                v.wait_ge(cx.sem["mm"], m0 + g_i + 1)
                if g_i >= 2:
                    v.wait_ge(cx.sem["qg"], g0 + 16 + 16 * (g_i - 2 + 1))
                v.tensor_copy(qk[g_i % 2][:], p2[g_i % 2][:]).then_inc(cx.sem["ve"], 1)
            for gv in range(NV):
                v.wait_ge(cx.sem["mm"], m0 + NG + gv + 1)
                if gv >= 2:
                    v.wait_ge(cx.sem["qg"], g0 + 16 + 16 * NG + 16 * (gv - 2 + 1))
                v.tensor_copy(vst[gv % 2][:], pv[gv % 2][:]).then_inc(cx.sem["ve"], 1)

        cx.n["qs"] = q0 + 8 * 16
        cx.n["qg"] = g0 + 16 + 16 * (NG + NV)
        cx.n["qo"] = o0 + 16
        cx.n["mm"] = m0 + NG + NV
        cx.n["ve"] = v0 + NG + NV


def s2b_norms(cx):
    nc = cx.nc
    C = cx.C
    bar = cx.snap()
    es = ExitStack()
    with es, nc.Block() as blk:
        qkc = [cx.sb(es, f"s2bq{i}", [128, S], BF16) for i in range(2)]
        sqc = [cx.sb(es, f"s2bs{i}", [128, S], BF16) for i in range(2)]
        n2q = cx.sb(es, "s2bnq", [2, 8 * S], BF16)
        n2k = cx.sb(es, "s2bnk", [2, 8 * S], BF16)
        km = cx.sb(es, "s2bkm", [2, 16], F32)
        pn = [cx.ps(es, f"s2bp{i}", [2, 512]) for i in range(2)]
        q0, g0, o0, m0, a0, v0 = (cx.n[k] for k in ("qs", "qg", "qo", "mm", "ac", "ve"))

        @blk.sync
        def _(s):
            cx.wait_snap(s, bar)
            for c in range(16):
                if c >= 2:
                    s.wait_ge(cx.sem["ac"], a0 + (c - 2) + 1)
                s.dma_start(out=qkc[c % 2][:], in_=cx.b["qkT"][c * 128:(c + 1) * 128, :]
                            ).then_inc(cx.sem["qs"], 16)

        @blk.scalar
        def _(a):
            cx.wait_snap(a, bar)
            for c in range(16):
                a.wait_ge(cx.sem["qs"], q0 + (c + 1) * 16)
                if c >= 2:
                    a.wait_ge(cx.sem["mm"], m0 + (c - 2) * 8 + 8)
                a.activation(sqc[c % 2][:], qkc[c % 2][:], AF.Square).then_inc(cx.sem["ac"], 1)
            a.wait_ge(cx.sem["ve"], v0 + 128 + 8)
            a.activation(km[:, 8:16], km[:, 0:8], AF.Sqrt).then_inc(cx.sem["ac"], 1)
            a.activation(n2q[:], n2q[:], AF.Sqrt).then_inc(cx.sem["ac"], 1)

        @blk.tensor
        def _(t):
            cx.wait_snap(t, bar)
            for c in range(16):
                t.wait_ge(cx.sem["ac"], a0 + c + 1)
                for tt in range(8):
                    i = c * 8 + tt
                    if i >= 2:
                        t.wait_ge(cx.sem["ve"], v0 + (i - 2) + 1)
                    t.matmul(pn[i % 2][:], C["ones2b"][:], sqc[c % 2][:, tt * 512:(tt + 1) * 512],
                             start=True, stop=True).then_inc(cx.sem["mm"], 1)

        @blk.vector
        def _(v):
            cx.wait_snap(v, bar)
            for c in range(16):
                for tt in range(8):
                    i = c * 8 + tt
                    v.wait_ge(cx.sem["mm"], m0 + i + 1)
                    tgt = n2q if c < 8 else n2k
                    c2 = c % 8
                    v.tensor_copy(tgt[:, c2 * S + tt * 512:c2 * S + (tt + 1) * 512],
                                  pn[i % 2][:]).then_inc(cx.sem["ve"], 1)
            for c2 in range(8):
                v.wait_ge(cx.sem["ve"], v0 + 128)
                v.reduce_max(km[:, c2:c2 + 1], n2k[:, c2 * S:(c2 + 1) * S], axis=AX
                             ).then_inc(cx.sem["ve"], 1)
            v.wait_ge(cx.sem["ac"], a0 + 18)
            for c2 in range(8):
                v.tensor_scalar(n2q[:, c2 * S:(c2 + 1) * S], n2q[:, c2 * S:(c2 + 1) * S],
                                km[0:2, 8 + c2:9 + c2], None, ALU.mult).then_inc(cx.sem["ve"], 1)

        @blk.gpsimd
        def _(g):
            cx.wait_snap(g, bar)
            g.wait_ge(cx.sem["ve"], v0 + 128 + 8 + 8)
            qoc = o0
            for h in range(NH):
                qoc = cx.sdma(g, cx.b["crow"][h:h + 1, :],
                              n2q[h % 2:h % 2 + 1, (h // 2) * S:(h // 2 + 1) * S], qoc)

        cx.n["qs"] = q0 + 16 * 16
        cx.n["qo"] = o0 + 16 * 16
        cx.n["mm"] = m0 + 128
        cx.n["ac"] = a0 + 18
        cx.n["ve"] = v0 + 128 + 16


def s3_attn(cx):
    nc = cx.nc
    C = cx.C
    bar = cx.snap()
    es = ExitStack()
    with es, nc.Block() as blk:
        Qs = [cx.sb(es, f"s3q{i}", [65, S], BF16) for i in range(2)]
        Ks = [cx.sb(es, f"s3k{i}", [65, S], BF16) for i in range(2)]
        Vs = [cx.sb(es, f"s3v{i}", [128, 32 * 65], BF16) for i in range(2)]
        probs = [cx.sb(es, f"s3p{i}", [128, 16 * 1024], BF16) for i in range(2)]
        ctx = cx.sb(es, "s3ctx", [128, 8 * S], BF16)
        rl = cx.sb(es, "s3rl", [1, 512], F32)
        rlb = [cx.sb(es, f"s3rlb{i}", [64, 512], F32) for i in range(2)]
        ps_s = [cx.ps(es, f"s3ps{i}", [128, 1024]) for i in range(2)]
        ps_c = [cx.ps(es, f"s3pc{i}", [65, 512]) for i in range(2)]
        ps_b = cx.ps(es, "s3pb", [64, 512])

        q0, g0, o0, m0, a0, v0 = (cx.n[k] for k in ("qs", "qg", "qo", "mm", "ac", "ve"))
        VPRE = 4
        NGQ = NH * 8

        def sc_idx(gq, g2):
            return m0 + g2 + 1 if gq == 0 else m0 + 16 + (gq - 1) * 18 + g2 + 1

        def ctx_idx(gq):
            if gq < NGQ - 1:
                return m0 + 16 + gq * 18 + 17
            return m0 + 16 + gq * 18 + 1

        def bc_idx(gq):
            return ctx_idx(gq) + 1

        ac_exp = lambda gq, g2: a0 + gq * 16 + g2 + 1
        ve_recip = lambda gq: v0 + VPRE + gq * 3 + 1
        ve_rlb = lambda gq: v0 + VPRE + gq * 3 + 2
        ve_cmul = lambda gq: v0 + VPRE + gq * 3 + 3

        def emit_scores(t, gq):
            h, qt = divmod(gq, 8)
            hb = h % 2
            t.wait_ge(cx.sem["qs"], q0 + (h + 1) * 16)
            t.wait_ge(cx.sem["qo"], o0 + (h + 1) * 48)
            for g2 in range(16):
                gg = gq * 16 + g2
                if gg >= 2:
                    pg = gg - 2
                    t.wait_ge(cx.sem["ac"], ac_exp(pg // 16, pg % 16))
                inst = None
                for j in range(2):
                    kk = g2 * 2 + j
                    inst = t.matmul(ps_s[g2 % 2][:, j * 512:(j + 1) * 512],
                                    Ks[hb][:, kk * 128:(kk + 1) * 128],
                                    Qs[hb][:, qt * 512:(qt + 1) * 512],
                                    start=True, stop=True)
                inst.then_inc(cx.sem["mm"], 1)

        @blk.sync
        def _(s):
            cx.wait_snap(s, bar)
            qoc = o0
            for h in range(NH):
                hb = h % 2
                if h >= 2:
                    s.wait_ge(cx.sem["mm"], bc_idx(8 * (h - 1) - 1))
                    s.wait_ge(cx.sem["ve"], v0 + VPRE + (8 * (h - 1)) * 3)
                qoc = cx.sdma(s, Qs[hb][0:64, :], cx.b["qkT"][h * 64:(h + 1) * 64, :], qoc)
                qoc = cx.sdma(s, Qs[hb][64:65, :], cx.b["crow"][h:h + 1, :], qoc)
                qoc = cx.sdma(s, Ks[hb][0:64, :], cx.b["qkT"][1024 + h * 64:1024 + (h + 1) * 64, :], qoc)
                s.dma_start(out=Vs[hb].rearrange("p (c w) -> p c w", w=65)[:, :, 0:64],
                            in_=cx.b["vtok"][h].rearrange("(c p) d -> p c d", p=128)
                            ).then_inc(cx.sem["qs"], 16)

        @blk.vector
        def _(v):
            cx.wait_snap(v, bar)
            v.memset(Ks[0][64:65, :], -1.0).then_inc(cx.sem["ve"], 1)
            v.memset(Ks[1][64:65, :], -1.0).then_inc(cx.sem["ve"], 1)
            v.memset(Vs[0].rearrange("p (c w) -> p c w", w=65)[:, :, 64:65], 1.0).then_inc(cx.sem["ve"], 1)
            v.memset(Vs[1].rearrange("p (c w) -> p c w", w=65)[:, :, 64:65], 1.0).then_inc(cx.sem["ve"], 1)
            for gq in range(NGQ):
                v.wait_ge(cx.sem["mm"], ctx_idx(gq))
                v.reciprocal(rl[:], ps_c[gq % 2][64:65, :]).then_inc(cx.sem["ve"], 1)
                v.wait_ge(cx.sem["mm"], bc_idx(gq))
                v.tensor_copy(rlb[gq % 2][:], ps_b[:]).then_inc(cx.sem["ve"], 1)
                v.wait_ge(cx.sem["ve"], ve_rlb(gq))
                h, qt = divmod(gq, 8)
                v.tensor_mul(
                    ctx[(h % 2) * 64:(h % 2) * 64 + 64,
                        (h // 2) * S + qt * 512:(h // 2) * S + (qt + 1) * 512],
                    ps_c[gq % 2][0:64, :], rlb[gq % 2][:]).then_inc(cx.sem["ve"], 1)

        @blk.scalar
        def _(a):
            cx.wait_snap(a, bar)
            for gq in range(NGQ):
                if gq >= 2:
                    a.wait_ge(cx.sem["mm"], ctx_idx(gq - 2))
                for g2 in range(16):
                    a.wait_ge(cx.sem["mm"], sc_idx(gq, g2))
                    a.activation(probs[gq % 2][:, g2 * 1024:(g2 + 1) * 1024],
                                 ps_s[g2 % 2][:], AF.Exp, scale=C["cv"][:, 2:3]
                                 ).then_inc(cx.sem["ac"], 1)

        @blk.tensor
        def _(t):
            cx.wait_snap(t, bar)
            t.wait_ge(cx.sem["ve"], v0 + VPRE)
            emit_scores(t, 0)
            for gq in range(NGQ):
                h, qt = divmod(gq, 8)
                hb = h % 2
                nh, nqt = divmod(gq + 1, 8)
                nhb = nh % 2
                if gq + 1 < NGQ:
                    t.wait_ge(cx.sem["qs"], q0 + (nh + 1) * 16)
                    t.wait_ge(cx.sem["qo"], o0 + (nh + 1) * 48)
                inst = None
                for g2 in range(16):
                    if gq + 1 < NGQ:
                        gg = (gq + 1) * 16 + g2
                        pg = gg - 2
                        t.wait_ge(cx.sem["ac"], ac_exp(pg // 16, pg % 16))
                        for j in range(2):
                            kk = g2 * 2 + j
                            inst2 = t.matmul(ps_s[g2 % 2][:, j * 512:(j + 1) * 512],
                                             Ks[nhb][:, kk * 128:(kk + 1) * 128],
                                             Qs[nhb][:, nqt * 512:(nqt + 1) * 512],
                                             start=True, stop=True)
                        inst2.then_inc(cx.sem["mm"], 1)
                    if g2 == 0:
                        t.wait_ge(cx.sem["ac"], ac_exp(gq, 0))
                        if gq >= 2:
                            t.wait_ge(cx.sem["ve"], ve_cmul(gq - 2))
                    else:
                        t.wait_ge(cx.sem["ac"], ac_exp(gq, g2))
                    for j in (g2 * 2, g2 * 2 + 1):
                        inst = t.matmul(ps_c[gq % 2][:], Vs[hb][:, j * 65:(j + 1) * 65],
                                        probs[gq % 2][:, j * 512:(j + 1) * 512],
                                        start=(j == 0), stop=(j == 31))
                inst.then_inc(cx.sem["mm"], 1)
                t.wait_ge(cx.sem["ve"], ve_recip(gq))
                if gq >= 1:
                    t.wait_ge(cx.sem["ve"], ve_rlb(gq - 1))
                t.matmul(ps_b[:], C["ones_l"][0:1, 0:64], rl[:],
                         start=True, stop=True).then_inc(cx.sem["mm"], 1)

        @blk.gpsimd
        def _(g):
            cx.wait_snap(g, bar)
            g.wait_ge(cx.sem["ve"], v0 + VPRE + NGQ * 3)
            for cc in range(8):
                g.dma_start(out=cx.b["ctxc"][cc * 128:(cc + 1) * 128, :],
                            in_=ctx[:, cc * S:(cc + 1) * S]).then_inc(cx.sem["qg"], 16)

        cx.n["qs"] = q0 + NH * 16
        cx.n["qo"] = o0 + NH * 48
        cx.n["qg"] = g0 + 8 * 16
        cx.n["mm"] = m0 + 16 + (NGQ - 1) * 18 + 2
        cx.n["ac"] = a0 + NGQ * 16
        cx.n["ve"] = v0 + VPRE + NGQ * 3


def s4_attnout(cx):
    nc = cx.nc
    C = cx.C
    bar = cx.snap()
    es = ExitStack()
    with es, nc.Block() as blk:
        ctxf = cx.sb(es, "s4cx", [128, 8 * S], BF16)
        wo = cx.sb(es, "s4w", [128, 8 * D], BF16)
        bo = cx.sb(es, "s4b", [1, D], BF16)
        xnt = [cx.sb(es, f"s4x{i}", [128, D], F32) for i in range(2)]
        s0s = [cx.sb(es, f"s4s{i}", [128, D], F32) for i in range(2)]
        po = [cx.ps(es, f"s4p{i}", [128, 1024]) for i in range(2)]

        q0, g0, m0, a0, v0 = (cx.n[k] for k in ("qs", "qg", "mm", "ac", "ve"))

        @blk.sync
        def _(s):
            cx.wait_snap(s, bar)
            for cc in range(8):
                s.dma_start(out=ctxf[:, cc * S:(cc + 1) * S],
                            in_=cx.b["ctxc"][cc * 128:(cc + 1) * 128, :]).then_inc(cx.sem["qs"], 16)
            for tc in range(32):
                if tc >= 2:
                    s.wait_ge(cx.sem["ve"], v0 + (tc - 2) + 1)
                s.dma_start(out=xnt[tc % 2][:],
                            in_=cx.b["xn0t_f32"][tc * 128:(tc + 1) * 128, :]).then_inc(cx.sem["qs"], 16)

        @blk.gpsimd
        def _(g):
            cx.wait_snap(g, bar)
            g.dma_start(out=wo.rearrange("p (c o) -> p c o", c=8),
                        in_=cx.i["woutT"].rearrange("(c p) o -> p c o", c=8)).then_inc(cx.sem["qg"], 16)
            g.dma_start(out=bo[:], in_=cx.i["boutr"]).then_inc(cx.sem["qg"], 16)
            for tc in range(32):
                g.wait_ge(cx.sem["ve"], v0 + tc + 1)
                g.dma_start(out=cx.b["s0t"][tc * 128:(tc + 1) * 128, :],
                            in_=s0s[tc % 2][:]).then_inc(cx.sem["qg"], 16)

        @blk.tensor
        def _(t):
            cx.wait_snap(t, bar)
            t.wait_ge(cx.sem["qs"], q0 + 8 * 16)
            t.wait_ge(cx.sem["qg"], g0 + 32)
            for tc in range(32):
                if tc >= 2:
                    t.wait_ge(cx.sem["ve"], v0 + (tc - 2) + 1)
                for dh in range(2):
                    for cc in range(8):
                        t.matmul(po[tc % 2][:, dh * 512:(dh + 1) * 512],
                                 ctxf[:, cc * S + tc * 128:cc * S + (tc + 1) * 128],
                                 wo[:, cc * D + dh * 512:cc * D + (dh + 1) * 512],
                                 start=(cc == 0), stop=False)
                    t.matmul(po[tc % 2][:, dh * 512:(dh + 1) * 512],
                             C["ones_lb"][:], bo[0:1, dh * 512:(dh + 1) * 512],
                             start=False, stop=True).then_inc(cx.sem["mm"], 1)

        @blk.vector
        def _(v):
            cx.wait_snap(v, bar)
            for tc in range(32):
                v.wait_ge(cx.sem["mm"], m0 + tc * 2 + 2)
                v.wait_ge(cx.sem["qs"], q0 + 8 * 16 + (tc + 1) * 16)
                if tc >= 2:
                    v.wait_ge(cx.sem["qg"], g0 + 32 + 16 * (tc - 2 + 1))
                v.tensor_add(s0s[tc % 2][:], po[tc % 2][:], xnt[tc % 2][:]).then_inc(cx.sem["ve"], 1)

        cx.n["qs"] = q0 + 40 * 16
        cx.n["qg"] = g0 + 34 * 16
        cx.n["mm"] = m0 + 64
        cx.n["ve"] = v0 + 32


def s5_route(cx):
    nc = cx.nc
    C = cx.C
    bar = cx.snap()
    es = ExitStack()
    NV = 23
    with es, nc.Block() as blk:
        xnt = [cx.sb(es, f"s5x{i}", [128, D], F32) for i in range(2)]
        xnT = cx.sb(es, "s5xT", [128, 8 * 128], F32)
        wr = cx.sb(es, "s5wr", [128, 8 * 16], F32)
        wg = cx.sb(es, "s5wg", [8, 16], F32)
        wst = cx.sb(es, "s5ws", [8, 20], F32)
        wgn = cx.sb(es, "s5wgn", [8, 16], F32)
        wgnT = cx.sb(es, "s5wgT", [16, 8], F32)
        redsb = cx.sb(es, "s5rd", [16, 128], F32)
        lg = cx.sb(es, "s5lg", [128, 8], F32)
        el = cx.sb(es, "s5el", [128, 8], F32)
        stt_ = cx.sb(es, "s5st", [128, 4], F32)
        m8 = cx.sb(es, "s5m8", [128, 8], F32)
        i8u = cx.sb(es, "s5i8", [128, 8], mybir.dt.uint32)
        msall = cx.sb(es, "s5ms", [128, 32 * 8], F32)
        ps_t = [cx.ps(es, f"s5pt{i}", [128, 128]) for i in range(2)]
        ps_r = cx.ps(es, "s5pr", [16, 128])
        ps_l = cx.ps(es, "s5pl", [128, 8])

        q0, g0, o0, m0, a0, v0 = (cx.n[k] for k in ("qs", "qg", "qo", "mm", "ac", "ve"))
        VP, AP_, MP = 5, 2, 1

        @blk.sync
        def _(s):
            cx.wait_snap(s, bar)
            qoc = cx.sdma(s, wg[:], cx.i["wg8"], o0)
            for tc in range(32):
                if tc >= 2:
                    s.wait_ge(cx.sem["mm"], m0 + MP + (tc - 2) * 10 + 8)
                s.dma_start(out=xnt[tc % 2][:], in_=cx.b["xn0t_f32"][tc * 128:(tc + 1) * 128, :]
                            ).then_inc(cx.sem["qs"], 16)

        @blk.gpsimd
        def _(g):
            cx.wait_snap(g, bar)
            g.dma_start(out=wr.rearrange("p (c g) -> p c g", c=8),
                        in_=cx.i["wred"].rearrange("(c p) g -> p c g", c=8)).then_inc(cx.sem["qg"], 16)
            for tc in range(32):
                g.wait_ge(cx.sem["ve"], v0 + VP + tc * NV + 8)
                g.dma_start(out=cx.b["xn0c_bf"].rearrange("(c p) t -> p c t", p=128)[
                                :, :, tc * 128:(tc + 1) * 128],
                            in_=xnT.rearrange("p (c t) -> p c t", c=8)).then_inc(cx.sem["qg"], 16)
            g.wait_ge(cx.sem["ve"], v0 + VP + 32 * NV)
            g.dma_start(out=cx.b["msb"].rearrange("(c p) e -> p c e", p=128),
                        in_=msall.rearrange("p (c e) -> p c e", e=8)).then_inc(cx.sem["qg"], 16)

        @blk.scalar
        def _(a):
            cx.wait_snap(a, bar)
            a.wait_ge(cx.sem["qo"], o0 + 16)
            a.activation(wst[:, 0:16], wg[:], AF.Square).then_inc(cx.sem["ac"], 1)
            a.wait_ge(cx.sem["ve"], v0 + 1)
            a.activation(wst[:, 17:18], wst[:, 16:17], AF.Sqrt).then_inc(cx.sem["ac"], 1)
            for tc in range(32):
                ab = a0 + AP_ + tc
                vb = v0 + VP + tc * NV
                a.wait_ge(cx.sem["ve"], vb + 11)
                a.activation(el[:], lg[:], AF.Exp).then_inc(cx.sem["ac"], 1)

        @blk.tensor
        def _(t):
            cx.wait_snap(t, bar)
            t.wait_ge(cx.sem["ve"], v0 + 4)
            t.transpose(ps_l[0:16, 0:8], wgn[:], C["eyef"][0:8, 0:8]).then_inc(cx.sem["mm"], 1)
            t.wait_ge(cx.sem["qg"], g0 + 16)
            for tc in range(32):
                mb = m0 + MP + tc * 10
                vb = v0 + VP + tc * NV
                t.wait_ge(cx.sem["qs"], q0 + (tc + 1) * 16)
                for dc in range(8):
                    i = tc * 8 + dc
                    if i >= 2:
                        pv = i - 2
                        t.wait_ge(cx.sem["ve"], v0 + VP + (pv // 8) * NV + (pv % 8) + 1)
                    t.transpose(ps_t[dc % 2][:], xnt[tc % 2][:, dc * 128:(dc + 1) * 128],
                                C["eyef"][:]).then_inc(cx.sem["mm"], 1)
                if tc >= 1:
                    t.wait_ge(cx.sem["ve"], v0 + VP + (tc - 1) * NV + 9)
                inst = None
                for dc in range(8):
                    t.wait_ge(cx.sem["ve"], vb + dc + 1)
                    inst = t.matmul(ps_r[:], wr[:, dc * 16:(dc + 1) * 16], xnT[:, dc * 128:(dc + 1) * 128],
                                    start=(dc == 0), stop=(dc == 7))
                inst.then_inc(cx.sem["mm"], 1)
                t.wait_ge(cx.sem["ve"], vb + 9)
                if tc >= 1:
                    t.wait_ge(cx.sem["ve"], v0 + VP + (tc - 1) * NV + 11)
                t.matmul(ps_l[:], redsb[:], wgnT[:], start=True, stop=True).then_inc(cx.sem["mm"], 1)

        @blk.vector
        def _(v):
            cx.wait_snap(v, bar)
            v.wait_ge(cx.sem["ac"], a0 + 1)
            v.reduce_sum(wst[:, 16:17], wst[:, 0:16], axis=AX).then_inc(cx.sem["ve"], 1)
            v.wait_ge(cx.sem["ac"], a0 + 2)
            v.tensor_scalar(wst[:, 18:19], wst[:, 17:18], 1e-4, None, ALU.max).then_inc(cx.sem["ve"], 1)
            v.wait_ge(cx.sem["ve"], v0 + 2)
            v.reciprocal(wst[:, 19:20], wst[:, 18:19]).then_inc(cx.sem["ve"], 1)
            v.wait_ge(cx.sem["ve"], v0 + 3)
            v.tensor_scalar(wgn[:], wg[:], wst[:, 19:20], None, ALU.mult).then_inc(cx.sem["ve"], 1)
            v.wait_ge(cx.sem["mm"], m0 + 1)
            v.tensor_copy(wgnT[:], ps_l[0:16, 0:8]).then_inc(cx.sem["ve"], 1)
            for tc in range(32):
                mb = m0 + MP + tc * 10
                vb = v0 + VP + tc * NV
                ab = a0 + AP_ + tc
                if tc >= 1:
                    v.wait_ge(cx.sem["qg"], g0 + 16 + tc * 16)
                for dc in range(8):
                    v.wait_ge(cx.sem["mm"], mb + dc + 1)
                    v.tensor_copy(xnT[:, dc * 128:(dc + 1) * 128], ps_t[dc % 2][:]
                                  ).then_inc(cx.sem["ve"], 1)
                v.wait_ge(cx.sem["mm"], mb + 9)
                v.tensor_copy(redsb[:], ps_r[:]).then_inc(cx.sem["ve"], 1)
                v.wait_ge(cx.sem["mm"], mb + 10)
                v.reduce_max(stt_[:, 0:1], ps_l[:], axis=AX).then_inc(cx.sem["ve"], 1)
                v.wait_ge(cx.sem["ve"], vb + 10)
                v.tensor_scalar(lg[:], ps_l[:], stt_[:, 0:1], None, ALU.subtract
                                ).then_inc(cx.sem["ve"], 1)
                v.wait_ge(cx.sem["ac"], ab + 1)
                v.reduce_sum(stt_[:, 1:2], el[:], axis=AX).then_inc(cx.sem["ve"], 1)
                v.wait_ge(cx.sem["ve"], vb + 12)
                v.reciprocal(stt_[:, 2:3], stt_[:, 1:2]).then_inc(cx.sem["ve"], 1)
                v.wait_ge(cx.sem["ve"], vb + 11)
                v.max(m8[:], lg[:]).then_inc(cx.sem["ve"], 1)
                v.wait_ge(cx.sem["ve"], vb + 14)
                v.max_index(i8u[:], m8[:], lg[:]).then_inc(cx.sem["ve"], 1)
                v.wait_ge(cx.sem["ve"], vb + 15)
                for e in range(8):
                    v.tensor_scalar(msall[:, tc * 8 + e:tc * 8 + e + 1], i8u[:, 0:1],
                                    float(e), stt_[:, 2:3], ALU.is_equal, ALU.mult
                                    ).then_inc(cx.sem["ve"], 1)

        cx.n["qs"] = q0 + 32 * 16
        cx.n["qg"] = g0 + 16 * 34
        cx.n["qo"] = o0 + 16
        cx.n["mm"] = m0 + MP + 32 * 10
        cx.n["ac"] = a0 + AP_ + 32
        cx.n["ve"] = v0 + VP + 32 * NV


def s6a_moe_h(cx, NEX=NE):
    nc = cx.nc
    C = cx.C
    bar = cx.snap()
    es = ExitStack()
    with es, nc.Block() as blk:
        w1sb = [cx.sb(es, f"s6w{i}", [128, 8 * FF], BF16) for i in range(2)]
        b1sb = [cx.sb(es, f"s6b{i}", [128, 32], F32) for i in range(2)]
        xnT = [cx.sb(es, f"s6x{i}", [128, 8 * 512], BF16) for i in range(2)]
        hst = cx.sb(es, "s6h", [128, 32 * 512], BF16)
        ps = [cx.ps(es, f"s6p{i}", [128, 512]) for i in range(4)]
        q0, g0, o0, m0, a0, v0 = (cx.n[k] for k in ("qs", "qg", "qo", "mm", "ac", "ve"))

        def mme(e):
            return m0 + e * 256

        def ace(e):
            return a0 + e * 256

        @blk.gpsimd
        def _(g):
            cx.wait_snap(g, bar)
            for e in range(NEX):
                if e >= 2:
                    g.wait_ge(cx.sem["mm"], mme(e - 1))
                for dc in range(8):
                    g.dma_start(out=w1sb[e % 2][:, dc * FF:(dc + 1) * FF],
                                in_=cx.i["w1e"][e][dc * 128:(dc + 1) * 128, :]
                                ).then_inc(cx.sem["qg"], 16)
                for tt in range(8):
                    for p4 in range(4):
                        g.wait_ge(cx.sem["ac"], ace(e) + tt * 32 + p4 * 8 + 8)
                        g.dma_start(
                            out=cx.b["hbuf"][e].rearrange("(fc p) t -> p fc t", p=128)[
                                :, p4 * 8:(p4 + 1) * 8, tt * 512:(tt + 1) * 512],
                            in_=hst.rearrange("p (fc t) -> p fc t", t=512)[
                                :, p4 * 8:(p4 + 1) * 8, :]
                        ).then_inc(cx.sem["qg"], 16)

        @blk.sync
        def _(s):
            cx.wait_snap(s, bar)
            qoc = o0
            for e in range(NEX):
                if e >= 2:
                    s.wait_ge(cx.sem["ac"], a0 + (e - 1) * 256)
                qoc = cx.sdma(s, b1sb[e % 2][:], cx.i["b1c"][e], qoc)
                for tt in range(8):
                    ge = e * 8 + tt
                    if ge >= 2:
                        s.wait_ge(cx.sem["mm"], mme((ge - 2) // 8) + ((ge - 2) % 8) * 32 + 32)
                    s.dma_start(out=xnT[tt % 2].rearrange("p (c t) -> p c t", c=8),
                                in_=cx.b["xn0c_bf"].rearrange("(c p) t -> p c t", p=128)[
                                    :, :, tt * 512:(tt + 1) * 512]).then_inc(cx.sem["qs"], 16)

        @blk.tensor
        def _(t):
            cx.wait_snap(t, bar)
            for e in range(NEX):
                t.wait_ge(cx.sem["qg"], g0 + e * 640 + 128)
                t.wait_ge(cx.sem["qo"], o0 + (e + 1) * 16)
                for tt in range(8):
                    ge = e * 8 + tt
                    t.wait_ge(cx.sem["qs"], q0 + (ge + 1) * 16)
                    for fc in range(32):
                        gi = ge * 32 + fc
                        if gi >= 4:
                            pv = gi - 4
                            t.wait_ge(cx.sem["ac"], a0 + pv + 1)
                        inst = None
                        for dc in range(8):
                            inst = t.matmul(ps[fc % 4][:],
                                            w1sb[e % 2][:, dc * FF + fc * 128:dc * FF + (fc + 1) * 128],
                                            xnT[tt % 2][:, dc * 512:(dc + 1) * 512],
                                            start=(dc == 0), stop=(dc == 7))
                        inst.then_inc(cx.sem["mm"], 1)

        @blk.scalar
        def _(a):
            cx.wait_snap(a, bar)
            for e in range(NEX):
                for tt in range(8):
                    ge = e * 8 + tt
                    for fc in range(32):
                        gi = ge * 32 + fc
                        a.wait_ge(cx.sem["mm"], m0 + gi + 1)
                        if ge >= 1:
                            pe, ptt = divmod(ge - 1, 8)
                            p4 = fc // 8
                            a.wait_ge(cx.sem["qg"],
                                      g0 + 16 * (pe * 40 + 8 + ptt * 4 + p4 + 1))
                        a.activation(hst[:, fc * 512:(fc + 1) * 512], ps[fc % 4][:], AF.Gelu,
                                     bias=b1sb[e % 2][:, fc:fc + 1]).then_inc(cx.sem["ac"], 1)

        cx.n["qs"] = q0 + NEX * 8 * 16
        cx.n["qg"] = g0 + NEX * 640
        cx.n["qo"] = o0 + NEX * 16
        cx.n["mm"] = m0 + NEX * 256
        cx.n["ac"] = a0 + NEX * 256


def s6b_moe_y(cx):
    nc = cx.nc
    C = cx.C
    bar = cx.snap()
    es = ExitStack()
    with es, nc.Block() as blk:
        w2sb = cx.sb(es, "s6bw", [128, 32 * D], BF16)
        b2b = cx.sb(es, "s6bb", [1, D], BF16)
        hsb = [cx.sb(es, f"s6bh{i}", [128, 32 * 128], BF16) for i in range(2)]
        msal = cx.sb(es, "s6bm", [128, 32 * 8], F32)
        sel = cx.sb(es, "s6bs", [128, 32 * D], BF16)
        ps = [cx.ps(es, f"s6bp{i}", [128, 1024]) for i in range(2)]
        q0, g0, o0, m0, a0, v0 = (cx.n[k] for k in ("qs", "qg", "qo", "mm", "ac", "ve"))

        @blk.sync
        def _(s):
            cx.wait_snap(s, bar)
            s.dma_start(out=msal.rearrange("p (c e) -> p c e", e=8),
                        in_=cx.b["msb"].rearrange("(c p) e -> p c e", p=128)
                        ).then_inc(cx.sem["qs"], 16)
            for e in range(NE):
                for tc in range(32):
                    ge = e * 32 + tc
                    if ge >= 2:
                        s.wait_ge(cx.sem["mm"], m0 + (ge - 2) * 2 + 2)
                    s.dma_start(out=hsb[ge % 2].rearrange("p (c t) -> p c t", c=32),
                                in_=cx.b["hbuf"][e].rearrange("(c p) t -> p c t", p=128)[
                                    :, :, tc * 128:(tc + 1) * 128]
                                ).then_inc(cx.sem["qs"], 16)

        @blk.gpsimd
        def _(g):
            cx.wait_snap(g, bar)
            qoc = o0
            for e in range(NE):
                if e >= 1:
                    g.wait_ge(cx.sem["mm"], m0 + e * 64)
                g.dma_start(out=w2sb.rearrange("p (c o) -> p c o", c=32),
                            in_=cx.i["w2e"][e].rearrange("(c p) o -> p c o", c=32)
                            ).then_inc(cx.sem["qg"], 16)
                qoc = cx.sdma(g, b2b[:], cx.i["b2r"][e:e + 1, :], qoc)
            g.wait_ge(cx.sem["ve"], v0 + NE * 32)
            g.dma_start(out=cx.b["selt"].rearrange("(c p) o -> p c o", p=128),
                        in_=sel.rearrange("p (c o) -> p c o", c=32)).then_inc(cx.sem["qg"], 16)

        @blk.tensor
        def _(t):
            cx.wait_snap(t, bar)
            for e in range(NE):
                t.wait_ge(cx.sem["qg"], g0 + (e + 1) * 16)
                t.wait_ge(cx.sem["qo"], o0 + (e + 1) * 16)
                for tc in range(32):
                    ge = e * 32 + tc
                    t.wait_ge(cx.sem["qs"], q0 + 16 + (ge + 1) * 16)
                    if ge >= 2:
                        t.wait_ge(cx.sem["ve"], v0 + (ge - 2) + 1)
                    for oh in range(2):
                        inst = None
                        for fc in range(32):
                            inst = t.matmul(ps[ge % 2][:, oh * 512:(oh + 1) * 512],
                                            hsb[ge % 2][:, fc * 128:(fc + 1) * 128],
                                            w2sb[:, fc * D + oh * 512:fc * D + (oh + 1) * 512],
                                            start=(fc == 0), stop=False)
                        inst = t.matmul(ps[ge % 2][:, oh * 512:(oh + 1) * 512],
                                        C["ones_lb"][:], b2b[0:1, oh * 512:(oh + 1) * 512],
                                        start=False, stop=True)
                        inst.then_inc(cx.sem["mm"], 1)

        @blk.vector
        def _(v):
            cx.wait_snap(v, bar)
            v.wait_ge(cx.sem["qs"], q0 + 16)
            for e in range(NE):
                for tc in range(32):
                    ge = e * 32 + tc
                    v.wait_ge(cx.sem["mm"], m0 + ge * 2 + 2)
                    mscol = msal[:, tc * 8 + e:tc * 8 + e + 1]
                    if e == 0:
                        v.tensor_scalar(sel[:, tc * D:(tc + 1) * D], ps[ge % 2][:],
                                        mscol, None, ALU.mult).then_inc(cx.sem["ve"], 1)
                    else:
                        v.scalar_tensor_tensor(sel[:, tc * D:(tc + 1) * D], ps[ge % 2][:],
                                               mscol, sel[:, tc * D:(tc + 1) * D],
                                               ALU.mult, ALU.add).then_inc(cx.sem["ve"], 1)

        cx.n["qs"] = q0 + 16 + NE * 32 * 16
        cx.n["qg"] = g0 + NE * 16 + 16
        cx.n["qo"] = o0 + NE * 16
        cx.n["mm"] = m0 + NE * 64
        cx.n["ve"] = v0 + NE * 32


def s7_combine0(cx):
    nc = cx.nc
    C = cx.C
    bar = cx.marks.get("pre_s6_bar") or cx.snap()
    selt_done = cx.marks.get("selt_done") or (lambda tc: 0)
    es = ExitStack()
    NV, NA = 8, 4
    with es, nc.Block() as blk:
        R = "right"
        xt = [cx.sb(es, f"s7x{i}", [128, D], F32, side=R) for i in range(2)]
        s0 = [cx.sb(es, f"s7s{i}", [128, D], F32, side=R) for i in range(2)]
        sl = [cx.sb(es, f"s7l{i}", [128, D], F32, side=R) for i in range(2)]
        g0f = cx.sb(es, "s7g", [128, D], F32, side=R)
        x1 = [cx.sb(es, f"s7o{i}", [128, D], F32, side=R) for i in range(2)]
        xn1 = [cx.sb(es, f"s7n{i}", [128, D], BF16, side=R) for i in range(2)]
        sq = cx.sb(es, "s7q", [128, D], F32, side=R)
        st = cx.sb(es, "s7t", [128, 8], F32, side=R)
        q0, g0_, o0, m0, a0, v0 = (cx.n[k] for k in ("qs", "qg", "qo", "mm", "ac", "ve"))

        @blk.sync
        def _(s):
            cx.wait_snap(s, bar)
            for tc in range(32):
                if tc >= 2:
                    s.wait_ge(cx.sem["ve"], v0 + (tc - 2) * NV + NV)
                    s.wait_ge(cx.sem["ac"], a0 + (tc - 2) * NA + 2)
                r = slice(tc * 128, (tc + 1) * 128)
                sd = selt_done(tc)
                if sd:
                    s.wait_ge(cx.sem["qg"], sd)
                s.dma_start(out=xt[tc % 2][:], in_=cx.i["x_tok"][r, :]).then_inc(cx.sem["qs"], 16)
                s.dma_start(out=s0[tc % 2][:], in_=cx.b["s0t"][r, :]).then_inc(cx.sem["qs"], 16)
                s.dma_start(out=sl[tc % 2][:], in_=cx.b["selt"][r, :]).then_inc(cx.sem["qs"], 16)

        @blk.scalar
        def _(a):
            cx.wait_snap(a, bar)
            for tc in range(32):
                b = tc % 2
                vb = v0 + tc * NV
                ab = a0 + tc * NA
                a.wait_ge(cx.sem["qs"], q0 + (tc + 1) * 48)
                if tc >= 1:
                    a.wait_ge(cx.sem["ve"], v0 + (tc - 1) * NV + 1)
                a.activation(g0f[:], sl[b][:], AF.Silu).then_inc(cx.sem["ac"], 1)
                a.wait_ge(cx.sem["ve"], vb + 2)
                a.activation(sq[:], x1[b][:], AF.Square).then_inc(cx.sem["ac"], 1)
                a.wait_ge(cx.sem["ve"], vb + 7)
                a.activation(st[:, 4:5], st[:, 3:4], AF.Ln, bias=C["cv"][:, 0:1]).then_inc(cx.sem["ac"], 1)
                a.wait_ge(cx.sem["ac"], ab + 3)
                a.activation(st[:, 5:6], st[:, 4:5], AF.Exp, scale=C["cv"][:, 1:2]).then_inc(cx.sem["ac"], 1)

        @blk.vector
        def _(v):
            cx.wait_snap(v, bar)
            for tc in range(32):
                b = tc % 2
                vb = v0 + tc * NV
                ab = a0 + tc * NA
                v.wait_ge(cx.sem["ac"], ab + 1)
                v.tensor_mul(g0f[:], s0[b][:], g0f[:]).then_inc(cx.sem["ve"], 1)
                v.wait_ge(cx.sem["ve"], vb + 1)
                if tc >= 2:
                    v.wait_ge(cx.sem["qg"], g0_ + ((tc - 2) * 2 + 1) * 16)
                v.tensor_add(x1[b][:], xt[b][:], g0f[:]).then_inc(cx.sem["ve"], 1)
                v.wait_ge(cx.sem["ve"], vb + 2)
                v.reduce_sum(st[:, 0:1], x1[b][:], axis=AX).then_inc(cx.sem["ve"], 1)
                v.wait_ge(cx.sem["ac"], ab + 2)
                v.reduce_sum(st[:, 1:2], sq[:], axis=AX).then_inc(cx.sem["ve"], 1)
                v.wait_ge(cx.sem["ve"], vb + 4)
                v.tensor_scalar(st[:, 2:4], st[:, 0:2], 1.0 / D, None, ALU.mult).then_inc(cx.sem["ve"], 1)
                v.wait_ge(cx.sem["ve"], vb + 5)
                v.tensor_mul(st[:, 6:7], st[:, 2:3], st[:, 2:3]).then_inc(cx.sem["ve"], 1)
                v.wait_ge(cx.sem["ve"], vb + 6)
                v.tensor_sub(st[:, 3:4], st[:, 3:4], st[:, 6:7]).then_inc(cx.sem["ve"], 1)
                v.wait_ge(cx.sem["ac"], ab + 4)
                if tc >= 2:
                    v.wait_ge(cx.sem["qg"], g0_ + ((tc - 2) * 2 + 2) * 16)
                v.tensor_scalar(xn1[b][:], x1[b][:], st[:, 2:3], st[:, 5:6],
                                ALU.subtract, ALU.mult).then_inc(cx.sem["ve"], 1)

        @blk.gpsimd
        def _(g):
            cx.wait_snap(g, bar)
            for tc in range(32):
                b = tc % 2
                vb = v0 + tc * NV
                r = slice(tc * 128, (tc + 1) * 128)
                g.wait_ge(cx.sem["ve"], vb + 2)
                g.dma_start(out=cx.b["x1t"][r, :], in_=x1[b][:]).then_inc(cx.sem["qg"], 16)
                g.wait_ge(cx.sem["ve"], vb + NV)
                g.dma_start(out=cx.b["xn1t_bf"][r, :], in_=xn1[b][:]).then_inc(cx.sem["qg"], 16)

        cx.n["qs"] = q0 + 32 * 48
        cx.n["qg"] = g0_ + 32 * 32
        cx.n["ac"] = a0 + 32 * NA
        cx.n["ve"] = v0 + 32 * NV


def s8_inproj(cx):
    nc = cx.nc
    C = cx.C
    bar = cx.snap()
    es = ExitStack()
    with es, nc.Block() as blk:
        xnT = cx.sb(es, "s8x", [128, 8 * S], BF16)
        wsb = cx.sb(es, "s8w", [128, 8 * DPROJ], BF16)
        stg = [cx.sb(es, f"s8s{i}", [128, 512], F32) for i in range(2)]
        dstg = [cx.sb(es, f"s8d{i}", [16, 512], F32) for i in range(2)]
        ps = [cx.ps(es, f"s8p{i}", [128, 512]) for i in range(2)]
        q0, g0, o0, m0, a0, v0 = (cx.n[k] for k in ("qs", "qg", "qo", "mm", "ac", "ve"))
        OCS = list(range(16, 33)) + [33] + list(range(16))
        NT = 34 * 8

        def qg_spill_count(i):
            n = 0
            for k in range(i + 1):
                if OCS[k // 8] != 33:
                    n += 1
            return g0 + (8 + n) * 16

        @blk.sync
        def _(s):
            cx.wait_snap(s, bar)
            qoc = o0
            for dc in range(8):
                qoc = cx.sdma(s, xnT[:, dc * S:(dc + 1) * S],
                              cx.b["xn1t_bf"][:, dc * 128:(dc + 1) * 128], qoc, transpose=True)

        @blk.gpsimd
        def _(g):
            cx.wait_snap(g, bar)
            for dc in range(8):
                g.dma_start(out=wsb[:, dc * DPROJ:(dc + 1) * DPROJ],
                            in_=cx.i["winT"][dc * 128:(dc + 1) * 128, :]).then_inc(cx.sem["qg"], 16)
            qoc = o0 + 8 * 16
            for i in range(NT):
                oc, tt = divmod(i, 8)
                oc = OCS[i // 8]
                g.wait_ge(cx.sem["ve"], v0 + i + 1)
                if oc == 33:
                    g.dma_start(out=cx.b["dtraw"][:, tt * 512:(tt + 1) * 512],
                                in_=dstg[i % 2][:]).then_inc(cx.sem["qo"], 16)
                    qoc += 16
                    g.wait_ge(cx.sem["qo"], qoc)
                elif oc >= 16:
                    g.dma_start(out=cx.b["xbc"][(oc - 16) * 128:(oc - 15) * 128, tt * 512:(tt + 1) * 512],
                                in_=stg[i % 2][:]).then_inc(cx.sem["qg"], 16)
                else:
                    g.dma_start(out=cx.b["zc"][oc * 128:(oc + 1) * 128, tt * 512:(tt + 1) * 512],
                                in_=stg[i % 2][:]).then_inc(cx.sem["qg"], 16)

        @blk.tensor
        def _(t):
            cx.wait_snap(t, bar)
            t.wait_ge(cx.sem["qo"], o0 + 8 * 16)
            t.wait_ge(cx.sem["qg"], g0 + 8 * 16)
            for i in range(NT):
                tt = i % 8
                oc = OCS[i // 8]
                if i >= 2:
                    t.wait_ge(cx.sem["ve"], v0 + (i - 2) + 1)
                inst = None
                wid = 128 if oc != 33 else 16
                for dc in range(8):
                    inst = t.matmul(ps[i % 2][0:wid, :],
                                    wsb[:, dc * DPROJ + oc * 128:dc * DPROJ + oc * 128 + wid],
                                    xnT[:, dc * S + tt * 512:dc * S + (tt + 1) * 512],
                                    start=(dc == 0), stop=(dc == 7))
                inst.then_inc(cx.sem["mm"], 1)

        @blk.vector
        def _(v):
            cx.wait_snap(v, bar)
            nq = 0
            for i in range(NT):
                tt = i % 8
                oc = OCS[i // 8]
                v.wait_ge(cx.sem["mm"], m0 + i + 1)
                if i >= 2:
                    if OCS[(i - 2) // 8] != 33:
                        v.wait_ge(cx.sem["qg"], qg_spill_count(i - 2))
                    else:
                        v.wait_ge(cx.sem["qo"], o0 + 8 * 16 + (((i - 2) % 8) + 1) * 16)
                if oc != 33:
                    v.tensor_copy(stg[i % 2][:], ps[i % 2][:]).then_inc(cx.sem["ve"], 1)
                else:
                    v.tensor_copy(dstg[i % 2][:], ps[i % 2][0:16, :]).then_inc(cx.sem["ve"], 1)

        cx.marks["pre_s8_bar"] = bar
        cx.marks["s8_xbc_done"] = lambda cc: qg_spill_count(cc * 8 + 7)
        cx.marks["s8_dtraw_done"] = o0 + 16 * 16
        cx.n["qo"] = o0 + 16 * 16
        cx.n["qg"] = g0 + (8 + 264) * 16
        cx.n["mm"] = m0 + NT
        cx.n["ve"] = v0 + NT

def s9_conv(cx):
    nc = cx.nc
    C = cx.C
    bar = cx.marks.get("pre_s8_bar") or cx.snap()
    xbc_done = cx.marks.get("s8_xbc_done") or (lambda cc: 0)
    dtraw_done = cx.marks.get("s8_dtraw_done", 0)
    es = ExitStack()
    R = "right"
    with es, nc.Block() as blk:
        dtr = cx.sb(es, "s9dt", [16, S], F32, side=R)
        dwork = cx.sb(es, "s9w", [16, 3 * S], F32, side=R)
        dtb = cx.sb(es, "s9b", [16, 1], F32, side=R)
        alog = cx.sb(es, "s9a", [16, 2], F32, side=R)
        xp = [cx.sb(es, f"s9x{i}", [128, S + 3], F32, side=R) for i in range(2)]
        ca = cx.sb(es, "s9ca", [128, S], F32, side=R)
        cb = cx.sb(es, "s9cb", [128, S], F32, side=R)
        sx = cx.sb(es, "s9sx", [128, S], F32, side=R)
        sxb = cx.sb(es, "s9sb", [128, S], BF16, side=R)
        cw = cx.sb(es, "s9cw", [128, 17 * 4], F32, side=R)
        cbb = cx.sb(es, "s9cbb", [128, 17], F32, side=R)
        q0, g0, o0, m0, a0, v0 = (cx.n[k] for k in ("qs", "qg", "qo", "mm", "ac", "ve"))
        VDT0 = 17 * 6
        ADT0 = 17

        @blk.sync
        def _(s):
            cx.wait_snap(s, bar)
            for cc in range(17):
                if cc >= 2:
                    s.wait_ge(cx.sem["ac"], a0 + (cc - 2) + 1)
                xd = xbc_done(cc)
                if xd:
                    s.wait_ge(cx.sem["qg"], xd)
                s.dma_start(out=xp[cc % 2][:, 3:S + 3], in_=cx.b["xbc"][cc * 128:(cc + 1) * 128, :]
                            ).then_inc(cx.sem["qs"], 16)
            qoc = o0
            if dtraw_done:
                s.wait_ge(cx.sem["qo"], dtraw_done)
            qoc = cx.sdma(s, dtr[:], cx.b["dtraw"], qoc)
            qoc = cx.sdma(s, dtb[:], cx.i["dtb"], qoc)
            qoc = cx.sdma(s, alog[:, 0:1], cx.i["alog"], qoc)

        @blk.gpsimd
        def _(g):
            cx.wait_snap(g, bar)
            g.dma_start(out=cw.rearrange("p (c w) -> p c w", c=17), in_=cx.i["convw"]).then_inc(cx.sem["qg"], 16)
            g.dma_start(out=cbb[:], in_=cx.i["convb"]).then_inc(cx.sem["qg"], 16)
            for cc in range(17):
                vb = v0 + cc * 6
                ab = a0 + cc
                g.wait_ge(cx.sem["ve"], vb + 6)
                if cc < 16:
                    g.dma_start(out=cx.b["xs_bf"][cc * 128:(cc + 1) * 128, :], in_=sxb[:]
                                ).then_inc(cx.sem["qg"], 16)
                    g.wait_ge(cx.sem["ac"], ab + 1)
                    g.dma_start(out=cx.b["xs_f32"][cc * 128:(cc + 1) * 128, :], in_=sx[:]
                                ).then_inc(cx.sem["qg"], 16)
                else:
                    g.dma_start(out=cx.b["bcbf"][:, :], in_=sxb[:]).then_inc(cx.sem["qg"], 16)
            g.wait_ge(cx.sem["ve"], v0 + VDT0 + 3)
            g.dma_start(out=cx.b["dtv"], in_=dwork[:, S:2 * S]).then_inc(cx.sem["qg"], 16)
            g.wait_ge(cx.sem["ve"], v0 + VDT0 + 20)
            g.dma_start(out=cx.b["acumd"], in_=dwork[:, 0:S]).then_inc(cx.sem["qg"], 16)
            g.wait_ge(cx.sem["qg"], g0 + (2 + 33 + 2) * 16)

        @blk.scalar
        def _(a):
            cx.wait_snap(a, bar)
            for cc in range(17):
                vb = v0 + cc * 6
                ab = a0 + cc
                a.wait_ge(cx.sem["ve"], vb + 5)
                if cc >= 1:
                    a.wait_ge(cx.sem["ve"], v0 + (cc - 1) * 6 + 6)
                    a.wait_ge(cx.sem["qg"], g0 + 16 * (2 + 2 * (cc - 1) + 2))
                a.activation(sx[:], cb[:], AF.Silu, bias=cbb[:, cc:cc + 1]).then_inc(cx.sem["ac"], 1)
            a.wait_ge(cx.sem["ve"], v0 + VDT0 + 1)
            a.activation(dwork[:, S:2 * S], dwork[:, 0:S], AF.Abs).then_inc(cx.sem["ac"], 1)
            a.wait_ge(cx.sem["ac"], a0 + ADT0 + 1)
            a.activation(dwork[:, S:2 * S], dwork[:, S:2 * S], AF.Exp, scale=C["cv"][0:16, 3:4]
                         ).then_inc(cx.sem["ac"], 1)
            a.wait_ge(cx.sem["ac"], a0 + ADT0 + 2)
            a.activation(dwork[:, S:2 * S], dwork[:, S:2 * S], AF.Ln, bias=C["cv"][0:16, 5:6]
                         ).then_inc(cx.sem["ac"], 1)
            a.activation(alog[:, 1:2], alog[:, 0:1], AF.Exp).then_inc(cx.sem["ac"], 1)

        @blk.vector
        def _(v):
            cx.wait_snap(v, bar)
            for cc in range(17):
                b = cc % 2
                vb = v0 + cc * 6
                ab = a0 + cc
                v.wait_ge(cx.sem["qs"], q0 + (cc + 1) * 16)
                v.wait_ge(cx.sem["qg"], g0 + 32)
                v.memset(xp[b][:, 0:3], 0.0).then_inc(cx.sem["ve"], 1)
                v.wait_ge(cx.sem["ve"], vb + 1)
                v.tensor_scalar(ca[:], xp[b][:, 0:S], cw[:, cc * 4 + 0:cc * 4 + 1], None,
                                ALU.mult).then_inc(cx.sem["ve"], 1)
                v.wait_ge(cx.sem["ve"], vb + 2)
                if cc >= 1:
                    v.wait_ge(cx.sem["ac"], a0 + cc)
                v.scalar_tensor_tensor(cb[:], xp[b][:, 1:S + 1], cw[:, cc * 4 + 1:cc * 4 + 2],
                                       ca[:], ALU.mult, ALU.add).then_inc(cx.sem["ve"], 1)
                v.wait_ge(cx.sem["ve"], vb + 3)
                v.scalar_tensor_tensor(ca[:], xp[b][:, 2:S + 2], cw[:, cc * 4 + 2:cc * 4 + 3],
                                       cb[:], ALU.mult, ALU.add).then_inc(cx.sem["ve"], 1)
                v.wait_ge(cx.sem["ve"], vb + 4)
                v.scalar_tensor_tensor(cb[:], xp[b][:, 3:S + 3], cw[:, cc * 4 + 3:cc * 4 + 4],
                                       ca[:], ALU.mult, ALU.add).then_inc(cx.sem["ve"], 1)
                v.wait_ge(cx.sem["ac"], ab + 1)
                if cc >= 2:
                    v.wait_ge(cx.sem["qg"], g0 + 32 + 16 * (2 * (cc - 2) + 1))
                v.tensor_copy(sxb[:], sx[:]).then_inc(cx.sem["ve"], 1)
            v.wait_ge(cx.sem["qo"], o0 + 48)
            v.tensor_scalar(dwork[:, 0:S], dtr[:], dtb[:, 0:1], None, ALU.add).then_inc(cx.sem["ve"], 1)
            v.wait_ge(cx.sem["ve"], v0 + VDT0 + 1)
            v.tensor_scalar(dwork[:, 2 * S:3 * S], dwork[:, 0:S], 0.0, None, ALU.max).then_inc(cx.sem["ve"], 1)
            v.wait_ge(cx.sem["ac"], a0 + ADT0 + 3)
            v.wait_ge(cx.sem["ve"], v0 + VDT0 + 2)
            v.tensor_add(dwork[:, S:2 * S], dwork[:, 2 * S:3 * S], dwork[:, S:2 * S]).then_inc(cx.sem["ve"], 1)
            v.wait_ge(cx.sem["ac"], a0 + ADT0 + 4)
            v.wait_ge(cx.sem["ve"], v0 + VDT0 + 3)
            v.tensor_scalar(dwork[:, 0:S], dwork[:, S:2 * S], alog[:, 1:2], -1.0,
                            ALU.mult, ALU.mult).then_inc(cx.sem["ve"], 1)
            src, dst = 0, 2
            for k in range(8):
                sh = 1 << k
                v.wait_ge(cx.sem["ve"], v0 + VDT0 + 4 + k * 2)
                va = dwork[:, src * S:(src + 1) * S].rearrange("p (c q) -> p c q", q=CH)
                vd = dwork[:, dst * S:(dst + 1) * S].rearrange("p (c q) -> p c q", q=CH)
                v.tensor_add(vd[:, :, sh:], va[:, :, sh:], va[:, :, 0:CH - sh]).then_inc(cx.sem["ve"], 1)
                v.tensor_copy(vd[:, :, 0:sh], va[:, :, 0:sh]).then_inc(cx.sem["ve"], 1)
                src, dst = dst, src

        cx.n["qs"] = q0 + 17 * 16
        cx.n["qg"] = g0 + (2 + 33 + 2) * 16
        cx.n["qo"] = o0 + 48
        cx.n["ac"] = a0 + ADT0 + 4
        cx.n["ve"] = v0 + VDT0 + 20

def s10_ssd(cx):
    nc = cx.nc
    C = cx.C
    bar = cx.snap()
    es = ExitStack()
    with es, nc.Block() as blk:
        dts = cx.sb(es, "sads", [16, S], F32)
        acs = cx.sb(es, "saas", [16, S], F32)
        bcbB = cx.sb(es, "sabB", [64, S], BF16)
        bcbC = cx.sb(es, "sabC", [64, S], BF16)
        xbf = [cx.sb(es, f"saxb{i}", [128, 16 * CH], BF16) for i in range(2)]
        xf = [cx.sb(es, f"saxf{i}", [128, 16 * CH], F32) for i in range(2)]
        dwk = cx.sb(es, "sadw", [16, 2 * CH], F32)
        WT = cx.sb(es, "sawt", [16, CH], F32)
        cdc = cx.sb(es, "sacd", [16, 2], F32)
        dtT = cx.sb(es, "sadT", [128, 32], F32)
        acT = cx.sb(es, "saaT", [128, 32], F32)
        WTT = cx.sb(es, "sawT", [128, 32], F32)
        Btk = cx.sb(es, "saBt", [128, 128], BF16)
        cdr = cx.sb(es, "sacr", [1, 16], F32)
        cdb = cx.sb(es, "sacb", [128, 16], F32)
        scT = cx.sb(es, "sasc", [128, 2 * CH], F32)
        dcy = [cx.sb(es, f"sady{i}", [128, 2 * CH], F32) for i in range(2)]
        MT = [cx.sb(es, f"samt{i}", [128, 2 * CH], BF16) for i in range(2)]
        eas = [cx.sb(es, f"saea{i}", [64, CH], F32) for i in range(2)]
        Cs = [cx.sb(es, f"sacs{i}", [64, CH], BF16) for i in range(2)]
        Bs = [cx.sb(es, f"sabs{i}", [128, 128], BF16) for i in range(2)]
        Sbf = [cx.sb(es, f"sasb{i}", [64, 128], BF16) for i in range(2)]
        Sst = [cx.sb(es, f"sast{i}", [64, 16 * 128], F32) for i in range(2)]
        xtk = [cx.sb(es, f"saxt{i}", [128, 2 * 128], BF16) for i in range(2)]
        yst = cx.sb(es, "says", [128, 16 * CH], F32)
        psA = cx.ps(es, "sapA", [128, CH])
        psY = [cx.ps(es, f"sapY{i}", [128, CH]) for i in range(2)]
        psS = cx.ps(es, "sapS", [64, 128])
        psT = [cx.ps(es, f"sapT{i}", [128, CH]) for i in range(2)]
        psTb = [cx.ps(es, f"sapB{i}", [128, CH], BF16) for i in range(2)]

        q0, g0, o0, m0, a0, v0 = (cx.n[k] for k in ("qs", "qg", "qo", "mm", "ac", "ve"))
        MMC, VEC, ACC = 92, 206, 50
        mmc = lambda c: m0 + c * MMC
        vec = lambda c: v0 + 2 + c * VEC
        acc = lambda c: a0 + c * ACC
        mh = lambda c, h: mmc(c) + 12 + h * 5
        vh = lambda c, h: vec(c) + 14 + h * 12
        ah = lambda c, h: acc(c) + 2 + h * 3

        @blk.sync
        def _(s):
            cx.wait_snap(s, bar)
            qoc = o0
            qoc = cx.sdma(s, dts[:], cx.b["dtv"], qoc)
            qoc = cx.sdma(s, acs[:], cx.b["acumd"], qoc)
            s.dma_start(out=bcbB[:], in_=cx.b["bcbf"][0:64, :]).then_inc(cx.sem["qs"], 16)
            s.dma_start(out=bcbC[:], in_=cx.b["bcbf"][64:128, :]).then_inc(cx.sem["qs"], 16)
            for c in range(NCH):
                if c >= 2:
                    s.wait_ge(cx.sem["mm"], mmc(c - 1))
                    s.wait_ge(cx.sem["ve"], vec(c - 1))
                s.dma_start(out=xbf[c % 2].rearrange("p (h t) -> p h t", t=CH),
                            in_=cx.b["xs_bf"].rearrange("(h p) t -> p h t", p=128)[
                                :, :, c * CH:(c + 1) * CH]).then_inc(cx.sem["qs"], 16)
                s.dma_start(out=xf[c % 2].rearrange("p (h t) -> p h t", t=CH),
                            in_=cx.b["xs_f32"].rearrange("(h p) t -> p h t", p=128)[
                                :, :, c * CH:(c + 1) * CH]).then_inc(cx.sem["qs"], 16)

        @blk.tensor
        def _(t):
            cx.wait_snap(t, bar)
            t.wait_ge(cx.sem["qo"], o0 + 32)
            t.wait_ge(cx.sem["qs"], q0 + 32)
            kt = [0]
            kb = [0]

            for c in range(NCH):
                cs = slice(c * CH, (c + 1) * CH)
                t.wait_ge(cx.sem["qs"], q0 + 32 + (c + 1) * 32)
                mb = mmc(c)
                vb = vec(c)
                ab = acc(c)
                fevac = [vb + 3, vb + 4, vb + 5, vb + 6, vb + 9, vb + 10, vb + 11, vb + 12, vb + 13]

                def tr_f32(in_ap, n_in_part, n_cols):
                    k = kt[0]
                    if k >= 2:
                        t.wait_ge(cx.sem["ve"], fevac[k - 2])
                    elif c >= 1:
                        t.wait_ge(cx.sem["ve"], vec(c - 1) + 13)
                    t.transpose(psT[k % 2][0:n_cols, 0:n_in_part], in_ap,
                                C["eyef"][0:n_in_part, 0:n_in_part]).then_inc(cx.sem["mm"], 1)
                    kt[0] += 1

                kt[0] = 0
                kb[0] = 0
                for j in range(2):
                    tr_f32(dts[:, c * CH + j * 128:c * CH + j * 128 + 128], 16, 128)
                for j in range(2):
                    tr_f32(acs[:, c * CH + j * 128:c * CH + j * 128 + 128], 16, 128)
                for j in range(2):
                    k = kb[0]
                    if k >= 2:
                        t.wait_ge(cx.sem["ve"], vb + 7 + (k - 2))
                    elif c >= 1:
                        t.wait_ge(cx.sem["ve"], vh(c - 1, 15) + 10)
                    t.transpose(psTb[k % 2][0:128, 0:64],
                                bcbB[:, c * CH + j * 128:c * CH + j * 128 + 128],
                                C["eyeb"][0:64, 0:64]).then_inc(cx.sem["mm"], 1)
                    kb[0] += 1
                for j in range(2):
                    k = kt[0]
                    if k >= 2:
                        t.wait_ge(cx.sem["ve"], fevac[k - 2])
                    t.matmul(psT[k % 2][:, 0:CH],
                             bcbB[:, c * CH + j * 128:c * CH + j * 128 + 128],
                             bcbC[:, cs], start=True, stop=True).then_inc(cx.sem["mm"], 1)
                    kt[0] += 1
                t.wait_ge(cx.sem["ve"], vb + 2)
                for j in range(2):
                    tr_f32(WT[:, j * 128:(j + 1) * 128], 16, 128)
                t.wait_ge(cx.sem["ac"], ab + 2)
                k = kt[0]
                if k >= 2:
                    t.wait_ge(cx.sem["ve"], fevac[k - 2])
                t.transpose(psT[k % 2][0:1, 0:16], cdc[:, 1:2], C["eyef"][0:16, 0:16]
                            ).then_inc(cx.sem["mm"], 1)
                kt[0] += 1
                t.wait_ge(cx.sem["ve"], vb + 13)
                if c >= 1:
                    t.wait_ge(cx.sem["ve"], vh(c - 1, 15) + 2)
                    t.wait_ge(cx.sem["ac"], ah(c - 1, 15) + 3)
                t.matmul(psA[:, 0:16], C["ones_l"][:], cdr[:], start=True, stop=True
                         ).then_inc(cx.sem["mm"], 1)
                for h in range(NH):
                    hb2 = h % 2
                    if h >= 1:
                        t.wait_ge(cx.sem["ve"], vh(c, h - 1) + 2)
                        t.wait_ge(cx.sem["ac"], ah(c, h - 1) + 3)
                    else:
                        t.wait_ge(cx.sem["ve"], vb + 14)
                    t.matmul(psA[:, 0:CH], C["hsel"][:, h * 128:(h + 1) * 128], acs[:, cs],
                             start=True, stop=True).then_inc(cx.sem["mm"], 1)
                    for j in range(2):
                        k = kb[0]
                        if k >= 2:
                            if k - 2 < 2:
                                t.wait_ge(cx.sem["ve"], vb + 7 + (k - 2))
                            else:
                                pk = k - 4
                                ph, pj = divmod(pk, 2)
                                t.wait_ge(cx.sem["ve"], vh(c, ph) + 9 + pj + 1)
                        t.transpose(psTb[k % 2][0:128, 0:128],
                                    xbf[c % 2][:, h * CH + j * 128:h * CH + j * 128 + 128],
                                    C["eyeb"][:]).then_inc(cx.sem["mm"], 1)
                        kb[0] += 1
                    t.wait_ge(cx.sem["ve"], vh(c, h) + 10)
                    if h >= 2:
                        t.wait_ge(cx.sem["ve"], vh(c, h - 2) + 12)
                    elif c >= 1:
                        t.wait_ge(cx.sem["ve"], vh(c - 1, 14 + h) + 12)
                    inst = None
                    for j in range(2):
                        inst = t.matmul(psY[hb2][:, 0:CH], xtk[hb2][:, j * 128:(j + 1) * 128],
                                        MT[hb2][:, j * CH:(j + 1) * CH], start=(j == 0), stop=False)
                    inst = t.matmul(psY[hb2][:, 0:CH], Sbf[hb2][:], Cs[hb2][:], start=False, stop=True)
                    inst.then_inc(cx.sem["mm"], 1)
                    if h >= 1:
                        t.wait_ge(cx.sem["ve"], vh(c, h - 1) + 11)
                    elif c >= 1:
                        t.wait_ge(cx.sem["ve"], vh(c - 1, 15) + 11)
                    inst = None
                    for j in range(2):
                        inst = t.matmul(psS[:], Bs[hb2][:, j * 64:(j + 1) * 64],
                                        xtk[hb2][:, j * 128:(j + 1) * 128],
                                        start=(j == 0), stop=(j == 1))
                    inst.then_inc(cx.sem["mm"], 1)

        @blk.scalar
        def _(a):
            cx.wait_snap(a, bar)
            for c in range(NCH):
                vb = vec(c)
                ab = acc(c)
                a.wait_ge(cx.sem["ve"], vb + 1)
                if c >= 1:
                    a.wait_ge(cx.sem["ve"], vec(c - 1) + 2)
                    a.wait_ge(cx.sem["mm"], mmc(c - 1) + 11)
                a.activation(dwk[:, CH:2 * CH], dwk[:, 0:CH], AF.Exp, scale=C["cv"][0:16, 3:4]
                             ).then_inc(cx.sem["ac"], 1)
                a.activation(cdc[:, 1:2], acs[:, c * CH + 255:c * CH + 256], AF.Exp
                             ).then_inc(cx.sem["ac"], 1)
                for h in range(NH):
                    hb2 = h % 2
                    a.wait_ge(cx.sem["ve"], vh(c, h) + 2)
                    a.activation(dcy[hb2][:, 0:CH], dcy[hb2][:, 0:CH], AF.Exp).then_inc(cx.sem["ac"], 1)
                    a.activation(dcy[hb2][:, CH:2 * CH], dcy[hb2][:, CH:2 * CH], AF.Exp).then_inc(cx.sem["ac"], 1)
                    a.wait_ge(cx.sem["mm"], mh(c, h) + 1)
                    if h >= 2:
                        a.wait_ge(cx.sem["ve"], vh(c, h - 2) + 5)
                    elif c >= 1:
                        a.wait_ge(cx.sem["ve"], vh(c - 1, 14 + h) + 5)
                    a.activation(eas[hb2][:], psA[0:64, 0:CH], AF.Exp).then_inc(cx.sem["ac"], 1)

        @blk.vector
        def _(v):
            cx.wait_snap(v, bar)
            v.memset(Sst[0][:], 0.0).then_inc(cx.sem["ve"], 1)
            v.memset(Sst[1][:], 0.0).then_inc(cx.sem["ve"], 1)
            v.wait_ge(cx.sem["qo"], o0 + 32)
            for c in range(NCH):
                cs = slice(c * CH, (c + 1) * CH)
                mb = mmc(c)
                vb = vec(c)
                ab = acc(c)
                cur, nxt = c % 2, (c + 1) % 2
                if c >= 1:
                    v.wait_ge(cx.sem["ac"], acc(c - 1) + 1)
                v.tensor_scalar(dwk[:, 0:CH], acs[:, cs], acs[:, c * CH + 255:c * CH + 256],
                                None, ALU.subtract).then_inc(cx.sem["ve"], 1)
                v.wait_ge(cx.sem["ac"], ab + 1)
                v.tensor_mul(WT[:, 0:CH], dts[:, cs], dwk[:, CH:2 * CH]).then_inc(cx.sem["ve"], 1)
                for i in range(4):
                    v.wait_ge(cx.sem["mm"], mb + i + 1)
                    tgt = dtT if i < 2 else acT
                    v.tensor_copy(tgt[:, (i % 2) * 16:(i % 2) * 16 + 16], psT[i % 2][:, 0:16]
                                  ).then_inc(cx.sem["ve"], 1)
                for j in range(2):
                    v.wait_ge(cx.sem["mm"], mb + 4 + j + 1)
                    v.tensor_copy(Btk[:, j * 64:(j + 1) * 64], psTb[j % 2][0:128, 0:64]
                                  ).then_inc(cx.sem["ve"], 1)
                for j in range(2):
                    v.wait_ge(cx.sem["mm"], mb + 6 + j + 1)
                    v.tensor_copy(scT[:, j * CH:(j + 1) * CH], psT[j % 2][:, 0:CH]
                                  ).then_inc(cx.sem["ve"], 1)
                for j in range(2):
                    v.wait_ge(cx.sem["mm"], mb + 8 + j + 1)
                    v.tensor_copy(WTT[:, j * 16:(j + 1) * 16], psT[j % 2][:, 0:16]
                                  ).then_inc(cx.sem["ve"], 1)
                v.wait_ge(cx.sem["mm"], mb + 11)
                v.tensor_copy(cdr[:], psT[0 if (10 % 2 == 0) else 1][0:1, 0:16]).then_inc(cx.sem["ve"], 1)
                v.wait_ge(cx.sem["mm"], mb + 12)
                v.tensor_copy(cdb[:], psA[:, 0:16]).then_inc(cx.sem["ve"], 1)
                for h in range(NH):
                    hb2 = h % 2
                    hs = slice(h * CH, (h + 1) * CH)
                    v.wait_ge(cx.sem["mm"], mh(c, h) + 1)
                    if h >= 2:
                        v.wait_ge(cx.sem["ac"], ah(c, h - 2) + 2)
                    elif c >= 1:
                        v.wait_ge(cx.sem["ac"], ah(c - 1, 14 + h) + 2)
                    for j in range(2):
                        v.scalar_tensor_tensor(dcy[hb2][:, j * CH:(j + 1) * CH], psA[:, 0:CH],
                                               acT[:, j * 16 + h:j * 16 + h + 1],
                                               C["trim"][:, j * CH:(j + 1) * CH],
                                               ALU.subtract, ALU.add).then_inc(cx.sem["ve"], 1)
                    v.wait_ge(cx.sem["ac"], ah(c, h) + 2)
                    if h >= 2:
                        v.wait_ge(cx.sem["mm"], mh(c, h - 2) + 5)
                    elif c >= 1:
                        v.wait_ge(cx.sem["mm"], mh(c - 1, 14 + h) + 5)
                    for j in range(2):
                        v.scalar_tensor_tensor(MT[hb2][:, j * CH:(j + 1) * CH],
                                               dcy[hb2][:, j * CH:(j + 1) * CH],
                                               dtT[:, j * 16 + h:j * 16 + h + 1],
                                               scT[:, j * CH:(j + 1) * CH],
                                               ALU.mult, ALU.mult).then_inc(cx.sem["ve"], 1)
                    v.wait_ge(cx.sem["ac"], ah(c, h) + 3)
                    v.tensor_mul(Cs[hb2][:], bcbC[:, cs], eas[hb2][:]).then_inc(cx.sem["ve"], 1)
                    for j in range(2):
                        v.tensor_scalar(Bs[hb2][:, j * 64:(j + 1) * 64], Btk[:, j * 64:(j + 1) * 64],
                                        WTT[:, j * 16 + h:j * 16 + h + 1], None, ALU.mult
                                        ).then_inc(cx.sem["ve"], 1)
                    v.tensor_copy(Sbf[hb2][:], Sst[cur][:, h * 128:(h + 1) * 128]).then_inc(cx.sem["ve"], 1)
                    for j in range(2):
                        v.wait_ge(cx.sem["mm"], mh(c, h) + 1 + j + 1)
                        v.tensor_copy(xtk[hb2][:, j * 128:(j + 1) * 128],
                                      psTb[(2 + h * 2 + j) % 2][:, 0:128]).then_inc(cx.sem["ve"], 1)
                    v.wait_ge(cx.sem["mm"], mh(c, h) + 5)
                    v.scalar_tensor_tensor(Sst[nxt][:, h * 128:(h + 1) * 128],
                                           Sst[cur][:, h * 128:(h + 1) * 128],
                                           cdb[0:64, h:h + 1], psS[:],
                                           ALU.mult, ALU.add).then_inc(cx.sem["ve"], 1)
                    v.wait_ge(cx.sem["mm"], mh(c, h) + 4)
                    if c >= 1:
                        v.wait_ge(cx.sem["qg"], g0 + c * 16)
                    v.scalar_tensor_tensor(yst[:, hs], xf[c % 2][:, hs], C["dpb"][:, h:h + 1],
                                           psY[hb2][:, 0:CH], ALU.mult, ALU.add
                                           ).then_inc(cx.sem["ve"], 1)

        @blk.gpsimd
        def _(g):
            cx.wait_snap(g, bar)
            for c in range(NCH):
                g.wait_ge(cx.sem["ve"], vec(c) + VEC)
                g.dma_start(out=cx.b["ybuf"].rearrange("(h p) t -> p h t", p=128)[
                                :, :, c * CH:(c + 1) * CH],
                            in_=yst.rearrange("p (h t) -> p h t", t=CH)).then_inc(cx.sem["qg"], 16)

        cx.n["qs"] = q0 + 32 + NCH * 32
        cx.n["qg"] = g0 + NCH * 16
        cx.n["qo"] = o0 + 32
        cx.n["mm"] = m0 + NCH * MMC
        cx.n["ac"] = a0 + NCH * ACC
        cx.n["ve"] = v0 + 2 + NCH * VEC


def s11_gate(cx):
    nc = cx.nc
    C = cx.C
    bar = cx.snap()
    es = ExitStack()
    with es, nc.Block() as blk:
        yb = [cx.sb(es, f"sby{i}", [128, 512], F32) for i in range(2)]
        zb = [cx.sb(es, f"sbz{i}", [128, 512], F32) for i in range(2)]
        sz = cx.sb(es, "sbsz", [128, 512], F32)
        sq = cx.sb(es, "sbsq", [128, 512], F32)
        rr = cx.sb(es, "sbr", [1, 1024], F32)
        rstdb = cx.sb(es, "sbrb", [128, S], F32)
        y2 = [cx.sb(es, f"sby2{i}", [128, 2048], F32) for i in range(2)]
        z2 = [cx.sb(es, f"sbz2{i}", [128, 2048], F32) for i in range(2)]
        sz2 = cx.sb(es, "sbs2", [128, 2048], F32)
        ynh = cx.sb(es, "sbyn", [128, 16 * 2048], BF16)
        wop = cx.sb(es, "sbw", [128, 16 * D], BF16)
        nw = cx.sb(es, "sbnw", [128, 16], F32)
        stg = [cx.sb(es, f"sbo{i}", [128, 512], F32) for i in range(2)]
        ps_r = cx.ps(es, "sbpr", [1, 512])
        ps_b = cx.ps(es, "sbpb", [128, 512])
        ps_o = [cx.ps(es, f"sbpo{i}", [128, 512]) for i in range(2)]
        q0, g0, o0, m0, a0, v0 = (cx.n[k] for k in ("qs", "qg", "qo", "mm", "ac", "ve"))
        MT1, VT1, AT1 = 17, 17, 34
        m1 = lambda tt: m0 + tt * MT1
        v1 = lambda tt: v0 + tt * VT1
        a1 = lambda tt: a0 + tt * AT1
        M1, V1, A1, Q1 = 8 * MT1, 8 * VT1, 8 * AT1, 8 * 32 * 16
        m2_ = lambda th: m0 + M1 + th * 32
        v2_ = lambda th: v0 + V1 + th * 64
        a2_ = lambda th: a0 + A1 + th * 16

        @blk.sync
        def _(s):
            cx.wait_snap(s, bar)
            s.dma_start(out=nw[:], in_=cx.i["normw"]).then_inc(cx.sem["qs"], 16)
            for tt in range(8):
                for ec in range(16):
                    i = tt * 16 + ec
                    if i >= 2:
                        pv = i - 2
                        s.wait_ge(cx.sem["ve"], v1(pv // 16) + (pv % 16) + 1)
                        s.wait_ge(cx.sem["ac"], a1(pv // 16) + (pv % 16) * 2 + 2)
                    s.dma_start(out=yb[i % 2][:], in_=cx.b["ybuf"][ec * 128:(ec + 1) * 128,
                                tt * 512:(tt + 1) * 512]).then_inc(cx.sem["qs"], 16)
                    s.dma_start(out=zb[i % 2][:], in_=cx.b["zc"][ec * 128:(ec + 1) * 128,
                                tt * 512:(tt + 1) * 512]).then_inc(cx.sem["qs"], 16)
            for th in range(2):
                for ec in range(16):
                    i = th * 16 + ec
                    if i >= 2:
                        pv = i - 2
                        s.wait_ge(cx.sem["ve"], v2_(pv // 16) + (pv % 16) * 2 + 2)
                    s.dma_start(out=y2[i % 2][:], in_=cx.b["ybuf"][ec * 128:(ec + 1) * 128,
                                th * 2048:(th + 1) * 2048]).then_inc(cx.sem["qs"], 16)
                    s.dma_start(out=z2[i % 2][:], in_=cx.b["zc"][ec * 128:(ec + 1) * 128,
                                th * 2048:(th + 1) * 2048]).then_inc(cx.sem["qs"], 16)

        @blk.gpsimd
        def _(g):
            cx.wait_snap(g, bar)
            g.dma_start(out=wop.rearrange("p (c o) -> p c o", c=16),
                        in_=cx.i["woprojT"].rearrange("(c p) o -> p c o", c=16)
                        ).then_inc(cx.sem["qg"], 16)
            for th in range(2):
                for dc in range(8):
                    for t4 in range(4):
                        i = (th * 8 + dc) * 4 + t4
                        g.wait_ge(cx.sem["ve"], v2_(th) + 32 + i % (32 * 4) + 1 if False else 0)
                        k = dc * 4 + t4
                        g.wait_ge(cx.sem["ve"], v2_(th) + 32 + k + 1)
                        g.dma_start(out=cx.b["s1c"][dc * 128:(dc + 1) * 128,
                                    th * 2048 + t4 * 512:th * 2048 + (t4 + 1) * 512],
                                    in_=stg[k % 2][:]).then_inc(cx.sem["qg"], 16)

        @blk.scalar
        def _(a):
            cx.wait_snap(a, bar)
            for tt in range(8):
                for ec in range(16):
                    i = tt * 16 + ec
                    a.wait_ge(cx.sem["qs"], q0 + 16 + (i + 1) * 32)
                    if i >= 1:
                        a.wait_ge(cx.sem["ve"], v1((i - 1) // 16) + ((i - 1) % 16) + 1)
                    a.activation(sz[:], zb[i % 2][:], AF.Silu).then_inc(cx.sem["ac"], 1)
                    a.wait_ge(cx.sem["ve"], v1(tt) + ec + 1)
                    if i >= 1:
                        a.wait_ge(cx.sem["mm"], m1((i - 1) // 16) + ((i - 1) % 16) + 1)
                    a.activation(sq[:], sz[:], AF.Square).then_inc(cx.sem["ac"], 1)
                a.wait_ge(cx.sem["mm"], m1(tt) + 16)
                a.activation(rr[0:1, 0:512], ps_r[:], AF.Ln, bias=C["cv"][0:1, 0:1],
                             scale=C["cv"][0:1, 4:5]).then_inc(cx.sem["ac"], 1)
                a.wait_ge(cx.sem["ac"], a1(tt) + 33)
                a.activation(rr[0:1, 512:1024], rr[0:1, 0:512], AF.Exp,
                             scale=C["cv"][0:1, 1:2]).then_inc(cx.sem["ac"], 1)
            for th in range(2):
                for ec in range(16):
                    i = th * 16 + ec
                    a.wait_ge(cx.sem["qs"], q0 + 16 + 256 * 16 + (i + 1) * 32)
                    if i >= 1:
                        a.wait_ge(cx.sem["ve"], v2_((i - 1) // 16) + ((i - 1) % 16) * 2 + 2)
                    a.activation(sz2[:], z2[i % 2][:], AF.Silu).then_inc(cx.sem["ac"], 1)

        @blk.tensor
        def _(t):
            cx.wait_snap(t, bar)
            for tt in range(8):
                for ec in range(16):
                    t.wait_ge(cx.sem["ac"], a1(tt) + ec * 2 + 2)
                    if tt >= 1 and ec == 0:
                        t.wait_ge(cx.sem["ac"], a1(tt - 1) + 33)
                    t.matmul(ps_r[:], C["ones_c"][:], sq[:], start=(ec == 0), stop=(ec == 15)
                             ).then_inc(cx.sem["mm"], 1)
                t.wait_ge(cx.sem["ac"], a1(tt) + 34)
                if tt >= 1:
                    t.wait_ge(cx.sem["ve"], v1(tt - 1) + 17)
                t.matmul(ps_b[:], C["ones_l"][:], rr[0:1, 512:1024], start=True, stop=True
                         ).then_inc(cx.sem["mm"], 1)
            t.wait_ge(cx.sem["qg"], g0 + 16)
            for th in range(2):
                for dc in range(8):
                    for t4 in range(4):
                        k = dc * 4 + t4
                        gk = th * 32 + k
                        t.wait_ge(cx.sem["ve"], v2_(th) + 16 + 16)
                        if gk >= 2:
                            pk = gk - 2
                            t.wait_ge(cx.sem["ve"], v2_(pk // 32) + 32 + (pk % 32) + 1)
                        inst = None
                        for ec in range(16):
                            inst = t.matmul(ps_o[gk % 2][:],
                                            wop[:, ec * D + dc * 128:ec * D + (dc + 1) * 128],
                                            ynh[:, ec * 2048 + t4 * 512:ec * 2048 + (t4 + 1) * 512],
                                            start=(ec == 0), stop=(ec == 15))
                        inst.then_inc(cx.sem["mm"], 1)

        @blk.vector
        def _(v):
            cx.wait_snap(v, bar)
            for tt in range(8):
                for ec in range(16):
                    i = tt * 16 + ec
                    v.wait_ge(cx.sem["ac"], a1(tt) + ec * 2 + 1)
                    v.tensor_mul(sz[:], yb[i % 2][:], sz[:]).then_inc(cx.sem["ve"], 1)
                v.wait_ge(cx.sem["mm"], m1(tt) + 17)
                v.tensor_copy(rstdb[:, tt * 512:(tt + 1) * 512], ps_b[:]).then_inc(cx.sem["ve"], 1)
            for th in range(2):
                for ec in range(16):
                    i = th * 16 + ec
                    v.wait_ge(cx.sem["ac"], a2_(th) + ec + 1)
                    v.tensor_mul(sz2[:], y2[i % 2][:], sz2[:]).then_inc(cx.sem["ve"], 1)
                    v.wait_ge(cx.sem["ve"], v2_(th) + ec * 2 + 1)
                    v.wait_ge(cx.sem["qs"], q0 + 16)
                    if th == 0 and ec == 0:
                        v.wait_ge(cx.sem["qg"], g0 + 16)
                    if th >= 1:
                        v.wait_ge(cx.sem["mm"], m0 + M1 + 32)
                    v.scalar_tensor_tensor(ynh[:, ec * 2048:(ec + 1) * 2048], sz2[:],
                                           nw[:, ec:ec + 1],
                                           rstdb[:, th * 2048:(th + 1) * 2048],
                                           ALU.mult, ALU.mult).then_inc(cx.sem["ve"], 1)
                for dc in range(8):
                    for t4 in range(4):
                        k = dc * 4 + t4
                        gk = th * 32 + k
                        v.wait_ge(cx.sem["mm"], m0 + M1 + gk + 1)
                        if gk >= 2:
                            v.wait_ge(cx.sem["qg"], g0 + 16 + (gk - 2 + 1) * 16)
                        v.tensor_copy(stg[k % 2][:], ps_o[gk % 2][:]).then_inc(cx.sem["ve"], 1)

        cx.n["qs"] = q0 + 16 + 256 * 16 + 32 * 2 * 16
        cx.n["qg"] = g0 + 16 + 64 * 16
        cx.n["mm"] = m0 + M1 + 64
        cx.n["ac"] = a0 + A1 + 32
        cx.n["ve"] = v0 + V1 + 2 * 64


def s12_ffn1(cx):
    nc = cx.nc
    C = cx.C
    bar = cx.snap()
    es = ExitStack()
    with es, nc.Block() as blk:
        xnT = cx.sb(es, "scx", [128, 8 * S], BF16)
        cbs = cx.sb(es, "scw", [128, 8 * D], BF16)
        scbr = cx.sb(es, "scs", [1, D], F32)
        fbr = cx.sb(es, "scf", [1, D], F32)
        scbb = cx.sb(es, "scsb", [128, D], F32)
        fbb = cx.sb(es, "scfb", [128, D], F32)
        x1t = [cx.sb(es, f"scx1{i}", [128, D], F32) for i in range(2)]
        s1c = [cx.sb(es, f"scs1{i}", [128, 8 * 128], F32) for i in range(2)]
        s1t = cx.sb(es, "scst", [128, D], F32)
        t1 = cx.sb(es, "sct1", [128, D], F32)
        g1 = cx.sb(es, "scg1", [128, D], F32)
        ot = [cx.sb(es, f"scot{i}", [128, D], F32) for i in range(2)]
        ps_g = [cx.ps(es, f"scpg{i}", [128, 1024]) for i in range(2)]
        ps_t = [cx.ps(es, f"scpt{i}", [128, 128]) for i in range(2)]
        ps_b = cx.ps(es, "scpb", [128, 1024])
        q0, g0, o0, m0, a0, v0 = (cx.n[k] for k in ("qs", "qg", "qo", "mm", "ac", "ve"))
        MT, VT = 10, 12
        mtc = lambda tc: m0 + 4 + tc * MT
        vtc = lambda tc: v0 + 4 + tc * VT

        @blk.sync
        def _(s):
            cx.wait_snap(s, bar)
            qoc = o0
            qoc = cx.sdma(s, scbr[:], cx.i["scbr"], qoc)
            qoc = cx.sdma(s, fbr[:], cx.i["fbiasr"], qoc)
            for dc in range(8):
                qoc = cx.sdma(s, xnT[:, dc * S:(dc + 1) * S],
                              cx.b["xn1t_bf"][:, dc * 128:(dc + 1) * 128], qoc, transpose=True)
            for tc in range(32):
                if tc >= 2:
                    s.wait_ge(cx.sem["ve"], vtc(tc - 2) + VT)
                s.dma_start(out=x1t[tc % 2][:], in_=cx.b["x1t"][tc * 128:(tc + 1) * 128, :]
                            ).then_inc(cx.sem["qs"], 16)
                s.dma_start(out=s1c[tc % 2].rearrange("p (c t) -> p c t", c=8),
                            in_=cx.b["s1c"].rearrange("(c p) t -> p c t", p=128)[
                                :, :, tc * 128:(tc + 1) * 128]).then_inc(cx.sem["qs"], 16)

        @blk.gpsimd
        def _(g):
            cx.wait_snap(g, bar)
            g.dma_start(out=cbs.rearrange("p (c o) -> p c o", c=8),
                        in_=cx.i["cbT"].rearrange("(c p) o -> p c o", c=8)).then_inc(cx.sem["qg"], 16)

        @blk.tensor
        def _(t):
            cx.wait_snap(t, bar)
            t.wait_ge(cx.sem["qo"], o0 + 32)
            for half in range(2):
                t.matmul(ps_b[:, half * 512:(half + 1) * 512], C["ones_l"][:],
                         scbr[0:1, half * 512:(half + 1) * 512], start=True, stop=True
                         ).then_inc(cx.sem["mm"], 1)
            t.wait_ge(cx.sem["ve"], v0 + 2)
            for half in range(2):
                t.matmul(ps_b[:, half * 512:(half + 1) * 512], C["ones_l"][:],
                         fbr[0:1, half * 512:(half + 1) * 512], start=True, stop=True
                         ).then_inc(cx.sem["mm"], 1)
            t.wait_ge(cx.sem["qo"], o0 + 160)
            t.wait_ge(cx.sem["qg"], g0 + 16)
            for tc in range(32):
                mb = mtc(tc)
                for oh in range(2):
                    gg = tc * 2 + oh
                    if gg >= 2:
                        t.wait_ge(cx.sem["ve"], vtc((gg - 2) // 2) + 9)
                    inst = None
                    for dc in range(8):
                        inst = t.matmul(ps_g[tc % 2][:, oh * 512:(oh + 1) * 512],
                                        xnT[:, dc * S + tc * 128:dc * S + (tc + 1) * 128],
                                        cbs[:, dc * D + oh * 512:dc * D + (oh + 1) * 512],
                                        start=(dc == 0), stop=(dc == 7))
                    inst.then_inc(cx.sem["mm"], 1)
                t.wait_ge(cx.sem["qs"], q0 + (tc + 1) * 32)
                for dc in range(8):
                    i = tc * 8 + dc
                    if i >= 2:
                        pv = i - 2
                        t.wait_ge(cx.sem["ve"], vtc(pv // 8) + (pv % 8) + 1)
                    t.transpose(ps_t[dc % 2][:], s1c[tc % 2][:, dc * 128:(dc + 1) * 128],
                                C["eyef"][:]).then_inc(cx.sem["mm"], 1)

        @blk.scalar
        def _(a):
            cx.wait_snap(a, bar)
            for tc in range(32):
                a.wait_ge(cx.sem["ve"], vtc(tc) + 10)
                if tc >= 1:
                    a.wait_ge(cx.sem["ve"], vtc(tc - 1) + 11)
                a.activation(g1[:], t1[:], AF.Silu).then_inc(cx.sem["ac"], 1)

        @blk.vector
        def _(v):
            cx.wait_snap(v, bar)
            for half in range(2):
                v.wait_ge(cx.sem["mm"], m0 + half + 1)
                v.tensor_copy(scbb[:, half * 512:(half + 1) * 512],
                              ps_b[:, half * 512:(half + 1) * 512]).then_inc(cx.sem["ve"], 1)
            for half in range(2):
                v.wait_ge(cx.sem["mm"], m0 + 2 + half + 1)
                v.tensor_copy(fbb[:, half * 512:(half + 1) * 512],
                              ps_b[:, half * 512:(half + 1) * 512]).then_inc(cx.sem["ve"], 1)
            for tc in range(32):
                vb = vtc(tc)
                mb = mtc(tc)
                for dc in range(8):
                    v.wait_ge(cx.sem["mm"], mb + 2 + dc + 1)
                    v.tensor_copy(s1t[:, dc * 128:(dc + 1) * 128], ps_t[dc % 2][:]
                                  ).then_inc(cx.sem["ve"], 1)
                v.wait_ge(cx.sem["mm"], mb + 2)
                if tc >= 1:
                    v.wait_ge(cx.sem["ac"], a0 + tc)
                v.scalar_tensor_tensor(t1[:], ps_g[tc % 2][:], 1.0 / 127.0, scbb[:],
                                       ALU.mult, ALU.mult).then_inc(cx.sem["ve"], 1)
                v.wait_ge(cx.sem["ve"], vb + 9)
                v.tensor_add(t1[:], t1[:], fbb[:]).then_inc(cx.sem["ve"], 1)
                v.wait_ge(cx.sem["ac"], a0 + tc + 1)
                v.tensor_mul(g1[:], g1[:], s1t[:]).then_inc(cx.sem["ve"], 1)
                v.wait_ge(cx.sem["ve"], vb + 11)
                if tc >= 2:
                    v.wait_ge(cx.sem["qg"], g0 + 16 + (tc - 2 + 1) * 16)
                v.tensor_add(ot[tc % 2][:], x1t[tc % 2][:], g1[:]).then_inc(cx.sem["ve"], 1)

        @blk.gpsimd
        def _(g):
            for tc in range(32):
                g.wait_ge(cx.sem["ve"], vtc(tc) + VT)
                g.dma_start(out=cx.b["outt"][tc * 128:(tc + 1) * 128, :], in_=ot[tc % 2][:]
                            ).then_inc(cx.sem["qg"], 16)
            g.wait_ge(cx.sem["qg"], g0 + 16 + 32 * 16)

        cx.n["qs"] = q0 + 32 * 32
        cx.n["qg"] = g0 + 16 + 32 * 16
        cx.n["qo"] = o0 + 160
        cx.n["mm"] = m0 + 4 + 32 * MT
        cx.n["ac"] = a0 + 32
        cx.n["ve"] = v0 + 4 + 32 * VT


def s6_moe(cx):
    nc = cx.nc
    C = cx.C
    bar = cx.snap()
    es = ExitStack()
    with es, nc.Block() as blk:
        w1sb = cx.sb(es, "s6w1", [128, 8 * FF], BF16)
        w2sb = cx.sb(es, "s6w2", [128, 32 * D], BF16)
        b1sb = [cx.sb(es, f"s6b{i}", [128, 32], F32) for i in range(2)]
        b2b = cx.sb(es, "s6b2", [1, D], BF16)
        xnT = [cx.sb(es, f"s6x{i}", [128, 8 * 512], BF16) for i in range(2)]
        hst = [cx.sb(es, f"s6h{i}", [128, 8 * 512], BF16) for i in range(2)]
        hsb = [cx.sb(es, f"s6g{i}", [128, 32 * 128], BF16) for i in range(2)]
        msal = cx.sb(es, "s6m", [128, 32 * 8], F32)
        tmp = [cx.sb(es, f"s6t{i}", [128, D], F32) for i in range(2)]
        ps4 = [cx.ps(es, f"s6p{i}", [128, 512]) for i in range(4)]
        psY = [cx.ps(es, f"s6q{i}", [128, 1024]) for i in range(2)]
        q0, g0, o0, m0, a0, v0 = (cx.n[k] for k in ("qs", "qg", "qo", "mm", "ac", "ve"))
        MME = 320

        def mmh(e, tt, fc):
            return m0 + e * MME + tt * 32 + fc + 1

        def mmy(e, tc, oh):
            return m0 + e * MME + 256 + tc * 2 + oh + 1

        def qbase(e):
            return g0 + (16 + e * 80) * 16

        def qg_piece(e, tt, p4):
            return qbase(e) + (tt * 4 + p4 + 1) * 16

        def qg_w1(e):
            return g0 + 8 * 16 if e == 0 else qbase(e - 1) + 40 * 16

        def qg_w2(e):
            return g0 + 16 * 16 if e == 0 else qbase(e - 1) + 80 * 16

        def qg_acc(e, tc):
            return qbase(e) + (40 + tc + 1) * 16

        def vey(e, tc):
            return v0 + e * 32 + tc + 1

        def ach(e, tt, fc):
            return a0 + e * 256 + tt * 32 + fc + 1

        @blk.gpsimd
        def _(g):
            cx.wait_snap(g, bar)
            qoc = o0
            for dc in range(8):
                g.dma_start(out=w1sb[:, dc * FF:(dc + 1) * FF],
                            in_=cx.i["w1e"][0][dc * 128:(dc + 1) * 128, :]).then_inc(cx.sem["qg"], 16)
            for j in range(8):
                g.dma_start(out=w2sb.rearrange("p (c o) -> p c o", c=32)[:, j * 4:(j + 1) * 4, :],
                            in_=cx.i["w2e"][0][j * 512:(j + 1) * 512, :].rearrange(
                                "(c p) o -> p c o", c=4)).then_inc(cx.sem["qg"], 16)
            for e in range(NE):
                if e >= 2:
                    g.wait_ge(cx.sem["ac"], a0 + (e - 1) * 256)
                qoc = cx.sdma(g, b1sb[e % 2][:], cx.i["b1c"][e], qoc)
                if e >= 1:
                    g.wait_ge(cx.sem["mm"], m0 + (e - 1) * MME + 256 + 64)
                qoc = cx.sdma(g, b2b[:], cx.i["b2r"][e:e + 1, :], qoc)
                for tt in range(8):
                    for p4 in range(4):
                        g.wait_ge(cx.sem["ac"], ach(e, tt, p4 * 8 + 7))
                        g.dma_start(
                            out=cx.b["hbuf"][e].rearrange("(fc p) t -> p fc t", p=128)[
                                :, p4 * 8:(p4 + 1) * 8, tt * 512:(tt + 1) * 512],
                            in_=hst[(e * 32 + tt * 4 + p4) % 2].rearrange(
                                "p (fc t) -> p fc t", t=512)).then_inc(cx.sem["qg"], 16)
                if e + 1 < NE:
                    g.wait_ge(cx.sem["mm"], mmh(e, 7, 31))
                    for dc in range(8):
                        g.dma_start(out=w1sb[:, dc * FF:(dc + 1) * FF],
                                    in_=cx.i["w1e"][e + 1][dc * 128:(dc + 1) * 128, :]
                                    ).then_inc(cx.sem["qg"], 16)
                else:
                    for dc in range(8):
                        g.dma_start(out=cx.b["junk"][:, 0:64],
                                    in_=hst[0][:, 0:64]).then_inc(cx.sem["qg"], 16)
                for tc in range(32):
                    g.wait_ge(cx.sem["ve"], vey(e, tc))
                    g.dma_start(out=cx.b["selt"][tc * 128:(tc + 1) * 128, :],
                                in_=tmp[tc % 2][:],
                                accum_op=(ALU.bypass if e == 0 else ALU.add)
                                ).then_inc(cx.sem["qg"], 16)
                if e + 1 < NE:
                    g.wait_ge(cx.sem["mm"], mmy(e, 31, 1))
                    for j in range(8):
                        g.dma_start(out=w2sb.rearrange("p (c o) -> p c o", c=32)[:, j * 4:(j + 1) * 4, :],
                                    in_=cx.i["w2e"][e + 1][j * 512:(j + 1) * 512, :].rearrange(
                                        "(c p) o -> p c o", c=4)).then_inc(cx.sem["qg"], 16)
                else:
                    for j in range(8):
                        g.dma_start(out=cx.b["junk"][:, 64:128],
                                    in_=hst[0][:, 64:128]).then_inc(cx.sem["qg"], 16)
            g.wait_ge(cx.sem["qg"], qbase(NE - 1) + 80 * 16)

        @blk.sync
        def _(s):
            cx.wait_snap(s, bar)
            qoc = o0
            s.dma_start(out=msal.rearrange("p (c e) -> p c e", e=8),
                        in_=cx.b["msb"].rearrange("(c p) e -> p c e", p=128)
                        ).then_inc(cx.sem["qs"], 16)
            for e in range(NE):
                for tt in range(8):
                    ge = e * 8 + tt
                    if ge >= 2:
                        s.wait_ge(cx.sem["mm"], mmh((ge - 2) // 8, (ge - 2) % 8, 31))
                    s.dma_start(out=xnT[tt % 2].rearrange("p (c t) -> p c t", c=8),
                                in_=cx.b["xn0c_bf"].rearrange("(c p) t -> p c t", p=128)[
                                    :, :, tt * 512:(tt + 1) * 512]).then_inc(cx.sem["qs"], 16)
                for tc in range(32):
                    gh = e * 32 + tc
                    if gh >= 2:
                        s.wait_ge(cx.sem["mm"], mmy((gh - 2) // 32, (gh - 2) % 32, 1))
                    s.wait_ge(cx.sem["qg"], qg_piece(e, tc // 4, 3))
                    s.dma_start(out=hsb[gh % 2].rearrange("p (c t) -> p c t", c=32),
                                in_=cx.b["hbuf"][e].rearrange("(c p) t -> p c t", p=128)[
                                    :, :, tc * 128:(tc + 1) * 128]).then_inc(cx.sem["qs"], 16)

        @blk.tensor
        def _(t):
            cx.wait_snap(t, bar)
            for e in range(NE):
                t.wait_ge(cx.sem["qg"], qg_w1(e))
                t.wait_ge(cx.sem["qo"], o0 + (e * 2 + 1) * 16)
                for tt in range(8):
                    ge = e * 8 + tt
                    t.wait_ge(cx.sem["qs"], q0 + 16 + (e * 40 + tt + 1) * 16)
                    for fc in range(32):
                        gi = ge * 32 + fc
                        if gi >= 4:
                            pv = gi - 4
                            t.wait_ge(cx.sem["ac"], a0 + pv + 1)
                        inst = None
                        for dc in range(8):
                            inst = t.matmul(ps4[fc % 4][:],
                                            w1sb[:, dc * FF + fc * 128:dc * FF + (fc + 1) * 128],
                                            xnT[tt % 2][:, dc * 512:(dc + 1) * 512],
                                            start=(dc == 0), stop=(dc == 7))
                        inst.then_inc(cx.sem["mm"], 1)
                t.wait_ge(cx.sem["qg"], qg_w2(e))
                t.wait_ge(cx.sem["qo"], o0 + (e * 2 + 2) * 16)
                for tc in range(32):
                    gh = e * 32 + tc
                    t.wait_ge(cx.sem["qs"], q0 + 16 + (e * 40 + 8 + tc + 1) * 16)
                    if gh >= 2:
                        t.wait_ge(cx.sem["ve"], vey((gh - 2) // 32, (gh - 2) % 32))
                    for oh in range(2):
                        inst = None
                        for fc in range(32):
                            inst = t.matmul(psY[tc % 2][:, oh * 512:(oh + 1) * 512],
                                            hsb[gh % 2][:, fc * 128:(fc + 1) * 128],
                                            w2sb[:, fc * D + oh * 512:fc * D + (oh + 1) * 512],
                                            start=(fc == 0), stop=False)
                        inst = t.matmul(psY[tc % 2][:, oh * 512:(oh + 1) * 512],
                                        C["ones_lb"][:], b2b[0:1, oh * 512:(oh + 1) * 512],
                                        start=False, stop=True)
                        inst.then_inc(cx.sem["mm"], 1)

        @blk.scalar
        def _(a):
            cx.wait_snap(a, bar)
            for e in range(NE):
                for tt in range(8):
                    for fc in range(32):
                        gi = (e * 8 + tt) * 32 + fc
                        pp = (e * 32 + tt * 4 + fc // 8)
                        a.wait_ge(cx.sem["mm"], mmh(e, tt, fc))
                        if pp >= 2:
                            pe2 = (pp - 2) // 32
                            rem = (pp - 2) % 32
                            a.wait_ge(cx.sem["qg"], qg_piece(pe2, rem // 4, rem % 4))
                        a.activation(hst[pp % 2][:, (fc % 8) * 512:(fc % 8 + 1) * 512],
                                     ps4[fc % 4][:], AF.Gelu,
                                     bias=b1sb[e % 2][:, fc:fc + 1]).then_inc(cx.sem["ac"], 1)

        @blk.vector
        def _(v):
            cx.wait_snap(v, bar)
            v.wait_ge(cx.sem["qs"], q0 + 16)
            for e in range(NE):
                for tc in range(32):
                    v.wait_ge(cx.sem["mm"], mmy(e, tc, 1))
                    gh = e * 32 + tc
                    if gh >= 2:
                        v.wait_ge(cx.sem["qg"], qg_acc((gh - 2) // 32, (gh - 2) % 32))
                    v.tensor_scalar(tmp[tc % 2][:], psY[tc % 2][:],
                                    msal[:, tc * 8 + e:tc * 8 + e + 1], None, ALU.mult
                                    ).then_inc(cx.sem["ve"], 1)

        cx.n["qs"] = q0 + 16 + NE * 40 * 16
        cx.n["qg"] = g0 + (16 + NE * 80) * 16
        cx.n["qo"] = o0 + NE * 2 * 16
        cx.n["mm"] = m0 + NE * MME
        cx.n["ac"] = a0 + NE * 256
        cx.n["ve"] = v0 + NE * 32
        cx.marks["pre_s6_bar"] = bar
        cx.marks["selt_done"] = lambda tc: qg_acc(NE - 1, tc)


STAGE_FNS = ["s1_ln0", "s2_qkv", "s2b_norms", "s3_attn", "s4_attnout", "s5_route",
             "s6_moe", "s7_combine0", "s8_inproj", "s9_conv",
             "s10_ssd", "s11_gate", "s12_ffn1"]


def _build():
    nc = bass.Bass("TRN2", target_bir_lowering=False)
    es = ExitStack()
    with es:
        cx = Ctx(nc, es)
        declare(nc, cx, {"outt": "ExternalOutput"})
        s0_prelude(cx)
        for fn in STAGE_FNS:
            globals()[fn](cx)
    return nc


def kernel(**inputs):
    sh, xs = prep_inputs(inputs)
    nc = _build()
    in_maps = [{**sh, "x_tok": xs[c]} for c in range(8)]
    res = run_bass_kernel_spmd(nc, in_maps, core_ids=list(range(8)))
    out = np.stack([np.asarray(res.results[c]["outt"]) for c in range(8)], axis=0)
    return out.astype(np.float32)


# version 06 (final)
# speedup vs baseline: 1.1867x; 1.1867x over previous
import numpy as np
import ml_dtypes
from contextlib import ExitStack
import concourse.bass as bass
import concourse.mybir as mybir
from concourse.bass_utils import run_bass_kernel_spmd

F32, BF16 = mybir.dt.float32, mybir.dt.bfloat16
AF = mybir.ActivationFunctionType
ALU = mybir.AluOpType
AX = mybir.AxisListType.X

S, D = 4096, 1024
NH, HD = 16, 64
NE, FF = 8, 4096
DIN, DST = 2048, 64
CONVD = DIN + 2 * DST
DPROJ = 2 * DIN + 2 * DST + NH
CH, NCH = 256, 16
EPS = 1e-5

BUFS = {
    "xn0t_f32": ((S, D), F32),
    "xn0t_bf": ((S, D), BF16),
    "xn0c_bf": ((D, S), BF16),
    "ln2t_bf": ((S, D), BF16),
    "qkT": ((2 * D, S), BF16),
    "qn2d": ((NH, S), F32),
    "kn2d": ((NH, S), F32),
    "crow": ((NH, S), BF16),
    "vtok": ((NH, S, HD), BF16),
    "ctxc": ((D, S), BF16),
    "s0t": ((S, D), F32),
    "msb": ((S, 8), F32),
    "hbuf": ((NE, FF, S), BF16),
    "selt": ((S, D), F32),
    "x1t": ((S, D), F32),
    "xn1t_bf": ((S, D), BF16),
    "zc": ((DIN, S), F32),
    "xbc": ((CONVD, S), F32),
    "dtraw": ((NH, S), F32),
    "dtv": ((NH, S), F32),
    "acumd": ((NH, S), F32),
    "xs_bf": ((DIN, S), BF16),
    "xs_f32": ((DIN, S), F32),
    "bcbf": ((2 * DST, S), BF16),
    "ybuf": ((DIN, S), F32),
    "s1c": ((D, S), F32),
    "outt": ((S, D), F32),
    "junk": ((128, 4096), BF16),
}

INPUT_SHAPES = {
    "x_tok": (S, D), "gamr": (1, D), "betr": (1, D), "wqkvT": (D, 3 * D),
    "bqkvr": (1, 3 * D), "woutT": (D, D), "boutr": (1, D), "wred": (D, 16),
    "wg8": (8, 16), "w1e": (NE, D, FF), "b1c": (NE, 128, 32), "w2e": (NE, FF, D),
    "b2r": (NE, D), "winT": (D, DPROJ), "convw": (128, 17, 4), "convb": (128, 17),
    "dtb": (NH, 1), "alog": (NH, 1), "dpr": (1, NH), "normw": (128, 16),
    "woprojT": (DIN, D), "cbT": (D, D), "scbr": (1, D), "fbiasr": (1, D),
    "eyeb": (128, 128), "eyef": (128, 128), "hsel3": (16, 16 * 128),
    "trimask": (128, 2 * CH),
}
INPUT_DTYPES = {nm: (BF16 if nm == "eyeb" else F32) for nm in INPUT_SHAPES}


def prep_inputs(ins):
    f32 = np.float32
    A = np.ascontiguousarray
    sh = {}
    sh["gamr"] = A(ins["L0_attn_ln_gamma"].reshape(1, D), f32)
    sh["betr"] = A(ins["L0_attn_ln_beta"].reshape(1, D), f32)
    sh["wqkvT"] = A(np.transpose(ins["L0_attn_W_qkv"]), f32)
    sh["bqkvr"] = A(ins["L0_attn_b_qkv"].reshape(1, 3 * D), f32)
    sh["woutT"] = A(np.transpose(ins["L0_attn_W_out"]), f32)
    sh["boutr"] = A(ins["L0_attn_b_out"].reshape(1, D), f32)
    sh["wred"] = A(ins["L0_ffn_w_red"], f32)
    sh["wg8"] = A(ins["L0_ffn_wg"], f32)
    sh["w1e"] = A(ins["L0_ffn_w1"], f32)
    sh["b1c"] = A(ins["L0_ffn_b1"].reshape(NE, 32, 128).transpose(0, 2, 1), f32)
    sh["w2e"] = A(ins["L0_ffn_w2"], f32)
    sh["b2r"] = A(ins["L0_ffn_b2"], f32)
    sh["winT"] = A(np.transpose(ins["L1_attn_in_proj_w"]), f32)
    sh["convw"] = A(ins["L1_attn_conv_w"].reshape(17, 128, 4).transpose(1, 0, 2), f32)
    sh["convb"] = A(ins["L1_attn_conv_b"].reshape(17, 128).T, f32)
    sh["dtb"] = A(ins["L1_attn_dt_bias"].reshape(NH, 1), f32)
    sh["alog"] = A(ins["L1_attn_A_log"].reshape(NH, 1), f32)
    sh["dpr"] = A(ins["L1_attn_Dp"].reshape(1, NH), f32)
    sh["normw"] = A(ins["L1_attn_norm_w"].reshape(16, 128).T, f32)
    sh["woprojT"] = A(np.transpose(ins["L1_attn_out_proj_w"]), f32)
    sh["cbT"] = A(np.transpose(ins["L1_ffn_CB"]), f32)
    sh["scbr"] = A(ins["L1_ffn_SCB"].reshape(1, D), f32)
    sh["fbiasr"] = A(ins["L1_ffn_bias"].reshape(1, D), f32)
    sh["eyeb"] = np.eye(128).astype(ml_dtypes.bfloat16)
    sh["eyef"] = np.eye(128).astype(f32)
    hs = np.zeros((16, 16, 128), f32)
    for h in range(16):
        hs[h, h, :] = 1.0
    sh["hsel3"] = A(hs.transpose(1, 0, 2).reshape(16, 16 * 128))
    tm = np.zeros((2, 128, CH), f32)
    for j in range(2):
        for p in range(128):
            tm[j, p, : j * 128 + p] = -1e9
    sh["trimask"] = A(tm.transpose(1, 0, 2).reshape(128, 2 * CH))
    xs = [A(ins["L0_attn_hidden_states"][c], f32) for c in range(8)]
    return sh, xs


class Ctx:
    def __init__(self, nc, es):
        self.nc = nc
        self.es = es
        self.marks = {}
        self.b = {}
        self.i = {}
        self.sem = {}
        self.n = {}
        for s_ in ("qs", "qg", "qo", "mm", "ac", "ve", "gp"):
            self.sem[s_] = es.enter_context(nc.semaphore(s_))
            self.n[s_] = 0
        self.C = {}

    def snap(self):
        return dict(self.n)

    def wait_snap(self, eng, snap):
        for s_, v in snap.items():
            if v > 0:
                eng.wait_ge(self.sem[s_], v)

    def sb(self, es, name, shape, dtype, side=None):
        if side:
            return es.enter_context(self.nc.sbuf_tensor(name, shape, dtype, side=side))
        return es.enter_context(self.nc.sbuf_tensor(name, shape, dtype))

    def ps(self, es, name, shape, dtype=F32):
        return es.enter_context(self.nc.psum_tensor(name, shape, dtype))

    def sdma(self, eng, out, in_, qoc, **kw):
        eng.dma_start(out=out, in_=in_, **kw).then_inc(self.sem["qo"], 16)
        qoc += 16
        eng.wait_ge(self.sem["qo"], qoc)
        return qoc


def declare(nc, cx, io_kinds):
    for nm, shp in INPUT_SHAPES.items():
        cx.i[nm] = nc.dram_tensor(nm, list(shp), INPUT_DTYPES[nm], kind="ExternalInput").ap()
    for nm, (shp, dt) in BUFS.items():
        kind = io_kinds.get(nm, "Internal")
        cx.b[nm] = nc.dram_tensor(nm, list(shp), dt, kind=kind).ap()


def s0_prelude(cx):
    nc = cx.nc
    C = cx.C
    es = cx.es
    C["eyeb"] = cx.sb(es, "c_eyeb", [128, 128], BF16)
    C["eyef"] = cx.sb(es, "c_eyef", [128, 128], F32)
    C["hsel"] = cx.sb(es, "c_hsel", [16, 16 * 128], F32)
    C["trim"] = cx.sb(es, "c_trim", [128, 2 * CH], F32)
    C["ones_l"] = cx.sb(es, "c_onl", [1, 128], F32)
    C["ones_lb"] = cx.sb(es, "c_onlb", [1, 128], BF16)
    C["ones_c"] = cx.sb(es, "c_onc", [128, 1], F32)
    C["ones_cb"] = cx.sb(es, "c_oncb", [128, 1], BF16)
    C["ones_rb"] = cx.sb(es, "c_onrb", [1, 512], BF16)
    C["ones2"] = cx.sb(es, "c_on2", [128, 2], F32)
    C["ones2b"] = cx.sb(es, "c_on2b", [128, 2], BF16)
    C["dpb"] = cx.sb(es, "c_dpb", [128, NH], F32)
    C["cv"] = cx.sb(es, "c_cv", [128, 8], F32)
    C["dbgf"] = cx.sb(es, "c_dbgf", [128, 8], F32)
    C["dprow"] = cx.sb(es, "c_dprow", [1, NH], F32)
    pes = ExitStack()
    pb = cx.ps(pes, "p_pre", [128, 512])

    bar = cx.snap()
    with nc.Block() as blk:
        @blk.sync
        def _(s):
            qoc = cx.n["qo"]
            for dst, src in ((C["eyeb"][:], cx.i["eyeb"]), (C["eyef"][:], cx.i["eyef"]),
                             (C["hsel"][:], cx.i["hsel3"]), (C["trim"][:], cx.i["trimask"]),
                             (C["dprow"][:], cx.i["dpr"])):
                qoc = cx.sdma(s, dst, src, qoc)
        cx.n["qo"] += 16 * 5

        @blk.vector
        def _(v):
            v.memset(C["ones_l"][:], 1.0)
            v.memset(C["ones_lb"][:], 1.0)
            v.memset(C["ones_c"][:], 1.0)
            v.memset(C["ones_cb"][:], 1.0)
            v.memset(C["ones_rb"][:], 1.0)
            v.memset(C["cv"][:, 0:1], EPS)
            v.memset(C["cv"][:, 1:2], -0.5)
            v.memset(C["cv"][:, 2:3], 0.125)
            v.memset(C["cv"][:, 3:4], -1.0)
            v.memset(C["cv"][:, 4:5], 1.0 / 2048.0)
            v.memset(C["cv"][:, 5:6], 1.0)
            v.memset(C["ones2"][:], 0.0)
            v.memset(C["ones2"][0:64, 0:1], 1.0)
            v.memset(C["ones2"][64:128, 1:2], 1.0)
            v.memset(C["ones2b"][:], 0.0)
            v.memset(C["ones2b"][0:64, 0:1], 1.0)
            v.memset(C["ones2b"][64:128, 1:2], 1.0).then_inc(cx.sem["ve"], 1)
        cx.n["ve"] += 1

        @blk.tensor
        def _(t):
            t.wait_ge(cx.sem["qo"], cx.n["qo"])
            t.wait_ge(cx.sem["ve"], cx.n["ve"])
            t.matmul(pb[:, 0:NH], C["ones_l"][:], C["dprow"][:],
                     start=True, stop=True).then_inc(cx.sem["mm"], 1)
        cx.n["mm"] += 1

        @blk.vector
        def _(v):
            v.wait_ge(cx.sem["mm"], cx.n["mm"])
            v.tensor_copy(C["dpb"][:], pb[:, 0:NH]).then_inc(cx.sem["ve"], 1)
        cx.n["ve"] += 1
    pes.close()


def bcast_row_f32(cx, blk_t, blk_v, row_ap, out_sb, psum, ncols, m0, v0):
    raise NotImplementedError


def s1_ln0(cx):
    nc = cx.nc
    C = cx.C
    bar = cx.snap()
    es = ExitStack()
    NV, NA = 14, 6
    with es, nc.Block() as blk:
        xt = [cx.sb(es, f"s1x{i}", [128, D], F32) for i in range(2)]
        xn = [cx.sb(es, f"s1n{i}", [128, D], F32) for i in range(2)]
        l2 = [cx.sb(es, f"s1l{i}", [128, D], BF16) for i in range(2)]
        sq = [cx.sb(es, f"s1q{i}", [128, D], F32) for i in range(2)]
        st = cx.sb(es, "s1st", [128, 32], F32)
        t1 = [cx.sb(es, f"s1t{i}", [128, D], F32) for i in range(2)]
        cgam = cx.sb(es, "s1cg", [128, D], F32)
        cbet = cx.sb(es, "s1cb", [128, D], F32)
        grow = cx.sb(es, "s1gr", [1, D], F32)
        brow = cx.sb(es, "s1br", [1, D], F32)
        pb = cx.ps(es, "s1pb", [128, 1024])

        q0, g0, m0, a0, v0 = (cx.n[k] for k in ("qs", "qg", "mm", "ac", "ve"))
        o0 = cx.n["qo"]

        def vidx(tc, k):
            return v0 + 4 + (tc // 2) * 2 * NV + (k - 1) * 2 + (tc % 2) + 1

        def aidx(tc, k):
            return a0 + (tc // 2) * 2 * NA + (k - 1) * 2 + (tc % 2) + 1

        @blk.sync
        def _(s):
            cx.wait_snap(s, bar)
            qoc = o0
            qoc = cx.sdma(s, grow[:], cx.i["gamr"], qoc)
            qoc = cx.sdma(s, brow[:], cx.i["betr"], qoc)
            for tc in range(32):
                if tc >= 2:
                    s.wait_ge(cx.sem["ve"], vidx(tc - 2, NV))
                s.dma_start(out=xt[tc % 2][:], in_=cx.i["x_tok"][tc * 128:(tc + 1) * 128, :]
                            ).then_inc(cx.sem["qs"], 16)

        @blk.tensor
        def _(t):
            cx.wait_snap(t, bar)
            t.wait_ge(cx.sem["qo"], o0 + 32)
            for half in range(2):
                t.matmul(pb[:, half * 512:(half + 1) * 512], C["ones_l"][:],
                         grow[:, half * 512:(half + 1) * 512], start=True, stop=True
                         ).then_inc(cx.sem["mm"], 1)
            t.wait_ge(cx.sem["ve"], v0 + 2)
            for half in range(2):
                t.matmul(pb[:, half * 512:(half + 1) * 512], C["ones_l"][:],
                         brow[:, half * 512:(half + 1) * 512], start=True, stop=True
                         ).then_inc(cx.sem["mm"], 1)

        @blk.vector
        def _(v):
            cx.wait_snap(v, bar)
            for half in range(2):
                v.wait_ge(cx.sem["mm"], m0 + half + 1)
                v.tensor_copy(cgam[:, half * 512:(half + 1) * 512],
                              pb[:, half * 512:(half + 1) * 512]).then_inc(cx.sem["ve"], 1)
            for half in range(2):
                v.wait_ge(cx.sem["mm"], m0 + 2 + half + 1)
                v.tensor_copy(cbet[:, half * 512:(half + 1) * 512],
                              pb[:, half * 512:(half + 1) * 512]).then_inc(cx.sem["ve"], 1)

            def ve_ops(tc):
                b = tc % 2
                so = (tc % 2) * 16
                yield lambda: (v.wait_ge(cx.sem["qs"], q0 + (tc + 1) * 16),
                               v.reduce_sum(st[:, so:so + 1], xt[b][:], axis=AX).then_inc(cx.sem["ve"], 1))
                yield lambda: (v.wait_ge(cx.sem["ac"], aidx(tc, 1)),
                               v.reduce_sum(st[:, so + 1:so + 2], sq[b][:], axis=AX).then_inc(cx.sem["ve"], 1))
                yield lambda: (v.wait_ge(cx.sem["ve"], vidx(tc, 2)),
                               v.tensor_scalar(st[:, so + 2:so + 4], st[:, so:so + 2], 1.0 / D,
                                               None, ALU.mult).then_inc(cx.sem["ve"], 1))
                yield lambda: (v.wait_ge(cx.sem["ve"], vidx(tc, 3)),
                               v.tensor_mul(st[:, so + 4:so + 5], st[:, so + 2:so + 3],
                                            st[:, so + 2:so + 3]).then_inc(cx.sem["ve"], 1))
                yield lambda: (v.wait_ge(cx.sem["ve"], vidx(tc, 4)),
                               v.tensor_sub(st[:, so + 5:so + 6], st[:, so + 3:so + 4],
                                            st[:, so + 4:so + 5]).then_inc(cx.sem["ve"], 1))
                def op6():
                    v.wait_ge(cx.sem["ac"], aidx(tc, 3))
                    if tc >= 2:
                        v.wait_ge(cx.sem["qg"], g0 + 16 * ((tc - 2) * 3 + 2))
                    v.tensor_scalar(xn[b][:], xt[b][:], st[:, so + 2:so + 3], st[:, so + 7:so + 8],
                                    ALU.subtract, ALU.mult).then_inc(cx.sem["ve"], 1)
                yield op6
                yield lambda: (v.wait_ge(cx.sem["ve"], vidx(tc, 6)),
                               v.reduce_sum(st[:, so + 8:so + 9], xn[b][:], axis=AX).then_inc(cx.sem["ve"], 1))
                yield lambda: (v.wait_ge(cx.sem["ac"], aidx(tc, 4)),
                               v.reduce_sum(st[:, so + 9:so + 10], sq[b][:], axis=AX).then_inc(cx.sem["ve"], 1))
                yield lambda: (v.wait_ge(cx.sem["ve"], vidx(tc, 8)),
                               v.tensor_scalar(st[:, so + 10:so + 12], st[:, so + 8:so + 10], 1.0 / D,
                                               None, ALU.mult).then_inc(cx.sem["ve"], 1))
                yield lambda: (v.wait_ge(cx.sem["ve"], vidx(tc, 9)),
                               v.tensor_mul(st[:, so + 12:so + 13], st[:, so + 10:so + 11],
                                            st[:, so + 10:so + 11]).then_inc(cx.sem["ve"], 1))
                yield lambda: (v.wait_ge(cx.sem["ve"], vidx(tc, 10)),
                               v.tensor_sub(st[:, so + 13:so + 14], st[:, so + 11:so + 12],
                                            st[:, so + 12:so + 13]).then_inc(cx.sem["ve"], 1))
                yield lambda: (v.wait_ge(cx.sem["ac"], aidx(tc, 6)),
                               v.tensor_scalar(t1[b][:], xn[b][:], st[:, so + 10:so + 11],
                                               st[:, so + 15:so + 16], ALU.subtract, ALU.mult
                                               ).then_inc(cx.sem["ve"], 1))
                yield lambda: (v.wait_ge(cx.sem["ve"], vidx(tc, 12)),
                               v.tensor_mul(t1[b][:], t1[b][:], cgam[:]).then_inc(cx.sem["ve"], 1))
                def op14():
                    v.wait_ge(cx.sem["ve"], vidx(tc, 13))
                    if tc >= 2:
                        v.wait_ge(cx.sem["qg"], g0 + 16 * ((tc - 2) * 3 + 3))
                    v.tensor_add(l2[b][:], t1[b][:], cbet[:]).then_inc(cx.sem["ve"], 1)
                yield op14

            for p in range(16):
                for a_, b_ in zip(ve_ops(2 * p), ve_ops(2 * p + 1)):
                    a_()
                    b_()

        @blk.scalar
        def _(a):
            cx.wait_snap(a, bar)

            def ac_ops(tc):
                b = tc % 2
                so = (tc % 2) * 16
                def op1():
                    a.wait_ge(cx.sem["qs"], q0 + (tc + 1) * 16)
                    if tc >= 2:
                        a.wait_ge(cx.sem["ve"], vidx(tc - 2, 8))
                    a.activation(sq[b][:], xt[b][:], AF.Square).then_inc(cx.sem["ac"], 1)
                yield op1
                yield lambda: (a.wait_ge(cx.sem["ve"], vidx(tc, 5)),
                               a.activation(st[:, so + 6:so + 7], st[:, so + 5:so + 6], AF.Ln,
                                            bias=C["cv"][:, 0:1]).then_inc(cx.sem["ac"], 1))
                yield lambda: (a.wait_ge(cx.sem["ac"], aidx(tc, 2)),
                               a.activation(st[:, so + 7:so + 8], st[:, so + 6:so + 7], AF.Exp,
                                            scale=C["cv"][:, 1:2]).then_inc(cx.sem["ac"], 1))
                yield lambda: (a.wait_ge(cx.sem["ve"], vidx(tc, 6)),
                               a.activation(sq[b][:], xn[b][:], AF.Square).then_inc(cx.sem["ac"], 1))
                yield lambda: (a.wait_ge(cx.sem["ve"], vidx(tc, 11)),
                               a.activation(st[:, so + 14:so + 15], st[:, so + 13:so + 14], AF.Ln,
                                            bias=C["cv"][:, 0:1]).then_inc(cx.sem["ac"], 1))
                yield lambda: (a.wait_ge(cx.sem["ac"], aidx(tc, 5)),
                               a.activation(st[:, so + 15:so + 16], st[:, so + 14:so + 15], AF.Exp,
                                            scale=C["cv"][:, 1:2]).then_inc(cx.sem["ac"], 1))

            for p in range(16):
                for a_, b_ in zip(ac_ops(2 * p), ac_ops(2 * p + 1)):
                    a_()
                    b_()

        @blk.gpsimd
        def _(g):
            cx.wait_snap(g, bar)
            for tc in range(32):
                b = tc % 2
                rows = slice(tc * 128, (tc + 1) * 128)
                g.wait_ge(cx.sem["ve"], vidx(tc, 6))
                g.dma_start(out=cx.b["xn0t_f32"][rows, :], in_=xn[b][:]).then_inc(cx.sem["qg"], 16)
                g.dma_start(out=cx.b["xn0t_bf"][rows, :], in_=xn[b][:]).then_inc(cx.sem["qg"], 16)
                g.wait_ge(cx.sem["ve"], vidx(tc, NV))
                g.dma_start(out=cx.b["ln2t_bf"][rows, :], in_=l2[b][:]).then_inc(cx.sem["qg"], 16)

        cx.n["qs"] = q0 + 32 * 16
        cx.n["qo"] = o0 + 32
        cx.n["qg"] = g0 + 96 * 16
        cx.n["mm"] = m0 + 4
        cx.n["ac"] = a0 + 32 * NA
        cx.n["ve"] = v0 + 4 + 32 * NV


def s2_qkv(cx):
    nc = cx.nc
    C = cx.C
    bar = cx.snap()
    es = ExitStack()
    with es, nc.Block() as blk:
        ln2T = cx.sb(es, "s2ln", [128, 8 * S], BF16)
        wsb = cx.sb(es, "s2w", [128, 8 * 3 * D], BF16)
        bqb = cx.sb(es, "s2bq", [1, 3 * D], BF16)
        qk = [cx.sb(es, f"s2qk{i}", [128, 512], BF16) for i in range(2)]
        vst = [cx.sb(es, f"s2v{i}", [128, 512], BF16) for i in range(2)]
        p2 = [cx.ps(es, f"s2p{i}", [128, 512]) for i in range(2)]
        pv = [cx.ps(es, f"s2pv{i}", [128, 512]) for i in range(2)]

        q0, g0, m0, a0, v0 = (cx.n[k] for k in ("qs", "qg", "mm", "ac", "ve"))
        NG = 16 * 8
        NV = 32 * 2

        @blk.sync
        def _(s):
            cx.wait_snap(s, bar)
            for dc in range(8):
                s.dma_start(out=ln2T[:, dc * S:(dc + 1) * S],
                            in_=cx.b["ln2t_bf"][:, dc * 128:(dc + 1) * 128],
                            transpose=True).then_inc(cx.sem["qs"], 16)

        o0 = cx.n["qo"]

        @blk.gpsimd
        def _(g):
            cx.wait_snap(g, bar)
            g.dma_start(out=wsb.rearrange("p (c o) -> p c o", c=8),
                        in_=cx.i["wqkvT"].rearrange("(c p) o -> p c o", c=8)
                        ).then_inc(cx.sem["qg"], 16)
            cx.sdma(g, bqb[:], cx.i["bqkvr"], o0)
            for g_i in range(NG):
                g.wait_ge(cx.sem["ve"], v0 + g_i + 1)
                oc, tt = divmod(g_i, 8)
                g.dma_start(out=cx.b["qkT"][oc * 128:(oc + 1) * 128, tt * 512:(tt + 1) * 512],
                            in_=qk[g_i % 2][:]).then_inc(cx.sem["qg"], 16)
            for gv in range(NV):
                g.wait_ge(cx.sem["ve"], v0 + NG + gv + 1)
                tc, oh = divmod(gv, 2)
                g.dma_start(
                    out=cx.b["vtok"].rearrange("h t d -> t h d")[
                        tc * 128:(tc + 1) * 128, oh * 8:(oh + 1) * 8, :],
                    in_=vst[gv % 2].rearrange("p (h d) -> p h d", h=8)
                ).then_inc(cx.sem["qg"], 16)

        @blk.tensor
        def _(t):
            cx.wait_snap(t, bar)
            t.wait_ge(cx.sem["qs"], q0 + 8 * 16)
            t.wait_ge(cx.sem["qg"], g0 + 16)
            t.wait_ge(cx.sem["qo"], o0 + 16)
            for g_i in range(NG):
                oc, tt = divmod(g_i, 8)
                if g_i >= 2:
                    t.wait_ge(cx.sem["ve"], v0 + (g_i - 2) + 1)
                for dc in range(8):
                    t.matmul(p2[g_i % 2][:],
                             wsb[:, dc * 3072 + oc * 128:dc * 3072 + (oc + 1) * 128],
                             ln2T[:, dc * S + tt * 512:dc * S + (tt + 1) * 512],
                             start=(dc == 0), stop=False)
                t.matmul(p2[g_i % 2][:], bqb[0:1, oc * 128:(oc + 1) * 128],
                         C["ones_rb"][:], start=False, stop=True).then_inc(cx.sem["mm"], 1)
            for gv in range(NV):
                tc, oh = divmod(gv, 2)
                if gv >= 2:
                    t.wait_ge(cx.sem["ve"], v0 + NG + (gv - 2) + 1)
                for dc in range(8):
                    t.matmul(pv[gv % 2][:],
                             ln2T[:, dc * S + tc * 128:dc * S + (tc + 1) * 128],
                             wsb[:, dc * 3072 + 2048 + oh * 512:dc * 3072 + 2048 + (oh + 1) * 512],
                             start=(dc == 0), stop=False)
                t.matmul(pv[gv % 2][:], C["ones_lb"][:],
                         bqb[0:1, 2048 + oh * 512:2048 + (oh + 1) * 512],
                         start=False, stop=True).then_inc(cx.sem["mm"], 1)

        @blk.vector
        def _(v):
            cx.wait_snap(v, bar)
            for g_i in range(NG):
                v.wait_ge(cx.sem["mm"], m0 + g_i + 1)
                if g_i >= 2:
                    v.wait_ge(cx.sem["qg"], g0 + 16 + 16 * (g_i - 2 + 1))
                v.tensor_copy(qk[g_i % 2][:], p2[g_i % 2][:]).then_inc(cx.sem["ve"], 1)
            for gv in range(NV):
                v.wait_ge(cx.sem["mm"], m0 + NG + gv + 1)
                if gv >= 2:
                    v.wait_ge(cx.sem["qg"], g0 + 16 + 16 * NG + 16 * (gv - 2 + 1))
                v.tensor_copy(vst[gv % 2][:], pv[gv % 2][:]).then_inc(cx.sem["ve"], 1)

        cx.n["qs"] = q0 + 8 * 16
        cx.n["qg"] = g0 + 16 + 16 * (NG + NV)
        cx.n["qo"] = o0 + 16
        cx.n["mm"] = m0 + NG + NV
        cx.n["ve"] = v0 + NG + NV


def s2b_norms(cx):
    nc = cx.nc
    C = cx.C
    bar = cx.snap()
    es = ExitStack()
    with es, nc.Block() as blk:
        qkc = [cx.sb(es, f"s2bq{i}", [128, S], BF16) for i in range(2)]
        sqc = [cx.sb(es, f"s2bs{i}", [128, S], BF16) for i in range(2)]
        n2q = cx.sb(es, "s2bnq", [2, 8 * S], BF16)
        n2k = cx.sb(es, "s2bnk", [2, 8 * S], BF16)
        km = cx.sb(es, "s2bkm", [2, 16], F32)
        pn = [cx.ps(es, f"s2bp{i}", [2, 512]) for i in range(2)]
        q0, g0, o0, m0, a0, v0 = (cx.n[k] for k in ("qs", "qg", "qo", "mm", "ac", "ve"))

        @blk.sync
        def _(s):
            cx.wait_snap(s, bar)
            for c in range(16):
                if c >= 2:
                    s.wait_ge(cx.sem["ac"], a0 + (c - 2) + 1)
                s.dma_start(out=qkc[c % 2][:], in_=cx.b["qkT"][c * 128:(c + 1) * 128, :]
                            ).then_inc(cx.sem["qs"], 16)

        @blk.scalar
        def _(a):
            cx.wait_snap(a, bar)
            for c in range(16):
                a.wait_ge(cx.sem["qs"], q0 + (c + 1) * 16)
                if c >= 2:
                    a.wait_ge(cx.sem["mm"], m0 + (c - 2) * 8 + 8)
                a.activation(sqc[c % 2][:], qkc[c % 2][:], AF.Square).then_inc(cx.sem["ac"], 1)
            a.wait_ge(cx.sem["ve"], v0 + 128 + 8)
            a.activation(km[:, 8:16], km[:, 0:8], AF.Sqrt).then_inc(cx.sem["ac"], 1)
            a.activation(n2q[:], n2q[:], AF.Sqrt).then_inc(cx.sem["ac"], 1)

        @blk.tensor
        def _(t):
            cx.wait_snap(t, bar)
            for c in range(16):
                t.wait_ge(cx.sem["ac"], a0 + c + 1)
                for tt in range(8):
                    i = c * 8 + tt
                    if i >= 2:
                        t.wait_ge(cx.sem["ve"], v0 + (i - 2) + 1)
                    t.matmul(pn[i % 2][:], C["ones2b"][:], sqc[c % 2][:, tt * 512:(tt + 1) * 512],
                             start=True, stop=True).then_inc(cx.sem["mm"], 1)

        @blk.vector
        def _(v):
            cx.wait_snap(v, bar)
            for c in range(16):
                for tt in range(8):
                    i = c * 8 + tt
                    v.wait_ge(cx.sem["mm"], m0 + i + 1)
                    tgt = n2q if c < 8 else n2k
                    c2 = c % 8
                    v.tensor_copy(tgt[:, c2 * S + tt * 512:c2 * S + (tt + 1) * 512],
                                  pn[i % 2][:]).then_inc(cx.sem["ve"], 1)
            for c2 in range(8):
                v.wait_ge(cx.sem["ve"], v0 + 128)
                v.reduce_max(km[:, c2:c2 + 1], n2k[:, c2 * S:(c2 + 1) * S], axis=AX
                             ).then_inc(cx.sem["ve"], 1)
            v.wait_ge(cx.sem["ac"], a0 + 18)
            for c2 in range(8):
                v.tensor_scalar(n2q[:, c2 * S:(c2 + 1) * S], n2q[:, c2 * S:(c2 + 1) * S],
                                km[0:2, 8 + c2:9 + c2], None, ALU.mult).then_inc(cx.sem["ve"], 1)

        @blk.gpsimd
        def _(g):
            cx.wait_snap(g, bar)
            g.wait_ge(cx.sem["ve"], v0 + 128 + 8 + 8)
            qoc = o0
            for h in range(NH):
                qoc = cx.sdma(g, cx.b["crow"][h:h + 1, :],
                              n2q[h % 2:h % 2 + 1, (h // 2) * S:(h // 2 + 1) * S], qoc)

        cx.n["qs"] = q0 + 16 * 16
        cx.n["qo"] = o0 + 16 * 16
        cx.n["mm"] = m0 + 128
        cx.n["ac"] = a0 + 18
        cx.n["ve"] = v0 + 128 + 16


def s3_attn(cx):
    nc = cx.nc
    C = cx.C
    bar = cx.snap()
    es = ExitStack()
    with es, nc.Block() as blk:
        Qs = [cx.sb(es, f"s3q{i}", [128, S], BF16) for i in range(2)]
        Ks = [cx.sb(es, f"s3k{i}", [128, S], BF16) for i in range(2)]
        Vs = [cx.sb(es, f"s3v{i}", [128, 32 * 65], BF16) for i in range(2)]
        probs = [cx.sb(es, f"s3p{i}", [128, 16 * 1024], BF16) for i in range(2)]
        ctx = cx.sb(es, "s3ctx", [128, 8 * S], BF16)
        rl = cx.sb(es, "s3rl", [1, 512], F32)
        rlb = [cx.sb(es, f"s3rlb{i}", [64, 512], F32) for i in range(2)]
        ps_s = [cx.ps(es, f"s3ps{i}", [128, 1024]) for i in range(2)]
        ps_c = [cx.ps(es, f"s3pc{i}", [65, 512]) for i in range(2)]
        ps_b = cx.ps(es, "s3pb", [64, 512])

        q0, g0, o0, m0, a0, v0 = (cx.n[k] for k in ("qs", "qg", "qo", "mm", "ac", "ve"))
        VPRE = 4
        NGQ = NH * 8

        def sc_idx(gq, g2):
            return m0 + g2 + 1 if gq == 0 else m0 + 16 + (gq - 1) * 18 + g2 + 1

        def ctx_idx(gq):
            if gq < NGQ - 1:
                return m0 + 16 + gq * 18 + 17
            return m0 + 16 + gq * 18 + 1

        def bc_idx(gq):
            return ctx_idx(gq) + 1

        ac_exp = lambda gq, g2: a0 + gq * 16 + g2 + 1
        ve_recip = lambda gq: v0 + VPRE + gq * 3 + 1
        ve_rlb = lambda gq: v0 + VPRE + gq * 3 + 2
        ve_cmul = lambda gq: v0 + VPRE + gq * 3 + 3

        def emit_scores(t, gq):
            h, qt = divmod(gq, 8)
            hb = h % 2
            t.wait_ge(cx.sem["qs"], q0 + (h + 1) * 16)
            t.wait_ge(cx.sem["qo"], o0 + (h + 1) * 48)
            for g2 in range(16):
                gg = gq * 16 + g2
                if gg >= 2:
                    pg = gg - 2
                    t.wait_ge(cx.sem["ac"], ac_exp(pg // 16, pg % 16))
                inst = None
                for j in range(2):
                    kk = g2 * 2 + j
                    inst = t.matmul(ps_s[g2 % 2][:, j * 512:(j + 1) * 512],
                                    Ks[hb][:, kk * 128:(kk + 1) * 128],
                                    Qs[hb][:, qt * 512:(qt + 1) * 512],
                                    start=True, stop=True)
                inst.then_inc(cx.sem["mm"], 1)

        @blk.sync
        def _(s):
            cx.wait_snap(s, bar)
            qoc = o0
            for h in range(NH):
                hb = h % 2
                if h >= 2:
                    s.wait_ge(cx.sem["mm"], bc_idx(8 * (h - 1) - 1))
                    s.wait_ge(cx.sem["ve"], v0 + VPRE + (8 * (h - 1)) * 3)
                qoc = cx.sdma(s, Qs[hb][0:64, :], cx.b["qkT"][h * 64:(h + 1) * 64, :], qoc)
                qoc = cx.sdma(s, Qs[hb][64:65, :], cx.b["crow"][h:h + 1, :], qoc)
                qoc = cx.sdma(s, Ks[hb][0:64, :], cx.b["qkT"][1024 + h * 64:1024 + (h + 1) * 64, :], qoc)
                s.dma_start(out=Vs[hb].rearrange("p (c w) -> p c w", w=65)[:, :, 0:64],
                            in_=cx.b["vtok"][h].rearrange("(c p) d -> p c d", p=128)
                            ).then_inc(cx.sem["qs"], 16)

        @blk.vector
        def _(v):
            cx.wait_snap(v, bar)
            v.memset(Ks[0][64:128, :], 0.0)
            v.memset(Ks[1][64:128, :], 0.0)
            v.memset(Qs[0][64:128, :], 0.0)
            v.memset(Qs[1][64:128, :], 0.0)
            v.memset(Ks[0][64:65, :], -1.0).then_inc(cx.sem["ve"], 1)
            v.memset(Ks[1][64:65, :], -1.0).then_inc(cx.sem["ve"], 1)
            v.memset(Vs[0].rearrange("p (c w) -> p c w", w=65)[:, :, 64:65], 1.0).then_inc(cx.sem["ve"], 1)
            v.memset(Vs[1].rearrange("p (c w) -> p c w", w=65)[:, :, 64:65], 1.0).then_inc(cx.sem["ve"], 1)
            for gq in range(NGQ):
                v.wait_ge(cx.sem["mm"], ctx_idx(gq))
                v.reciprocal(rl[:], ps_c[gq % 2][64:65, :]).then_inc(cx.sem["ve"], 1)
                v.wait_ge(cx.sem["mm"], bc_idx(gq))
                v.tensor_copy(rlb[gq % 2][:], ps_b[:]).then_inc(cx.sem["ve"], 1)
                v.wait_ge(cx.sem["ve"], ve_rlb(gq))
                h, qt = divmod(gq, 8)
                v.tensor_mul(
                    ctx[(h % 2) * 64:(h % 2) * 64 + 64,
                        (h // 2) * S + qt * 512:(h // 2) * S + (qt + 1) * 512],
                    ps_c[gq % 2][0:64, :], rlb[gq % 2][:]).then_inc(cx.sem["ve"], 1)

        @blk.scalar
        def _(a):
            cx.wait_snap(a, bar)
            for gq in range(NGQ):
                if gq >= 2:
                    a.wait_ge(cx.sem["mm"], ctx_idx(gq - 2))
                for g2 in range(16):
                    a.wait_ge(cx.sem["mm"], sc_idx(gq, g2))
                    a.activation(probs[gq % 2][:, g2 * 1024:(g2 + 1) * 1024],
                                 ps_s[g2 % 2][:], AF.Exp, scale=C["cv"][:, 2:3]
                                 ).then_inc(cx.sem["ac"], 1)

        @blk.tensor
        def _(t):
            cx.wait_snap(t, bar)
            t.wait_ge(cx.sem["ve"], v0 + VPRE)
            emit_scores(t, 0)
            for gq in range(NGQ):
                h, qt = divmod(gq, 8)
                hb = h % 2
                nh, nqt = divmod(gq + 1, 8)
                nhb = nh % 2
                if gq + 1 < NGQ:
                    t.wait_ge(cx.sem["qs"], q0 + (nh + 1) * 16)
                    t.wait_ge(cx.sem["qo"], o0 + (nh + 1) * 48)
                inst = None
                for g2 in range(16):
                    if gq + 1 < NGQ:
                        gg = (gq + 1) * 16 + g2
                        pg = gg - 2
                        t.wait_ge(cx.sem["ac"], ac_exp(pg // 16, pg % 16))
                        for j in range(2):
                            kk = g2 * 2 + j
                            inst2 = t.matmul(ps_s[g2 % 2][:, j * 512:(j + 1) * 512],
                                             Ks[nhb][:, kk * 128:(kk + 1) * 128],
                                             Qs[nhb][:, nqt * 512:(nqt + 1) * 512],
                                             start=True, stop=True)
                        inst2.then_inc(cx.sem["mm"], 1)
                    if g2 == 0:
                        t.wait_ge(cx.sem["ac"], ac_exp(gq, 0))
                        if gq >= 2:
                            t.wait_ge(cx.sem["ve"], ve_cmul(gq - 2))
                    else:
                        t.wait_ge(cx.sem["ac"], ac_exp(gq, g2))
                    for j in (g2 * 2, g2 * 2 + 1):
                        inst = t.matmul(ps_c[gq % 2][:], Vs[hb][:, j * 65:(j + 1) * 65],
                                        probs[gq % 2][:, j * 512:(j + 1) * 512],
                                        start=(j == 0), stop=(j == 31))
                inst.then_inc(cx.sem["mm"], 1)
                t.wait_ge(cx.sem["ve"], ve_recip(gq))
                if gq >= 1:
                    t.wait_ge(cx.sem["ve"], ve_rlb(gq - 1))
                t.matmul(ps_b[:], C["ones_l"][0:1, 0:64], rl[:],
                         start=True, stop=True).then_inc(cx.sem["mm"], 1)

        @blk.gpsimd
        def _(g):
            cx.wait_snap(g, bar)
            g.wait_ge(cx.sem["ve"], v0 + VPRE + NGQ * 3)
            for cc in range(8):
                g.dma_start(out=cx.b["ctxc"][cc * 128:(cc + 1) * 128, :],
                            in_=ctx[:, cc * S:(cc + 1) * S]).then_inc(cx.sem["qg"], 16)

        cx.n["qs"] = q0 + NH * 16
        cx.n["qo"] = o0 + NH * 48
        cx.n["qg"] = g0 + 8 * 16
        cx.n["mm"] = m0 + 16 + (NGQ - 1) * 18 + 2
        cx.n["ac"] = a0 + NGQ * 16
        cx.n["ve"] = v0 + VPRE + NGQ * 3


def s4_attnout(cx):
    nc = cx.nc
    C = cx.C
    bar = cx.snap()
    es = ExitStack()
    with es, nc.Block() as blk:
        ctxf = cx.sb(es, "s4cx", [128, 8 * S], BF16)
        wo = cx.sb(es, "s4w", [128, 8 * D], BF16)
        bo = cx.sb(es, "s4b", [1, D], BF16)
        xnt = [cx.sb(es, f"s4x{i}", [128, D], F32) for i in range(2)]
        s0s = [cx.sb(es, f"s4s{i}", [128, D], F32) for i in range(2)]
        po = [cx.ps(es, f"s4p{i}", [128, 1024]) for i in range(2)]

        q0, g0, m0, a0, v0 = (cx.n[k] for k in ("qs", "qg", "mm", "ac", "ve"))

        @blk.sync
        def _(s):
            cx.wait_snap(s, bar)
            for cc in range(8):
                s.dma_start(out=ctxf[:, cc * S:(cc + 1) * S],
                            in_=cx.b["ctxc"][cc * 128:(cc + 1) * 128, :]).then_inc(cx.sem["qs"], 16)
            for tc in range(32):
                if tc >= 2:
                    s.wait_ge(cx.sem["ve"], v0 + (tc - 2) + 1)
                s.dma_start(out=xnt[tc % 2][:],
                            in_=cx.b["xn0t_f32"][tc * 128:(tc + 1) * 128, :]).then_inc(cx.sem["qs"], 16)

        @blk.gpsimd
        def _(g):
            cx.wait_snap(g, bar)
            g.dma_start(out=wo.rearrange("p (c o) -> p c o", c=8),
                        in_=cx.i["woutT"].rearrange("(c p) o -> p c o", c=8)).then_inc(cx.sem["qg"], 16)
            g.dma_start(out=bo[:], in_=cx.i["boutr"]).then_inc(cx.sem["qg"], 16)
            for tc in range(32):
                g.wait_ge(cx.sem["ve"], v0 + tc + 1)
                g.dma_start(out=cx.b["s0t"][tc * 128:(tc + 1) * 128, :],
                            in_=s0s[tc % 2][:]).then_inc(cx.sem["qg"], 16)

        @blk.tensor
        def _(t):
            cx.wait_snap(t, bar)
            t.wait_ge(cx.sem["qs"], q0 + 8 * 16)
            t.wait_ge(cx.sem["qg"], g0 + 32)
            for tc in range(32):
                if tc >= 2:
                    t.wait_ge(cx.sem["ve"], v0 + (tc - 2) + 1)
                for dh in range(2):
                    for cc in range(8):
                        t.matmul(po[tc % 2][:, dh * 512:(dh + 1) * 512],
                                 ctxf[:, cc * S + tc * 128:cc * S + (tc + 1) * 128],
                                 wo[:, cc * D + dh * 512:cc * D + (dh + 1) * 512],
                                 start=(cc == 0), stop=False)
                    t.matmul(po[tc % 2][:, dh * 512:(dh + 1) * 512],
                             C["ones_lb"][:], bo[0:1, dh * 512:(dh + 1) * 512],
                             start=False, stop=True).then_inc(cx.sem["mm"], 1)

        @blk.vector
        def _(v):
            cx.wait_snap(v, bar)
            for tc in range(32):
                v.wait_ge(cx.sem["mm"], m0 + tc * 2 + 2)
                v.wait_ge(cx.sem["qs"], q0 + 8 * 16 + (tc + 1) * 16)
                if tc >= 2:
                    v.wait_ge(cx.sem["qg"], g0 + 32 + 16 * (tc - 2 + 1))
                v.tensor_add(s0s[tc % 2][:], po[tc % 2][:], xnt[tc % 2][:]).then_inc(cx.sem["ve"], 1)

        cx.n["qs"] = q0 + 40 * 16
        cx.n["qg"] = g0 + 34 * 16
        cx.n["mm"] = m0 + 64
        cx.n["ve"] = v0 + 32


def s5_route(cx):
    nc = cx.nc
    C = cx.C
    bar = cx.snap()
    es = ExitStack()
    NV = 23
    with es, nc.Block() as blk:
        xnt = [cx.sb(es, f"s5x{i}", [128, D], F32) for i in range(2)]
        xnT = cx.sb(es, "s5xT", [128, 8 * 128], F32)
        wr = cx.sb(es, "s5wr", [128, 8 * 16], F32)
        wg = cx.sb(es, "s5wg", [8, 16], F32)
        wst = cx.sb(es, "s5ws", [8, 20], F32)
        wgn = cx.sb(es, "s5wgn", [8, 16], F32)
        wgnT = cx.sb(es, "s5wgT", [16, 8], F32)
        redsb = cx.sb(es, "s5rd", [16, 128], F32)
        lg = cx.sb(es, "s5lg", [128, 8], F32)
        el = cx.sb(es, "s5el", [128, 8], F32)
        stt_ = cx.sb(es, "s5st", [128, 4], F32)
        m8 = cx.sb(es, "s5m8", [128, 8], F32)
        i8u = cx.sb(es, "s5i8", [128, 8], mybir.dt.uint32)
        msall = cx.sb(es, "s5ms", [128, 32 * 8], F32)
        ps_t = [cx.ps(es, f"s5pt{i}", [128, 128]) for i in range(2)]
        ps_r = cx.ps(es, "s5pr", [16, 128])
        ps_l = cx.ps(es, "s5pl", [128, 8])

        q0, g0, o0, m0, a0, v0 = (cx.n[k] for k in ("qs", "qg", "qo", "mm", "ac", "ve"))
        VP, AP_, MP = 5, 2, 1

        @blk.sync
        def _(s):
            cx.wait_snap(s, bar)
            qoc = cx.sdma(s, wg[:], cx.i["wg8"], o0)
            for tc in range(32):
                if tc >= 2:
                    s.wait_ge(cx.sem["mm"], m0 + MP + (tc - 2) * 10 + 8)
                s.dma_start(out=xnt[tc % 2][:], in_=cx.b["xn0t_f32"][tc * 128:(tc + 1) * 128, :]
                            ).then_inc(cx.sem["qs"], 16)

        @blk.gpsimd
        def _(g):
            cx.wait_snap(g, bar)
            g.dma_start(out=wr.rearrange("p (c g) -> p c g", c=8),
                        in_=cx.i["wred"].rearrange("(c p) g -> p c g", c=8)).then_inc(cx.sem["qg"], 16)
            for tc in range(32):
                g.wait_ge(cx.sem["ve"], v0 + VP + tc * NV + 8)
                g.dma_start(out=cx.b["xn0c_bf"].rearrange("(c p) t -> p c t", p=128)[
                                :, :, tc * 128:(tc + 1) * 128],
                            in_=xnT.rearrange("p (c t) -> p c t", c=8)).then_inc(cx.sem["qg"], 16)
            g.wait_ge(cx.sem["ve"], v0 + VP + 32 * NV)
            g.dma_start(out=cx.b["msb"].rearrange("(c p) e -> p c e", p=128),
                        in_=msall.rearrange("p (c e) -> p c e", e=8)).then_inc(cx.sem["qg"], 16)

        @blk.scalar
        def _(a):
            cx.wait_snap(a, bar)
            a.wait_ge(cx.sem["qo"], o0 + 16)
            a.activation(wst[:, 0:16], wg[:], AF.Square).then_inc(cx.sem["ac"], 1)
            a.wait_ge(cx.sem["ve"], v0 + 1)
            a.activation(wst[:, 17:18], wst[:, 16:17], AF.Sqrt).then_inc(cx.sem["ac"], 1)
            for tc in range(32):
                ab = a0 + AP_ + tc
                vb = v0 + VP + tc * NV
                a.wait_ge(cx.sem["ve"], vb + 11)
                a.activation(el[:], lg[:], AF.Exp).then_inc(cx.sem["ac"], 1)

        @blk.tensor
        def _(t):
            cx.wait_snap(t, bar)
            t.wait_ge(cx.sem["ve"], v0 + 4)
            t.transpose(ps_l[0:16, 0:8], wgn[:], C["eyef"][0:8, 0:8]).then_inc(cx.sem["mm"], 1)
            t.wait_ge(cx.sem["qg"], g0 + 16)
            for tc in range(32):
                mb = m0 + MP + tc * 10
                vb = v0 + VP + tc * NV
                t.wait_ge(cx.sem["qs"], q0 + (tc + 1) * 16)
                for dc in range(8):
                    i = tc * 8 + dc
                    if i >= 2:
                        pv = i - 2
                        t.wait_ge(cx.sem["ve"], v0 + VP + (pv // 8) * NV + (pv % 8) + 1)
                    t.transpose(ps_t[dc % 2][:], xnt[tc % 2][:, dc * 128:(dc + 1) * 128],
                                C["eyef"][:]).then_inc(cx.sem["mm"], 1)
                if tc >= 1:
                    t.wait_ge(cx.sem["ve"], v0 + VP + (tc - 1) * NV + 9)
                inst = None
                for dc in range(8):
                    t.wait_ge(cx.sem["ve"], vb + dc + 1)
                    inst = t.matmul(ps_r[:], wr[:, dc * 16:(dc + 1) * 16], xnT[:, dc * 128:(dc + 1) * 128],
                                    start=(dc == 0), stop=(dc == 7))
                inst.then_inc(cx.sem["mm"], 1)
                t.wait_ge(cx.sem["ve"], vb + 9)
                if tc >= 1:
                    t.wait_ge(cx.sem["ve"], v0 + VP + (tc - 1) * NV + 11)
                t.matmul(ps_l[:], redsb[:], wgnT[:], start=True, stop=True).then_inc(cx.sem["mm"], 1)

        @blk.vector
        def _(v):
            cx.wait_snap(v, bar)
            v.wait_ge(cx.sem["ac"], a0 + 1)
            v.reduce_sum(wst[:, 16:17], wst[:, 0:16], axis=AX).then_inc(cx.sem["ve"], 1)
            v.wait_ge(cx.sem["ac"], a0 + 2)
            v.tensor_scalar(wst[:, 18:19], wst[:, 17:18], 1e-4, None, ALU.max).then_inc(cx.sem["ve"], 1)
            v.wait_ge(cx.sem["ve"], v0 + 2)
            v.reciprocal(wst[:, 19:20], wst[:, 18:19]).then_inc(cx.sem["ve"], 1)
            v.wait_ge(cx.sem["ve"], v0 + 3)
            v.tensor_scalar(wgn[:], wg[:], wst[:, 19:20], None, ALU.mult).then_inc(cx.sem["ve"], 1)
            v.wait_ge(cx.sem["mm"], m0 + 1)
            v.tensor_copy(wgnT[:], ps_l[0:16, 0:8]).then_inc(cx.sem["ve"], 1)
            for tc in range(32):
                mb = m0 + MP + tc * 10
                vb = v0 + VP + tc * NV
                ab = a0 + AP_ + tc
                if tc >= 1:
                    v.wait_ge(cx.sem["qg"], g0 + 16 + tc * 16)
                for dc in range(8):
                    v.wait_ge(cx.sem["mm"], mb + dc + 1)
                    v.tensor_copy(xnT[:, dc * 128:(dc + 1) * 128], ps_t[dc % 2][:]
                                  ).then_inc(cx.sem["ve"], 1)
                v.wait_ge(cx.sem["mm"], mb + 9)
                v.tensor_copy(redsb[:], ps_r[:]).then_inc(cx.sem["ve"], 1)
                v.wait_ge(cx.sem["mm"], mb + 10)
                v.reduce_max(stt_[:, 0:1], ps_l[:], axis=AX).then_inc(cx.sem["ve"], 1)
                v.wait_ge(cx.sem["ve"], vb + 10)
                v.tensor_scalar(lg[:], ps_l[:], stt_[:, 0:1], None, ALU.subtract
                                ).then_inc(cx.sem["ve"], 1)
                v.wait_ge(cx.sem["ac"], ab + 1)
                v.reduce_sum(stt_[:, 1:2], el[:], axis=AX).then_inc(cx.sem["ve"], 1)
                v.wait_ge(cx.sem["ve"], vb + 12)
                v.reciprocal(stt_[:, 2:3], stt_[:, 1:2]).then_inc(cx.sem["ve"], 1)
                v.wait_ge(cx.sem["ve"], vb + 11)
                v.max(m8[:], lg[:]).then_inc(cx.sem["ve"], 1)
                v.wait_ge(cx.sem["ve"], vb + 14)
                v.max_index(i8u[:], m8[:], lg[:]).then_inc(cx.sem["ve"], 1)
                v.wait_ge(cx.sem["ve"], vb + 15)
                for e in range(8):
                    v.tensor_scalar(msall[:, tc * 8 + e:tc * 8 + e + 1], i8u[:, 0:1],
                                    float(e), stt_[:, 2:3], ALU.is_equal, ALU.mult
                                    ).then_inc(cx.sem["ve"], 1)

        cx.n["qs"] = q0 + 32 * 16
        cx.n["qg"] = g0 + 16 * 34
        cx.n["qo"] = o0 + 16
        cx.n["mm"] = m0 + MP + 32 * 10
        cx.n["ac"] = a0 + AP_ + 32
        cx.n["ve"] = v0 + VP + 32 * NV


def s6a_moe_h(cx, NEX=NE):
    nc = cx.nc
    C = cx.C
    bar = cx.snap()
    es = ExitStack()
    with es, nc.Block() as blk:
        w1sb = [cx.sb(es, f"s6w{i}", [128, 8 * FF], BF16) for i in range(2)]
        b1sb = [cx.sb(es, f"s6b{i}", [128, 32], F32) for i in range(2)]
        xnT = [cx.sb(es, f"s6x{i}", [128, 8 * 512], BF16) for i in range(2)]
        hst = cx.sb(es, "s6h", [128, 32 * 512], BF16)
        ps = [cx.ps(es, f"s6p{i}", [128, 512]) for i in range(4)]
        q0, g0, o0, m0, a0, v0 = (cx.n[k] for k in ("qs", "qg", "qo", "mm", "ac", "ve"))

        def mme(e):
            return m0 + e * 256

        def ace(e):
            return a0 + e * 256

        @blk.gpsimd
        def _(g):
            cx.wait_snap(g, bar)
            for e in range(NEX):
                if e >= 2:
                    g.wait_ge(cx.sem["mm"], mme(e - 1))
                for dc in range(8):
                    g.dma_start(out=w1sb[e % 2][:, dc * FF:(dc + 1) * FF],
                                in_=cx.i["w1e"][e][dc * 128:(dc + 1) * 128, :]
                                ).then_inc(cx.sem["qg"], 16)
                for tt in range(8):
                    for p4 in range(4):
                        g.wait_ge(cx.sem["ac"], ace(e) + tt * 32 + p4 * 8 + 8)
                        g.dma_start(
                            out=cx.b["hbuf"][e].rearrange("(fc p) t -> p fc t", p=128)[
                                :, p4 * 8:(p4 + 1) * 8, tt * 512:(tt + 1) * 512],
                            in_=hst.rearrange("p (fc t) -> p fc t", t=512)[
                                :, p4 * 8:(p4 + 1) * 8, :]
                        ).then_inc(cx.sem["qg"], 16)

        @blk.sync
        def _(s):
            cx.wait_snap(s, bar)
            qoc = o0
            for e in range(NEX):
                if e >= 2:
                    s.wait_ge(cx.sem["ac"], a0 + (e - 1) * 256)
                qoc = cx.sdma(s, b1sb[e % 2][:], cx.i["b1c"][e], qoc)
                for tt in range(8):
                    ge = e * 8 + tt
                    if ge >= 2:
                        s.wait_ge(cx.sem["mm"], mme((ge - 2) // 8) + ((ge - 2) % 8) * 32 + 32)
                    s.dma_start(out=xnT[tt % 2].rearrange("p (c t) -> p c t", c=8),
                                in_=cx.b["xn0c_bf"].rearrange("(c p) t -> p c t", p=128)[
                                    :, :, tt * 512:(tt + 1) * 512]).then_inc(cx.sem["qs"], 16)

        @blk.tensor
        def _(t):
            cx.wait_snap(t, bar)
            for e in range(NEX):
                t.wait_ge(cx.sem["qg"], g0 + e * 640 + 128)
                t.wait_ge(cx.sem["qo"], o0 + (e + 1) * 16)
                for tt in range(8):
                    ge = e * 8 + tt
                    t.wait_ge(cx.sem["qs"], q0 + (ge + 1) * 16)
                    for fc in range(32):
                        gi = ge * 32 + fc
                        if gi >= 4:
                            pv = gi - 4
                            t.wait_ge(cx.sem["ac"], a0 + pv + 1)
                        inst = None
                        for dc in range(8):
                            inst = t.matmul(ps[fc % 4][:],
                                            w1sb[e % 2][:, dc * FF + fc * 128:dc * FF + (fc + 1) * 128],
                                            xnT[tt % 2][:, dc * 512:(dc + 1) * 512],
                                            start=(dc == 0), stop=(dc == 7))
                        inst.then_inc(cx.sem["mm"], 1)

        @blk.scalar
        def _(a):
            cx.wait_snap(a, bar)
            for e in range(NEX):
                for tt in range(8):
                    ge = e * 8 + tt
                    for fc in range(32):
                        gi = ge * 32 + fc
                        a.wait_ge(cx.sem["mm"], m0 + gi + 1)
                        if ge >= 1:
                            pe, ptt = divmod(ge - 1, 8)
                            p4 = fc // 8
                            a.wait_ge(cx.sem["qg"],
                                      g0 + 16 * (pe * 40 + 8 + ptt * 4 + p4 + 1))
                        a.activation(hst[:, fc * 512:(fc + 1) * 512], ps[fc % 4][:], AF.Gelu,
                                     bias=b1sb[e % 2][:, fc:fc + 1]).then_inc(cx.sem["ac"], 1)

        cx.n["qs"] = q0 + NEX * 8 * 16
        cx.n["qg"] = g0 + NEX * 640
        cx.n["qo"] = o0 + NEX * 16
        cx.n["mm"] = m0 + NEX * 256
        cx.n["ac"] = a0 + NEX * 256


def s6b_moe_y(cx):
    nc = cx.nc
    C = cx.C
    bar = cx.snap()
    es = ExitStack()
    with es, nc.Block() as blk:
        w2sb = cx.sb(es, "s6bw", [128, 32 * D], BF16)
        b2b = cx.sb(es, "s6bb", [1, D], BF16)
        hsb = [cx.sb(es, f"s6bh{i}", [128, 32 * 128], BF16) for i in range(2)]
        msal = cx.sb(es, "s6bm", [128, 32 * 8], F32)
        sel = cx.sb(es, "s6bs", [128, 32 * D], BF16)
        ps = [cx.ps(es, f"s6bp{i}", [128, 1024]) for i in range(2)]
        q0, g0, o0, m0, a0, v0 = (cx.n[k] for k in ("qs", "qg", "qo", "mm", "ac", "ve"))

        @blk.sync
        def _(s):
            cx.wait_snap(s, bar)
            s.dma_start(out=msal.rearrange("p (c e) -> p c e", e=8),
                        in_=cx.b["msb"].rearrange("(c p) e -> p c e", p=128)
                        ).then_inc(cx.sem["qs"], 16)
            for e in range(NE):
                for tc in range(32):
                    ge = e * 32 + tc
                    if ge >= 2:
                        s.wait_ge(cx.sem["mm"], m0 + (ge - 2) * 2 + 2)
                    s.dma_start(out=hsb[ge % 2].rearrange("p (c t) -> p c t", c=32),
                                in_=cx.b["hbuf"][e].rearrange("(c p) t -> p c t", p=128)[
                                    :, :, tc * 128:(tc + 1) * 128]
                                ).then_inc(cx.sem["qs"], 16)

        @blk.gpsimd
        def _(g):
            cx.wait_snap(g, bar)
            qoc = o0
            for e in range(NE):
                if e >= 1:
                    g.wait_ge(cx.sem["mm"], m0 + e * 64)
                g.dma_start(out=w2sb.rearrange("p (c o) -> p c o", c=32),
                            in_=cx.i["w2e"][e].rearrange("(c p) o -> p c o", c=32)
                            ).then_inc(cx.sem["qg"], 16)
                qoc = cx.sdma(g, b2b[:], cx.i["b2r"][e:e + 1, :], qoc)
            g.wait_ge(cx.sem["ve"], v0 + NE * 32)
            g.dma_start(out=cx.b["selt"].rearrange("(c p) o -> p c o", p=128),
                        in_=sel.rearrange("p (c o) -> p c o", c=32)).then_inc(cx.sem["qg"], 16)

        @blk.tensor
        def _(t):
            cx.wait_snap(t, bar)
            for e in range(NE):
                t.wait_ge(cx.sem["qg"], g0 + (e + 1) * 16)
                t.wait_ge(cx.sem["qo"], o0 + (e + 1) * 16)
                for tc in range(32):
                    ge = e * 32 + tc
                    t.wait_ge(cx.sem["qs"], q0 + 16 + (ge + 1) * 16)
                    if ge >= 2:
                        t.wait_ge(cx.sem["ve"], v0 + (ge - 2) + 1)
                    for oh in range(2):
                        inst = None
                        for fc in range(32):
                            inst = t.matmul(ps[ge % 2][:, oh * 512:(oh + 1) * 512],
                                            hsb[ge % 2][:, fc * 128:(fc + 1) * 128],
                                            w2sb[:, fc * D + oh * 512:fc * D + (oh + 1) * 512],
                                            start=(fc == 0), stop=False)
                        inst = t.matmul(ps[ge % 2][:, oh * 512:(oh + 1) * 512],
                                        C["ones_lb"][:], b2b[0:1, oh * 512:(oh + 1) * 512],
                                        start=False, stop=True)
                        inst.then_inc(cx.sem["mm"], 1)

        @blk.vector
        def _(v):
            cx.wait_snap(v, bar)
            v.wait_ge(cx.sem["qs"], q0 + 16)
            for e in range(NE):
                for tc in range(32):
                    ge = e * 32 + tc
                    v.wait_ge(cx.sem["mm"], m0 + ge * 2 + 2)
                    mscol = msal[:, tc * 8 + e:tc * 8 + e + 1]
                    if e == 0:
                        v.tensor_scalar(sel[:, tc * D:(tc + 1) * D], ps[ge % 2][:],
                                        mscol, None, ALU.mult).then_inc(cx.sem["ve"], 1)
                    else:
                        v.scalar_tensor_tensor(sel[:, tc * D:(tc + 1) * D], ps[ge % 2][:],
                                               mscol, sel[:, tc * D:(tc + 1) * D],
                                               ALU.mult, ALU.add).then_inc(cx.sem["ve"], 1)

        cx.n["qs"] = q0 + 16 + NE * 32 * 16
        cx.n["qg"] = g0 + NE * 16 + 16
        cx.n["qo"] = o0 + NE * 16
        cx.n["mm"] = m0 + NE * 64
        cx.n["ve"] = v0 + NE * 32


def s7_combine0(cx):
    nc = cx.nc
    C = cx.C
    bar = cx.marks.get("pre_s6_bar") or cx.snap()
    selt_done = cx.marks.get("selt_done") or (lambda tc: 0)
    es = ExitStack()
    NV, NA = 8, 4
    with es, nc.Block() as blk:
        R = "right"
        xt = [cx.sb(es, f"s7x{i}", [128, D], F32, side=R) for i in range(2)]
        s0 = [cx.sb(es, f"s7s{i}", [128, D], F32, side=R) for i in range(2)]
        sl = [cx.sb(es, f"s7l{i}", [128, D], F32, side=R) for i in range(2)]
        g0f = cx.sb(es, "s7g", [128, D], F32, side=R)
        x1 = [cx.sb(es, f"s7o{i}", [128, D], F32, side=R) for i in range(2)]
        xn1 = [cx.sb(es, f"s7n{i}", [128, D], BF16, side=R) for i in range(2)]
        sq = cx.sb(es, "s7q", [128, D], F32, side=R)
        st = cx.sb(es, "s7t", [128, 8], F32, side=R)
        q0, g0_, o0, m0, a0, v0 = (cx.n[k] for k in ("qs", "qg", "qo", "mm", "ac", "ve"))

        @blk.sync
        def _(s):
            cx.wait_snap(s, bar)
            for tc in range(32):
                if tc >= 2:
                    s.wait_ge(cx.sem["ve"], v0 + (tc - 2) * NV + NV)
                    s.wait_ge(cx.sem["ac"], a0 + (tc - 2) * NA + 2)
                r = slice(tc * 128, (tc + 1) * 128)
                sd = selt_done(tc)
                if sd:
                    s.wait_ge(cx.sem["qg"], sd)
                s.dma_start(out=xt[tc % 2][:], in_=cx.i["x_tok"][r, :]).then_inc(cx.sem["qs"], 16)
                s.dma_start(out=s0[tc % 2][:], in_=cx.b["s0t"][r, :]).then_inc(cx.sem["qs"], 16)
                s.dma_start(out=sl[tc % 2][:], in_=cx.b["selt"][r, :]).then_inc(cx.sem["qs"], 16)

        @blk.scalar
        def _(a):
            cx.wait_snap(a, bar)
            for tc in range(32):
                b = tc % 2
                vb = v0 + tc * NV
                ab = a0 + tc * NA
                a.wait_ge(cx.sem["qs"], q0 + (tc + 1) * 48)
                if tc >= 1:
                    a.wait_ge(cx.sem["ve"], v0 + (tc - 1) * NV + 1)
                a.activation(g0f[:], sl[b][:], AF.Silu).then_inc(cx.sem["ac"], 1)
                a.wait_ge(cx.sem["ve"], vb + 2)
                a.activation(sq[:], x1[b][:], AF.Square).then_inc(cx.sem["ac"], 1)
                a.wait_ge(cx.sem["ve"], vb + 7)
                a.activation(st[:, 4:5], st[:, 3:4], AF.Ln, bias=C["cv"][:, 0:1]).then_inc(cx.sem["ac"], 1)
                a.wait_ge(cx.sem["ac"], ab + 3)
                a.activation(st[:, 5:6], st[:, 4:5], AF.Exp, scale=C["cv"][:, 1:2]).then_inc(cx.sem["ac"], 1)

        @blk.vector
        def _(v):
            cx.wait_snap(v, bar)
            for tc in range(32):
                b = tc % 2
                vb = v0 + tc * NV
                ab = a0 + tc * NA
                v.wait_ge(cx.sem["ac"], ab + 1)
                v.tensor_mul(g0f[:], s0[b][:], g0f[:]).then_inc(cx.sem["ve"], 1)
                v.wait_ge(cx.sem["ve"], vb + 1)
                if tc >= 2:
                    v.wait_ge(cx.sem["qg"], g0_ + ((tc - 2) * 2 + 1) * 16)
                v.tensor_add(x1[b][:], xt[b][:], g0f[:]).then_inc(cx.sem["ve"], 1)
                v.wait_ge(cx.sem["ve"], vb + 2)
                v.reduce_sum(st[:, 0:1], x1[b][:], axis=AX).then_inc(cx.sem["ve"], 1)
                v.wait_ge(cx.sem["ac"], ab + 2)
                v.reduce_sum(st[:, 1:2], sq[:], axis=AX).then_inc(cx.sem["ve"], 1)
                v.wait_ge(cx.sem["ve"], vb + 4)
                v.tensor_scalar(st[:, 2:4], st[:, 0:2], 1.0 / D, None, ALU.mult).then_inc(cx.sem["ve"], 1)
                v.wait_ge(cx.sem["ve"], vb + 5)
                v.tensor_mul(st[:, 6:7], st[:, 2:3], st[:, 2:3]).then_inc(cx.sem["ve"], 1)
                v.wait_ge(cx.sem["ve"], vb + 6)
                v.tensor_sub(st[:, 3:4], st[:, 3:4], st[:, 6:7]).then_inc(cx.sem["ve"], 1)
                v.wait_ge(cx.sem["ac"], ab + 4)
                if tc >= 2:
                    v.wait_ge(cx.sem["qg"], g0_ + ((tc - 2) * 2 + 2) * 16)
                v.tensor_scalar(xn1[b][:], x1[b][:], st[:, 2:3], st[:, 5:6],
                                ALU.subtract, ALU.mult).then_inc(cx.sem["ve"], 1)

        @blk.gpsimd
        def _(g):
            cx.wait_snap(g, bar)
            for tc in range(32):
                b = tc % 2
                vb = v0 + tc * NV
                r = slice(tc * 128, (tc + 1) * 128)
                g.wait_ge(cx.sem["ve"], vb + 2)
                g.dma_start(out=cx.b["x1t"][r, :], in_=x1[b][:]).then_inc(cx.sem["qg"], 16)
                g.wait_ge(cx.sem["ve"], vb + NV)
                g.dma_start(out=cx.b["xn1t_bf"][r, :], in_=xn1[b][:]).then_inc(cx.sem["qg"], 16)

        cx.n["qs"] = q0 + 32 * 48
        cx.n["qg"] = g0_ + 32 * 32
        cx.n["ac"] = a0 + 32 * NA
        cx.n["ve"] = v0 + 32 * NV


def s8_inproj(cx):
    nc = cx.nc
    C = cx.C
    bar = cx.snap()
    es = ExitStack()
    with es, nc.Block() as blk:
        xnT = cx.sb(es, "s8x", [128, 8 * S], BF16)
        wsb = cx.sb(es, "s8w", [128, 8 * DPROJ], BF16)
        stg = [cx.sb(es, f"s8s{i}", [128, 512], F32) for i in range(2)]
        dstg = [cx.sb(es, f"s8d{i}", [16, 512], F32) for i in range(2)]
        ps = [cx.ps(es, f"s8p{i}", [128, 512]) for i in range(2)]
        q0, g0, o0, m0, a0, v0 = (cx.n[k] for k in ("qs", "qg", "qo", "mm", "ac", "ve"))
        OCS = list(range(16, 33)) + [33] + list(range(16))
        NT = 34 * 8

        def qg_spill_count(i):
            n = 0
            for k in range(i + 1):
                if OCS[k // 8] != 33:
                    n += 1
            return g0 + (8 + n) * 16

        @blk.sync
        def _(s):
            cx.wait_snap(s, bar)
            qoc = o0
            for dc in range(8):
                qoc = cx.sdma(s, xnT[:, dc * S:(dc + 1) * S],
                              cx.b["xn1t_bf"][:, dc * 128:(dc + 1) * 128], qoc, transpose=True)

        @blk.gpsimd
        def _(g):
            cx.wait_snap(g, bar)
            for dc in range(8):
                g.dma_start(out=wsb[:, dc * DPROJ:(dc + 1) * DPROJ],
                            in_=cx.i["winT"][dc * 128:(dc + 1) * 128, :]).then_inc(cx.sem["qg"], 16)
            qoc = o0 + 8 * 16
            for i in range(NT):
                oc, tt = divmod(i, 8)
                oc = OCS[i // 8]
                g.wait_ge(cx.sem["ve"], v0 + i + 1)
                if oc == 33:
                    g.dma_start(out=cx.b["dtraw"][:, tt * 512:(tt + 1) * 512],
                                in_=dstg[i % 2][:]).then_inc(cx.sem["qo"], 16)
                    qoc += 16
                    g.wait_ge(cx.sem["qo"], qoc)
                elif oc >= 16:
                    g.dma_start(out=cx.b["xbc"][(oc - 16) * 128:(oc - 15) * 128, tt * 512:(tt + 1) * 512],
                                in_=stg[i % 2][:]).then_inc(cx.sem["qg"], 16)
                else:
                    g.dma_start(out=cx.b["zc"][oc * 128:(oc + 1) * 128, tt * 512:(tt + 1) * 512],
                                in_=stg[i % 2][:]).then_inc(cx.sem["qg"], 16)

        @blk.tensor
        def _(t):
            cx.wait_snap(t, bar)
            t.wait_ge(cx.sem["qo"], o0 + 8 * 16)
            t.wait_ge(cx.sem["qg"], g0 + 8 * 16)
            for i in range(NT):
                tt = i % 8
                oc = OCS[i // 8]
                if i >= 2:
                    t.wait_ge(cx.sem["ve"], v0 + (i - 2) + 1)
                inst = None
                wid = 128 if oc != 33 else 16
                for dc in range(8):
                    inst = t.matmul(ps[i % 2][0:wid, :],
                                    wsb[:, dc * DPROJ + oc * 128:dc * DPROJ + oc * 128 + wid],
                                    xnT[:, dc * S + tt * 512:dc * S + (tt + 1) * 512],
                                    start=(dc == 0), stop=(dc == 7))
                inst.then_inc(cx.sem["mm"], 1)

        @blk.vector
        def _(v):
            cx.wait_snap(v, bar)
            nq = 0
            for i in range(NT):
                tt = i % 8
                oc = OCS[i // 8]
                v.wait_ge(cx.sem["mm"], m0 + i + 1)
                if i >= 2:
                    if OCS[(i - 2) // 8] != 33:
                        v.wait_ge(cx.sem["qg"], qg_spill_count(i - 2))
                    else:
                        v.wait_ge(cx.sem["qo"], o0 + 8 * 16 + (((i - 2) % 8) + 1) * 16)
                if oc != 33:
                    v.tensor_copy(stg[i % 2][:], ps[i % 2][:]).then_inc(cx.sem["ve"], 1)
                else:
                    v.tensor_copy(dstg[i % 2][:], ps[i % 2][0:16, :]).then_inc(cx.sem["ve"], 1)

        cx.marks["pre_s8_bar"] = bar
        cx.marks["s8_xbc_done"] = lambda cc: qg_spill_count(cc * 8 + 7)
        cx.marks["s8_dtraw_done"] = o0 + 16 * 16
        cx.n["qo"] = o0 + 16 * 16
        cx.n["qg"] = g0 + (8 + 264) * 16
        cx.n["mm"] = m0 + NT
        cx.n["ve"] = v0 + NT

def s9_conv(cx):
    nc = cx.nc
    C = cx.C
    bar = cx.marks.get("pre_s8_bar") or cx.snap()
    xbc_done = cx.marks.get("s8_xbc_done") or (lambda cc: 0)
    dtraw_done = cx.marks.get("s8_dtraw_done", 0)
    es = ExitStack()
    R = "right"
    with es, nc.Block() as blk:
        dtr = cx.sb(es, "s9dt", [16, S], F32, side=R)
        dwork = cx.sb(es, "s9w", [16, 3 * S], F32, side=R)
        dtb = cx.sb(es, "s9b", [16, 1], F32, side=R)
        alog = cx.sb(es, "s9a", [16, 2], F32, side=R)
        xp = [cx.sb(es, f"s9x{i}", [128, S + 3], F32, side=R) for i in range(2)]
        ca = cx.sb(es, "s9ca", [128, S], F32, side=R)
        cb = cx.sb(es, "s9cb", [128, S], F32, side=R)
        sx = cx.sb(es, "s9sx", [128, S], F32, side=R)
        sxb = cx.sb(es, "s9sb", [128, S], BF16, side=R)
        cw = cx.sb(es, "s9cw", [128, 17 * 4], F32, side=R)
        cbb = cx.sb(es, "s9cbb", [128, 17], F32, side=R)
        q0, g0, o0, m0, a0, v0 = (cx.n[k] for k in ("qs", "qg", "qo", "mm", "ac", "ve"))
        VDT0 = 17 * 6
        ADT0 = 17

        @blk.sync
        def _(s):
            cx.wait_snap(s, bar)
            for cc in range(17):
                if cc >= 2:
                    s.wait_ge(cx.sem["ac"], a0 + (cc - 2) + 1)
                xd = xbc_done(cc)
                if xd:
                    s.wait_ge(cx.sem["qg"], xd)
                s.dma_start(out=xp[cc % 2][:, 3:S + 3], in_=cx.b["xbc"][cc * 128:(cc + 1) * 128, :]
                            ).then_inc(cx.sem["qs"], 16)
            qoc = o0
            if dtraw_done:
                s.wait_ge(cx.sem["qo"], dtraw_done)
            qoc = cx.sdma(s, dtr[:], cx.b["dtraw"], qoc)
            qoc = cx.sdma(s, dtb[:], cx.i["dtb"], qoc)
            qoc = cx.sdma(s, alog[:, 0:1], cx.i["alog"], qoc)

        @blk.gpsimd
        def _(g):
            cx.wait_snap(g, bar)
            g.dma_start(out=cw.rearrange("p (c w) -> p c w", c=17), in_=cx.i["convw"]).then_inc(cx.sem["qg"], 16)
            g.dma_start(out=cbb[:], in_=cx.i["convb"]).then_inc(cx.sem["qg"], 16)
            for cc in range(17):
                vb = v0 + cc * 6
                ab = a0 + cc
                g.wait_ge(cx.sem["ve"], vb + 6)
                if cc < 16:
                    g.dma_start(out=cx.b["xs_bf"][cc * 128:(cc + 1) * 128, :], in_=sxb[:]
                                ).then_inc(cx.sem["qg"], 16)
                    g.wait_ge(cx.sem["ac"], ab + 1)
                    g.dma_start(out=cx.b["xs_f32"][cc * 128:(cc + 1) * 128, :], in_=sx[:]
                                ).then_inc(cx.sem["qg"], 16)
                else:
                    g.dma_start(out=cx.b["bcbf"][:, :], in_=sxb[:]).then_inc(cx.sem["qg"], 16)
            g.wait_ge(cx.sem["ve"], v0 + VDT0 + 3)
            g.dma_start(out=cx.b["dtv"], in_=dwork[:, S:2 * S]).then_inc(cx.sem["qg"], 16)
            g.wait_ge(cx.sem["ve"], v0 + VDT0 + 20)
            g.dma_start(out=cx.b["acumd"], in_=dwork[:, 0:S]).then_inc(cx.sem["qg"], 16)
            g.wait_ge(cx.sem["qg"], g0 + (2 + 33 + 2) * 16)

        @blk.scalar
        def _(a):
            cx.wait_snap(a, bar)
            for cc in range(17):
                vb = v0 + cc * 6
                ab = a0 + cc
                a.wait_ge(cx.sem["ve"], vb + 5)
                if cc >= 1:
                    a.wait_ge(cx.sem["ve"], v0 + (cc - 1) * 6 + 6)
                    a.wait_ge(cx.sem["qg"], g0 + 16 * (2 + 2 * (cc - 1) + 2))
                a.activation(sx[:], cb[:], AF.Silu, bias=cbb[:, cc:cc + 1]).then_inc(cx.sem["ac"], 1)
            a.wait_ge(cx.sem["ve"], v0 + VDT0 + 1)
            a.activation(dwork[:, S:2 * S], dwork[:, 0:S], AF.Abs).then_inc(cx.sem["ac"], 1)
            a.wait_ge(cx.sem["ac"], a0 + ADT0 + 1)
            a.activation(dwork[:, S:2 * S], dwork[:, S:2 * S], AF.Exp, scale=C["cv"][0:16, 3:4]
                         ).then_inc(cx.sem["ac"], 1)
            a.wait_ge(cx.sem["ac"], a0 + ADT0 + 2)
            a.activation(dwork[:, S:2 * S], dwork[:, S:2 * S], AF.Ln, bias=C["cv"][0:16, 5:6]
                         ).then_inc(cx.sem["ac"], 1)
            a.activation(alog[:, 1:2], alog[:, 0:1], AF.Exp).then_inc(cx.sem["ac"], 1)

        @blk.vector
        def _(v):
            cx.wait_snap(v, bar)
            for cc in range(17):
                b = cc % 2
                vb = v0 + cc * 6
                ab = a0 + cc
                v.wait_ge(cx.sem["qs"], q0 + (cc + 1) * 16)
                v.wait_ge(cx.sem["qg"], g0 + 32)
                v.memset(xp[b][:, 0:3], 0.0).then_inc(cx.sem["ve"], 1)
                v.wait_ge(cx.sem["ve"], vb + 1)
                v.tensor_scalar(ca[:], xp[b][:, 0:S], cw[:, cc * 4 + 0:cc * 4 + 1], None,
                                ALU.mult).then_inc(cx.sem["ve"], 1)
                v.wait_ge(cx.sem["ve"], vb + 2)
                if cc >= 1:
                    v.wait_ge(cx.sem["ac"], a0 + cc)
                v.scalar_tensor_tensor(cb[:], xp[b][:, 1:S + 1], cw[:, cc * 4 + 1:cc * 4 + 2],
                                       ca[:], ALU.mult, ALU.add).then_inc(cx.sem["ve"], 1)
                v.wait_ge(cx.sem["ve"], vb + 3)
                v.scalar_tensor_tensor(ca[:], xp[b][:, 2:S + 2], cw[:, cc * 4 + 2:cc * 4 + 3],
                                       cb[:], ALU.mult, ALU.add).then_inc(cx.sem["ve"], 1)
                v.wait_ge(cx.sem["ve"], vb + 4)
                v.scalar_tensor_tensor(cb[:], xp[b][:, 3:S + 3], cw[:, cc * 4 + 3:cc * 4 + 4],
                                       ca[:], ALU.mult, ALU.add).then_inc(cx.sem["ve"], 1)
                v.wait_ge(cx.sem["ac"], ab + 1)
                if cc >= 2:
                    v.wait_ge(cx.sem["qg"], g0 + 32 + 16 * (2 * (cc - 2) + 1))
                v.tensor_copy(sxb[:], sx[:]).then_inc(cx.sem["ve"], 1)
            v.wait_ge(cx.sem["qo"], o0 + 48)
            v.tensor_scalar(dwork[:, 0:S], dtr[:], dtb[:, 0:1], None, ALU.add).then_inc(cx.sem["ve"], 1)
            v.wait_ge(cx.sem["ve"], v0 + VDT0 + 1)
            v.tensor_scalar(dwork[:, 2 * S:3 * S], dwork[:, 0:S], 0.0, None, ALU.max).then_inc(cx.sem["ve"], 1)
            v.wait_ge(cx.sem["ac"], a0 + ADT0 + 3)
            v.wait_ge(cx.sem["ve"], v0 + VDT0 + 2)
            v.tensor_add(dwork[:, S:2 * S], dwork[:, 2 * S:3 * S], dwork[:, S:2 * S]).then_inc(cx.sem["ve"], 1)
            v.wait_ge(cx.sem["ac"], a0 + ADT0 + 4)
            v.wait_ge(cx.sem["ve"], v0 + VDT0 + 3)
            v.tensor_scalar(dwork[:, 0:S], dwork[:, S:2 * S], alog[:, 1:2], -1.0,
                            ALU.mult, ALU.mult).then_inc(cx.sem["ve"], 1)
            src, dst = 0, 2
            for k in range(8):
                sh = 1 << k
                v.wait_ge(cx.sem["ve"], v0 + VDT0 + 4 + k * 2)
                va = dwork[:, src * S:(src + 1) * S].rearrange("p (c q) -> p c q", q=CH)
                vd = dwork[:, dst * S:(dst + 1) * S].rearrange("p (c q) -> p c q", q=CH)
                v.tensor_add(vd[:, :, sh:], va[:, :, sh:], va[:, :, 0:CH - sh]).then_inc(cx.sem["ve"], 1)
                v.tensor_copy(vd[:, :, 0:sh], va[:, :, 0:sh]).then_inc(cx.sem["ve"], 1)
                src, dst = dst, src

        cx.n["qs"] = q0 + 17 * 16
        cx.n["qg"] = g0 + (2 + 33 + 2) * 16
        cx.n["qo"] = o0 + 48
        cx.n["ac"] = a0 + ADT0 + 4
        cx.n["ve"] = v0 + VDT0 + 20

def s10_ssd(cx):
    nc = cx.nc
    C = cx.C
    bar = cx.snap()
    es = ExitStack()
    with es, nc.Block() as blk:
        dts = cx.sb(es, "sads", [16, S], F32)
        acs = cx.sb(es, "saas", [16, S], F32)
        bcbB = cx.sb(es, "sabB", [64, S], BF16)
        bcbC = cx.sb(es, "sabC", [64, S], BF16)
        xbf = [cx.sb(es, f"saxb{i}", [128, 16 * CH], BF16) for i in range(2)]
        xf = [cx.sb(es, f"saxf{i}", [128, 16 * CH], F32) for i in range(2)]
        dwk = cx.sb(es, "sadw", [16, 2 * CH], F32)
        WT = cx.sb(es, "sawt", [16, CH], F32)
        cdc = cx.sb(es, "sacd", [16, 2], F32)
        dtT = cx.sb(es, "sadT", [128, 32], F32)
        acT = cx.sb(es, "saaT", [128, 32], F32)
        WTT = cx.sb(es, "sawT", [128, 32], F32)
        Btk = cx.sb(es, "saBt", [128, 128], BF16)
        cdr = cx.sb(es, "sacr", [1, 16], F32)
        cdb = cx.sb(es, "sacb", [128, 16], F32)
        scT = cx.sb(es, "sasc", [128, 2 * CH], F32)
        dcy = [cx.sb(es, f"sady{i}", [128, 2 * CH], F32) for i in range(2)]
        MT = [cx.sb(es, f"samt{i}", [128, 2 * CH], BF16) for i in range(2)]
        eas = [cx.sb(es, f"saea{i}", [64, CH], F32) for i in range(2)]
        Cs = [cx.sb(es, f"sacs{i}", [64, CH], BF16) for i in range(2)]
        Bs = [cx.sb(es, f"sabs{i}", [128, 128], BF16) for i in range(2)]
        Sbf = [cx.sb(es, f"sasb{i}", [64, 128], BF16) for i in range(2)]
        Sst = [cx.sb(es, f"sast{i}", [64, 16 * 128], F32) for i in range(2)]
        xtk = [cx.sb(es, f"saxt{i}", [128, 2 * 128], BF16) for i in range(2)]
        yst = cx.sb(es, "says", [128, 16 * CH], F32)
        psA = cx.ps(es, "sapA", [128, CH])
        psY = [cx.ps(es, f"sapY{i}", [128, CH]) for i in range(2)]
        psS = cx.ps(es, "sapS", [64, 128])
        psT = [cx.ps(es, f"sapT{i}", [128, CH]) for i in range(2)]
        psTb = [cx.ps(es, f"sapB{i}", [128, CH], BF16) for i in range(2)]

        q0, g0, o0, m0, a0, v0 = (cx.n[k] for k in ("qs", "qg", "qo", "mm", "ac", "ve"))
        MMC, VEC, ACC = 92, 206, 50
        mmc = lambda c: m0 + c * MMC
        vec = lambda c: v0 + 2 + c * VEC
        acc = lambda c: a0 + c * ACC
        mh = lambda c, h: mmc(c) + 12 + h * 5
        vh = lambda c, h: vec(c) + 14 + h * 12
        ah = lambda c, h: acc(c) + 2 + h * 3

        @blk.sync
        def _(s):
            cx.wait_snap(s, bar)
            qoc = o0
            qoc = cx.sdma(s, dts[:], cx.b["dtv"], qoc)
            qoc = cx.sdma(s, acs[:], cx.b["acumd"], qoc)
            s.dma_start(out=bcbB[:], in_=cx.b["bcbf"][0:64, :]).then_inc(cx.sem["qs"], 16)
            s.dma_start(out=bcbC[:], in_=cx.b["bcbf"][64:128, :]).then_inc(cx.sem["qs"], 16)
            for c in range(NCH):
                if c >= 2:
                    s.wait_ge(cx.sem["mm"], mmc(c - 1))
                    s.wait_ge(cx.sem["ve"], vec(c - 1))
                s.dma_start(out=xbf[c % 2].rearrange("p (h t) -> p h t", t=CH),
                            in_=cx.b["xs_bf"].rearrange("(h p) t -> p h t", p=128)[
                                :, :, c * CH:(c + 1) * CH]).then_inc(cx.sem["qs"], 16)
                s.dma_start(out=xf[c % 2].rearrange("p (h t) -> p h t", t=CH),
                            in_=cx.b["xs_f32"].rearrange("(h p) t -> p h t", p=128)[
                                :, :, c * CH:(c + 1) * CH]).then_inc(cx.sem["qs"], 16)

        @blk.tensor
        def _(t):
            cx.wait_snap(t, bar)
            t.wait_ge(cx.sem["qo"], o0 + 32)
            t.wait_ge(cx.sem["qs"], q0 + 32)
            kt = [0]
            kb = [0]

            for c in range(NCH):
                cs = slice(c * CH, (c + 1) * CH)
                t.wait_ge(cx.sem["qs"], q0 + 32 + (c + 1) * 32)
                mb = mmc(c)
                vb = vec(c)
                ab = acc(c)
                fevac = [vb + 3, vb + 4, vb + 5, vb + 6, vb + 9, vb + 10, vb + 11, vb + 12, vb + 13]

                def tr_f32(in_ap, n_in_part, n_cols):
                    k = kt[0]
                    if k >= 2:
                        t.wait_ge(cx.sem["ve"], fevac[k - 2])
                    elif c >= 1:
                        t.wait_ge(cx.sem["ve"], vec(c - 1) + 13)
                    t.transpose(psT[k % 2][0:n_cols, 0:n_in_part], in_ap,
                                C["eyef"][0:n_in_part, 0:n_in_part]).then_inc(cx.sem["mm"], 1)
                    kt[0] += 1

                kt[0] = 0
                kb[0] = 0
                for j in range(2):
                    tr_f32(dts[:, c * CH + j * 128:c * CH + j * 128 + 128], 16, 128)
                for j in range(2):
                    tr_f32(acs[:, c * CH + j * 128:c * CH + j * 128 + 128], 16, 128)
                for j in range(2):
                    k = kb[0]
                    if k >= 2:
                        t.wait_ge(cx.sem["ve"], vb + 7 + (k - 2))
                    elif c >= 1:
                        t.wait_ge(cx.sem["ve"], vh(c - 1, 15) + 10)
                    t.transpose(psTb[k % 2][0:128, 0:64],
                                bcbB[:, c * CH + j * 128:c * CH + j * 128 + 128],
                                C["eyeb"][0:64, 0:64]).then_inc(cx.sem["mm"], 1)
                    kb[0] += 1
                for j in range(2):
                    k = kt[0]
                    if k >= 2:
                        t.wait_ge(cx.sem["ve"], fevac[k - 2])
                    t.matmul(psT[k % 2][:, 0:CH],
                             bcbB[:, c * CH + j * 128:c * CH + j * 128 + 128],
                             bcbC[:, cs], start=True, stop=True).then_inc(cx.sem["mm"], 1)
                    kt[0] += 1
                t.wait_ge(cx.sem["ve"], vb + 2)
                for j in range(2):
                    tr_f32(WT[:, j * 128:(j + 1) * 128], 16, 128)
                t.wait_ge(cx.sem["ac"], ab + 2)
                k = kt[0]
                if k >= 2:
                    t.wait_ge(cx.sem["ve"], fevac[k - 2])
                t.transpose(psT[k % 2][0:1, 0:16], cdc[:, 1:2], C["eyef"][0:16, 0:16]
                            ).then_inc(cx.sem["mm"], 1)
                kt[0] += 1
                t.wait_ge(cx.sem["ve"], vb + 13)
                if c >= 1:
                    t.wait_ge(cx.sem["ve"], vh(c - 1, 15) + 2)
                    t.wait_ge(cx.sem["ac"], ah(c - 1, 15) + 3)
                t.matmul(psA[:, 0:16], C["ones_l"][:], cdr[:], start=True, stop=True
                         ).then_inc(cx.sem["mm"], 1)
                for h in range(NH):
                    hb2 = h % 2
                    if h >= 1:
                        t.wait_ge(cx.sem["ve"], vh(c, h - 1) + 2)
                        t.wait_ge(cx.sem["ac"], ah(c, h - 1) + 3)
                    else:
                        t.wait_ge(cx.sem["ve"], vb + 14)
                    t.matmul(psA[:, 0:CH], C["hsel"][:, h * 128:(h + 1) * 128], acs[:, cs],
                             start=True, stop=True).then_inc(cx.sem["mm"], 1)
                    for j in range(2):
                        k = kb[0]
                        if k >= 2:
                            if k - 2 < 2:
                                t.wait_ge(cx.sem["ve"], vb + 7 + (k - 2))
                            else:
                                pk = k - 4
                                ph, pj = divmod(pk, 2)
                                t.wait_ge(cx.sem["ve"], vh(c, ph) + 9 + pj + 1)
                        t.transpose(psTb[k % 2][0:128, 0:128],
                                    xbf[c % 2][:, h * CH + j * 128:h * CH + j * 128 + 128],
                                    C["eyeb"][:]).then_inc(cx.sem["mm"], 1)
                        kb[0] += 1
                    t.wait_ge(cx.sem["ve"], vh(c, h) + 10)
                    if h >= 2:
                        t.wait_ge(cx.sem["ve"], vh(c, h - 2) + 12)
                    elif c >= 1:
                        t.wait_ge(cx.sem["ve"], vh(c - 1, 14 + h) + 12)
                    inst = None
                    for j in range(2):
                        inst = t.matmul(psY[hb2][:, 0:CH], xtk[hb2][:, j * 128:(j + 1) * 128],
                                        MT[hb2][:, j * CH:(j + 1) * CH], start=(j == 0), stop=False)
                    inst = t.matmul(psY[hb2][:, 0:CH], Sbf[hb2][:], Cs[hb2][:], start=False, stop=True)
                    inst.then_inc(cx.sem["mm"], 1)
                    if h >= 1:
                        t.wait_ge(cx.sem["ve"], vh(c, h - 1) + 11)
                    elif c >= 1:
                        t.wait_ge(cx.sem["ve"], vh(c - 1, 15) + 11)
                    inst = None
                    for j in range(2):
                        inst = t.matmul(psS[:], Bs[hb2][:, j * 64:(j + 1) * 64],
                                        xtk[hb2][:, j * 128:(j + 1) * 128],
                                        start=(j == 0), stop=(j == 1))
                    inst.then_inc(cx.sem["mm"], 1)

        @blk.scalar
        def _(a):
            cx.wait_snap(a, bar)
            for c in range(NCH):
                vb = vec(c)
                ab = acc(c)
                a.wait_ge(cx.sem["ve"], vb + 1)
                if c >= 1:
                    a.wait_ge(cx.sem["ve"], vec(c - 1) + 2)
                    a.wait_ge(cx.sem["mm"], mmc(c - 1) + 11)
                a.activation(dwk[:, CH:2 * CH], dwk[:, 0:CH], AF.Exp, scale=C["cv"][0:16, 3:4]
                             ).then_inc(cx.sem["ac"], 1)
                a.activation(cdc[:, 1:2], acs[:, c * CH + 255:c * CH + 256], AF.Exp
                             ).then_inc(cx.sem["ac"], 1)
                for h in range(NH):
                    hb2 = h % 2
                    a.wait_ge(cx.sem["ve"], vh(c, h) + 2)
                    a.activation(dcy[hb2][:, 0:CH], dcy[hb2][:, 0:CH], AF.Exp).then_inc(cx.sem["ac"], 1)
                    a.activation(dcy[hb2][:, CH:2 * CH], dcy[hb2][:, CH:2 * CH], AF.Exp).then_inc(cx.sem["ac"], 1)
                    a.wait_ge(cx.sem["mm"], mh(c, h) + 1)
                    if h >= 2:
                        a.wait_ge(cx.sem["ve"], vh(c, h - 2) + 5)
                    elif c >= 1:
                        a.wait_ge(cx.sem["ve"], vh(c - 1, 14 + h) + 5)
                    a.activation(eas[hb2][:], psA[0:64, 0:CH], AF.Exp).then_inc(cx.sem["ac"], 1)

        @blk.vector
        def _(v):
            cx.wait_snap(v, bar)
            v.memset(Sst[0][:], 0.0).then_inc(cx.sem["ve"], 1)
            v.memset(Sst[1][:], 0.0).then_inc(cx.sem["ve"], 1)
            v.wait_ge(cx.sem["qo"], o0 + 32)
            for c in range(NCH):
                cs = slice(c * CH, (c + 1) * CH)
                mb = mmc(c)
                vb = vec(c)
                ab = acc(c)
                cur, nxt = c % 2, (c + 1) % 2
                if c >= 1:
                    v.wait_ge(cx.sem["ac"], acc(c - 1) + 1)
                v.tensor_scalar(dwk[:, 0:CH], acs[:, cs], acs[:, c * CH + 255:c * CH + 256],
                                None, ALU.subtract).then_inc(cx.sem["ve"], 1)
                v.wait_ge(cx.sem["ac"], ab + 1)
                v.tensor_mul(WT[:, 0:CH], dts[:, cs], dwk[:, CH:2 * CH]).then_inc(cx.sem["ve"], 1)
                for i in range(4):
                    v.wait_ge(cx.sem["mm"], mb + i + 1)
                    tgt = dtT if i < 2 else acT
                    v.tensor_copy(tgt[:, (i % 2) * 16:(i % 2) * 16 + 16], psT[i % 2][:, 0:16]
                                  ).then_inc(cx.sem["ve"], 1)
                for j in range(2):
                    v.wait_ge(cx.sem["mm"], mb + 4 + j + 1)
                    v.tensor_copy(Btk[:, j * 64:(j + 1) * 64], psTb[j % 2][0:128, 0:64]
                                  ).then_inc(cx.sem["ve"], 1)
                for j in range(2):
                    v.wait_ge(cx.sem["mm"], mb + 6 + j + 1)
                    v.tensor_copy(scT[:, j * CH:(j + 1) * CH], psT[j % 2][:, 0:CH]
                                  ).then_inc(cx.sem["ve"], 1)
                for j in range(2):
                    v.wait_ge(cx.sem["mm"], mb + 8 + j + 1)
                    v.tensor_copy(WTT[:, j * 16:(j + 1) * 16], psT[j % 2][:, 0:16]
                                  ).then_inc(cx.sem["ve"], 1)
                v.wait_ge(cx.sem["mm"], mb + 11)
                v.tensor_copy(cdr[:], psT[0 if (10 % 2 == 0) else 1][0:1, 0:16]).then_inc(cx.sem["ve"], 1)
                v.wait_ge(cx.sem["mm"], mb + 12)
                v.tensor_copy(cdb[:], psA[:, 0:16]).then_inc(cx.sem["ve"], 1)
                for h in range(NH):
                    hb2 = h % 2
                    hs = slice(h * CH, (h + 1) * CH)
                    v.wait_ge(cx.sem["mm"], mh(c, h) + 1)
                    if h >= 2:
                        v.wait_ge(cx.sem["ac"], ah(c, h - 2) + 2)
                    elif c >= 1:
                        v.wait_ge(cx.sem["ac"], ah(c - 1, 14 + h) + 2)
                    for j in range(2):
                        v.scalar_tensor_tensor(dcy[hb2][:, j * CH:(j + 1) * CH], psA[:, 0:CH],
                                               acT[:, j * 16 + h:j * 16 + h + 1],
                                               C["trim"][:, j * CH:(j + 1) * CH],
                                               ALU.subtract, ALU.add).then_inc(cx.sem["ve"], 1)
                    v.wait_ge(cx.sem["ac"], ah(c, h) + 2)
                    if h >= 2:
                        v.wait_ge(cx.sem["mm"], mh(c, h - 2) + 5)
                    elif c >= 1:
                        v.wait_ge(cx.sem["mm"], mh(c - 1, 14 + h) + 5)
                    for j in range(2):
                        v.scalar_tensor_tensor(MT[hb2][:, j * CH:(j + 1) * CH],
                                               dcy[hb2][:, j * CH:(j + 1) * CH],
                                               dtT[:, j * 16 + h:j * 16 + h + 1],
                                               scT[:, j * CH:(j + 1) * CH],
                                               ALU.mult, ALU.mult).then_inc(cx.sem["ve"], 1)
                    v.wait_ge(cx.sem["ac"], ah(c, h) + 3)
                    v.tensor_mul(Cs[hb2][:], bcbC[:, cs], eas[hb2][:]).then_inc(cx.sem["ve"], 1)
                    for j in range(2):
                        v.tensor_scalar(Bs[hb2][:, j * 64:(j + 1) * 64], Btk[:, j * 64:(j + 1) * 64],
                                        WTT[:, j * 16 + h:j * 16 + h + 1], None, ALU.mult
                                        ).then_inc(cx.sem["ve"], 1)
                    v.tensor_copy(Sbf[hb2][:], Sst[cur][:, h * 128:(h + 1) * 128]).then_inc(cx.sem["ve"], 1)
                    for j in range(2):
                        v.wait_ge(cx.sem["mm"], mh(c, h) + 1 + j + 1)
                        v.tensor_copy(xtk[hb2][:, j * 128:(j + 1) * 128],
                                      psTb[(2 + h * 2 + j) % 2][:, 0:128]).then_inc(cx.sem["ve"], 1)
                    v.wait_ge(cx.sem["mm"], mh(c, h) + 5)
                    v.scalar_tensor_tensor(Sst[nxt][:, h * 128:(h + 1) * 128],
                                           Sst[cur][:, h * 128:(h + 1) * 128],
                                           cdb[0:64, h:h + 1], psS[:],
                                           ALU.mult, ALU.add).then_inc(cx.sem["ve"], 1)
                    v.wait_ge(cx.sem["mm"], mh(c, h) + 4)
                    if c >= 1:
                        v.wait_ge(cx.sem["qg"], g0 + c * 16)
                    v.scalar_tensor_tensor(yst[:, hs], xf[c % 2][:, hs], C["dpb"][:, h:h + 1],
                                           psY[hb2][:, 0:CH], ALU.mult, ALU.add
                                           ).then_inc(cx.sem["ve"], 1)

        @blk.gpsimd
        def _(g):
            cx.wait_snap(g, bar)
            for c in range(NCH):
                g.wait_ge(cx.sem["ve"], vec(c) + VEC)
                g.dma_start(out=cx.b["ybuf"].rearrange("(h p) t -> p h t", p=128)[
                                :, :, c * CH:(c + 1) * CH],
                            in_=yst.rearrange("p (h t) -> p h t", t=CH)).then_inc(cx.sem["qg"], 16)

        cx.n["qs"] = q0 + 32 + NCH * 32
        cx.n["qg"] = g0 + NCH * 16
        cx.n["qo"] = o0 + 32
        cx.n["mm"] = m0 + NCH * MMC
        cx.n["ac"] = a0 + NCH * ACC
        cx.n["ve"] = v0 + 2 + NCH * VEC


def s11_gate(cx):
    nc = cx.nc
    C = cx.C
    bar = cx.snap()
    es = ExitStack()
    with es, nc.Block() as blk:
        yb = [cx.sb(es, f"sby{i}", [128, 512], F32) for i in range(2)]
        zb = [cx.sb(es, f"sbz{i}", [128, 512], F32) for i in range(2)]
        sz = cx.sb(es, "sbsz", [128, 512], F32)
        sq = cx.sb(es, "sbsq", [128, 512], F32)
        rr = cx.sb(es, "sbr", [1, 1024], F32)
        rstdb = cx.sb(es, "sbrb", [128, S], F32)
        y2 = [cx.sb(es, f"sby2{i}", [128, 2048], F32) for i in range(2)]
        z2 = [cx.sb(es, f"sbz2{i}", [128, 2048], F32) for i in range(2)]
        sz2 = cx.sb(es, "sbs2", [128, 2048], F32)
        ynh = cx.sb(es, "sbyn", [128, 16 * 2048], BF16)
        wop = cx.sb(es, "sbw", [128, 16 * D], BF16)
        nw = cx.sb(es, "sbnw", [128, 16], F32)
        stg = [cx.sb(es, f"sbo{i}", [128, 512], F32) for i in range(2)]
        ps_r = cx.ps(es, "sbpr", [1, 512])
        ps_b = cx.ps(es, "sbpb", [128, 512])
        ps_o = [cx.ps(es, f"sbpo{i}", [128, 512]) for i in range(2)]
        q0, g0, o0, m0, a0, v0 = (cx.n[k] for k in ("qs", "qg", "qo", "mm", "ac", "ve"))
        MT1, VT1, AT1 = 17, 17, 34
        m1 = lambda tt: m0 + tt * MT1
        v1 = lambda tt: v0 + tt * VT1
        a1 = lambda tt: a0 + tt * AT1
        M1, V1, A1, Q1 = 8 * MT1, 8 * VT1, 8 * AT1, 8 * 32 * 16
        m2_ = lambda th: m0 + M1 + th * 32
        v2_ = lambda th: v0 + V1 + th * 64
        a2_ = lambda th: a0 + A1 + th * 16

        @blk.sync
        def _(s):
            cx.wait_snap(s, bar)
            s.dma_start(out=nw[:], in_=cx.i["normw"]).then_inc(cx.sem["qs"], 16)
            for tt in range(8):
                for ec in range(16):
                    i = tt * 16 + ec
                    if i >= 2:
                        pv = i - 2
                        s.wait_ge(cx.sem["ve"], v1(pv // 16) + (pv % 16) + 1)
                        s.wait_ge(cx.sem["ac"], a1(pv // 16) + (pv % 16) * 2 + 2)
                    s.dma_start(out=yb[i % 2][:], in_=cx.b["ybuf"][ec * 128:(ec + 1) * 128,
                                tt * 512:(tt + 1) * 512]).then_inc(cx.sem["qs"], 16)
                    s.dma_start(out=zb[i % 2][:], in_=cx.b["zc"][ec * 128:(ec + 1) * 128,
                                tt * 512:(tt + 1) * 512]).then_inc(cx.sem["qs"], 16)
            for th in range(2):
                for ec in range(16):
                    i = th * 16 + ec
                    if i >= 2:
                        pv = i - 2
                        s.wait_ge(cx.sem["ve"], v2_(pv // 16) + (pv % 16) * 2 + 2)
                    s.dma_start(out=y2[i % 2][:], in_=cx.b["ybuf"][ec * 128:(ec + 1) * 128,
                                th * 2048:(th + 1) * 2048]).then_inc(cx.sem["qs"], 16)
                    s.dma_start(out=z2[i % 2][:], in_=cx.b["zc"][ec * 128:(ec + 1) * 128,
                                th * 2048:(th + 1) * 2048]).then_inc(cx.sem["qs"], 16)

        @blk.gpsimd
        def _(g):
            cx.wait_snap(g, bar)
            g.dma_start(out=wop.rearrange("p (c o) -> p c o", c=16),
                        in_=cx.i["woprojT"].rearrange("(c p) o -> p c o", c=16)
                        ).then_inc(cx.sem["qg"], 16)
            for th in range(2):
                for dc in range(8):
                    for t4 in range(4):
                        i = (th * 8 + dc) * 4 + t4
                        g.wait_ge(cx.sem["ve"], v2_(th) + 32 + i % (32 * 4) + 1 if False else 0)
                        k = dc * 4 + t4
                        g.wait_ge(cx.sem["ve"], v2_(th) + 32 + k + 1)
                        g.dma_start(out=cx.b["s1c"][dc * 128:(dc + 1) * 128,
                                    th * 2048 + t4 * 512:th * 2048 + (t4 + 1) * 512],
                                    in_=stg[k % 2][:]).then_inc(cx.sem["qg"], 16)

        @blk.scalar
        def _(a):
            cx.wait_snap(a, bar)
            for tt in range(8):
                for ec in range(16):
                    i = tt * 16 + ec
                    a.wait_ge(cx.sem["qs"], q0 + 16 + (i + 1) * 32)
                    if i >= 1:
                        a.wait_ge(cx.sem["ve"], v1((i - 1) // 16) + ((i - 1) % 16) + 1)
                    a.activation(sz[:], zb[i % 2][:], AF.Silu).then_inc(cx.sem["ac"], 1)
                    a.wait_ge(cx.sem["ve"], v1(tt) + ec + 1)
                    if i >= 1:
                        a.wait_ge(cx.sem["mm"], m1((i - 1) // 16) + ((i - 1) % 16) + 1)
                    a.activation(sq[:], sz[:], AF.Square).then_inc(cx.sem["ac"], 1)
                a.wait_ge(cx.sem["mm"], m1(tt) + 16)
                a.activation(rr[0:1, 0:512], ps_r[:], AF.Ln, bias=C["cv"][0:1, 0:1],
                             scale=C["cv"][0:1, 4:5]).then_inc(cx.sem["ac"], 1)
                a.wait_ge(cx.sem["ac"], a1(tt) + 33)
                a.activation(rr[0:1, 512:1024], rr[0:1, 0:512], AF.Exp,
                             scale=C["cv"][0:1, 1:2]).then_inc(cx.sem["ac"], 1)
            for th in range(2):
                for ec in range(16):
                    i = th * 16 + ec
                    a.wait_ge(cx.sem["qs"], q0 + 16 + 256 * 16 + (i + 1) * 32)
                    if i >= 1:
                        a.wait_ge(cx.sem["ve"], v2_((i - 1) // 16) + ((i - 1) % 16) * 2 + 2)
                    a.activation(sz2[:], z2[i % 2][:], AF.Silu).then_inc(cx.sem["ac"], 1)

        @blk.tensor
        def _(t):
            cx.wait_snap(t, bar)
            for tt in range(8):
                for ec in range(16):
                    t.wait_ge(cx.sem["ac"], a1(tt) + ec * 2 + 2)
                    if tt >= 1 and ec == 0:
                        t.wait_ge(cx.sem["ac"], a1(tt - 1) + 33)
                    t.matmul(ps_r[:], C["ones_c"][:], sq[:], start=(ec == 0), stop=(ec == 15)
                             ).then_inc(cx.sem["mm"], 1)
                t.wait_ge(cx.sem["ac"], a1(tt) + 34)
                if tt >= 1:
                    t.wait_ge(cx.sem["ve"], v1(tt - 1) + 17)
                t.matmul(ps_b[:], C["ones_l"][:], rr[0:1, 512:1024], start=True, stop=True
                         ).then_inc(cx.sem["mm"], 1)
            t.wait_ge(cx.sem["qg"], g0 + 16)
            for th in range(2):
                for dc in range(8):
                    for t4 in range(4):
                        k = dc * 4 + t4
                        gk = th * 32 + k
                        t.wait_ge(cx.sem["ve"], v2_(th) + 16 + 16)
                        if gk >= 2:
                            pk = gk - 2
                            t.wait_ge(cx.sem["ve"], v2_(pk // 32) + 32 + (pk % 32) + 1)
                        inst = None
                        for ec in range(16):
                            inst = t.matmul(ps_o[gk % 2][:],
                                            wop[:, ec * D + dc * 128:ec * D + (dc + 1) * 128],
                                            ynh[:, ec * 2048 + t4 * 512:ec * 2048 + (t4 + 1) * 512],
                                            start=(ec == 0), stop=(ec == 15))
                        inst.then_inc(cx.sem["mm"], 1)

        @blk.vector
        def _(v):
            cx.wait_snap(v, bar)
            for tt in range(8):
                for ec in range(16):
                    i = tt * 16 + ec
                    v.wait_ge(cx.sem["ac"], a1(tt) + ec * 2 + 1)
                    v.tensor_mul(sz[:], yb[i % 2][:], sz[:]).then_inc(cx.sem["ve"], 1)
                v.wait_ge(cx.sem["mm"], m1(tt) + 17)
                v.tensor_copy(rstdb[:, tt * 512:(tt + 1) * 512], ps_b[:]).then_inc(cx.sem["ve"], 1)
            for th in range(2):
                for ec in range(16):
                    i = th * 16 + ec
                    v.wait_ge(cx.sem["ac"], a2_(th) + ec + 1)
                    v.tensor_mul(sz2[:], y2[i % 2][:], sz2[:]).then_inc(cx.sem["ve"], 1)
                    v.wait_ge(cx.sem["ve"], v2_(th) + ec * 2 + 1)
                    v.wait_ge(cx.sem["qs"], q0 + 16)
                    if th == 0 and ec == 0:
                        v.wait_ge(cx.sem["qg"], g0 + 16)
                    if th >= 1:
                        v.wait_ge(cx.sem["mm"], m0 + M1 + 32)
                    v.scalar_tensor_tensor(ynh[:, ec * 2048:(ec + 1) * 2048], sz2[:],
                                           nw[:, ec:ec + 1],
                                           rstdb[:, th * 2048:(th + 1) * 2048],
                                           ALU.mult, ALU.mult).then_inc(cx.sem["ve"], 1)
                for dc in range(8):
                    for t4 in range(4):
                        k = dc * 4 + t4
                        gk = th * 32 + k
                        v.wait_ge(cx.sem["mm"], m0 + M1 + gk + 1)
                        if gk >= 2:
                            v.wait_ge(cx.sem["qg"], g0 + 16 + (gk - 2 + 1) * 16)
                        v.tensor_copy(stg[k % 2][:], ps_o[gk % 2][:]).then_inc(cx.sem["ve"], 1)

        cx.n["qs"] = q0 + 16 + 256 * 16 + 32 * 2 * 16
        cx.n["qg"] = g0 + 16 + 64 * 16
        cx.n["mm"] = m0 + M1 + 64
        cx.n["ac"] = a0 + A1 + 32
        cx.n["ve"] = v0 + V1 + 2 * 64


def s12_ffn1(cx):
    nc = cx.nc
    C = cx.C
    bar = cx.snap()
    es = ExitStack()
    with es, nc.Block() as blk:
        xnT = cx.sb(es, "scx", [128, 8 * S], BF16)
        cbs = cx.sb(es, "scw", [128, 8 * D], BF16)
        scbr = cx.sb(es, "scs", [1, D], F32)
        fbr = cx.sb(es, "scf", [1, D], F32)
        scbb = cx.sb(es, "scsb", [128, D], F32)
        fbb = cx.sb(es, "scfb", [128, D], F32)
        x1t = [cx.sb(es, f"scx1{i}", [128, D], F32) for i in range(2)]
        s1c = [cx.sb(es, f"scs1{i}", [128, 8 * 128], F32) for i in range(2)]
        s1t = cx.sb(es, "scst", [128, D], F32)
        t1 = cx.sb(es, "sct1", [128, D], F32)
        g1 = cx.sb(es, "scg1", [128, D], F32)
        ot = [cx.sb(es, f"scot{i}", [128, D], F32) for i in range(2)]
        ps_g = [cx.ps(es, f"scpg{i}", [128, 1024]) for i in range(2)]
        ps_t = [cx.ps(es, f"scpt{i}", [128, 128]) for i in range(2)]
        ps_b = cx.ps(es, "scpb", [128, 1024])
        q0, g0, o0, m0, a0, v0 = (cx.n[k] for k in ("qs", "qg", "qo", "mm", "ac", "ve"))
        MT, VT = 10, 12
        mtc = lambda tc: m0 + 4 + tc * MT
        vtc = lambda tc: v0 + 4 + tc * VT

        @blk.sync
        def _(s):
            cx.wait_snap(s, bar)
            qoc = o0
            qoc = cx.sdma(s, scbr[:], cx.i["scbr"], qoc)
            qoc = cx.sdma(s, fbr[:], cx.i["fbiasr"], qoc)
            for dc in range(8):
                qoc = cx.sdma(s, xnT[:, dc * S:(dc + 1) * S],
                              cx.b["xn1t_bf"][:, dc * 128:(dc + 1) * 128], qoc, transpose=True)
            for tc in range(32):
                if tc >= 2:
                    s.wait_ge(cx.sem["ve"], vtc(tc - 2) + VT)
                s.dma_start(out=x1t[tc % 2][:], in_=cx.b["x1t"][tc * 128:(tc + 1) * 128, :]
                            ).then_inc(cx.sem["qs"], 16)
                s.dma_start(out=s1c[tc % 2].rearrange("p (c t) -> p c t", c=8),
                            in_=cx.b["s1c"].rearrange("(c p) t -> p c t", p=128)[
                                :, :, tc * 128:(tc + 1) * 128]).then_inc(cx.sem["qs"], 16)

        @blk.gpsimd
        def _(g):
            cx.wait_snap(g, bar)
            g.dma_start(out=cbs.rearrange("p (c o) -> p c o", c=8),
                        in_=cx.i["cbT"].rearrange("(c p) o -> p c o", c=8)).then_inc(cx.sem["qg"], 16)

        @blk.tensor
        def _(t):
            cx.wait_snap(t, bar)
            t.wait_ge(cx.sem["qo"], o0 + 32)
            for half in range(2):
                t.matmul(ps_b[:, half * 512:(half + 1) * 512], C["ones_l"][:],
                         scbr[0:1, half * 512:(half + 1) * 512], start=True, stop=True
                         ).then_inc(cx.sem["mm"], 1)
            t.wait_ge(cx.sem["ve"], v0 + 2)
            for half in range(2):
                t.matmul(ps_b[:, half * 512:(half + 1) * 512], C["ones_l"][:],
                         fbr[0:1, half * 512:(half + 1) * 512], start=True, stop=True
                         ).then_inc(cx.sem["mm"], 1)
            t.wait_ge(cx.sem["qo"], o0 + 160)
            t.wait_ge(cx.sem["qg"], g0 + 16)
            for tc in range(32):
                mb = mtc(tc)
                for oh in range(2):
                    gg = tc * 2 + oh
                    if gg >= 2:
                        t.wait_ge(cx.sem["ve"], vtc((gg - 2) // 2) + 9)
                    inst = None
                    for dc in range(8):
                        inst = t.matmul(ps_g[tc % 2][:, oh * 512:(oh + 1) * 512],
                                        xnT[:, dc * S + tc * 128:dc * S + (tc + 1) * 128],
                                        cbs[:, dc * D + oh * 512:dc * D + (oh + 1) * 512],
                                        start=(dc == 0), stop=(dc == 7))
                    inst.then_inc(cx.sem["mm"], 1)
                t.wait_ge(cx.sem["qs"], q0 + (tc + 1) * 32)
                for dc in range(8):
                    i = tc * 8 + dc
                    if i >= 2:
                        pv = i - 2
                        t.wait_ge(cx.sem["ve"], vtc(pv // 8) + (pv % 8) + 1)
                    t.transpose(ps_t[dc % 2][:], s1c[tc % 2][:, dc * 128:(dc + 1) * 128],
                                C["eyef"][:]).then_inc(cx.sem["mm"], 1)

        @blk.scalar
        def _(a):
            cx.wait_snap(a, bar)
            for tc in range(32):
                a.wait_ge(cx.sem["ve"], vtc(tc) + 10)
                if tc >= 1:
                    a.wait_ge(cx.sem["ve"], vtc(tc - 1) + 11)
                a.activation(g1[:], t1[:], AF.Silu).then_inc(cx.sem["ac"], 1)

        @blk.vector
        def _(v):
            cx.wait_snap(v, bar)
            for half in range(2):
                v.wait_ge(cx.sem["mm"], m0 + half + 1)
                v.tensor_copy(scbb[:, half * 512:(half + 1) * 512],
                              ps_b[:, half * 512:(half + 1) * 512]).then_inc(cx.sem["ve"], 1)
            for half in range(2):
                v.wait_ge(cx.sem["mm"], m0 + 2 + half + 1)
                v.tensor_copy(fbb[:, half * 512:(half + 1) * 512],
                              ps_b[:, half * 512:(half + 1) * 512]).then_inc(cx.sem["ve"], 1)
            for tc in range(32):
                vb = vtc(tc)
                mb = mtc(tc)
                for dc in range(8):
                    v.wait_ge(cx.sem["mm"], mb + 2 + dc + 1)
                    v.tensor_copy(s1t[:, dc * 128:(dc + 1) * 128], ps_t[dc % 2][:]
                                  ).then_inc(cx.sem["ve"], 1)
                v.wait_ge(cx.sem["mm"], mb + 2)
                if tc >= 1:
                    v.wait_ge(cx.sem["ac"], a0 + tc)
                v.scalar_tensor_tensor(t1[:], ps_g[tc % 2][:], 1.0 / 127.0, scbb[:],
                                       ALU.mult, ALU.mult).then_inc(cx.sem["ve"], 1)
                v.wait_ge(cx.sem["ve"], vb + 9)
                v.tensor_add(t1[:], t1[:], fbb[:]).then_inc(cx.sem["ve"], 1)
                v.wait_ge(cx.sem["ac"], a0 + tc + 1)
                v.tensor_mul(g1[:], g1[:], s1t[:]).then_inc(cx.sem["ve"], 1)
                v.wait_ge(cx.sem["ve"], vb + 11)
                if tc >= 2:
                    v.wait_ge(cx.sem["qg"], g0 + 16 + (tc - 2 + 1) * 16)
                v.tensor_add(ot[tc % 2][:], x1t[tc % 2][:], g1[:]).then_inc(cx.sem["ve"], 1)

        @blk.gpsimd
        def _(g):
            for tc in range(32):
                g.wait_ge(cx.sem["ve"], vtc(tc) + VT)
                g.dma_start(out=cx.b["outt"][tc * 128:(tc + 1) * 128, :], in_=ot[tc % 2][:]
                            ).then_inc(cx.sem["qg"], 16)
            g.wait_ge(cx.sem["qg"], g0 + 16 + 32 * 16)

        cx.n["qs"] = q0 + 32 * 32
        cx.n["qg"] = g0 + 16 + 32 * 16
        cx.n["qo"] = o0 + 160
        cx.n["mm"] = m0 + 4 + 32 * MT
        cx.n["ac"] = a0 + 32
        cx.n["ve"] = v0 + 4 + 32 * VT


def s6_moe(cx):
    nc = cx.nc
    C = cx.C
    bar = cx.snap()
    es = ExitStack()
    with es, nc.Block() as blk:
        w1sb = cx.sb(es, "s6w1", [128, 8 * FF], BF16)
        w2sb = cx.sb(es, "s6w2", [128, 32 * D], BF16)
        b1sb = [cx.sb(es, f"s6b{i}", [128, 32], F32) for i in range(2)]
        b2b = cx.sb(es, "s6b2", [1, D], BF16)
        xnT = [cx.sb(es, f"s6x{i}", [128, 8 * 512], BF16) for i in range(2)]
        hst = [cx.sb(es, f"s6h{i}", [128, 8 * 512], BF16) for i in range(2)]
        hsb = [cx.sb(es, f"s6g{i}", [128, 32 * 128], BF16) for i in range(2)]
        msal = cx.sb(es, "s6m", [128, 32 * 8], F32)
        tmp = [cx.sb(es, f"s6t{i}", [128, D], F32) for i in range(2)]
        ps4 = [cx.ps(es, f"s6p{i}", [128, 512]) for i in range(4)]
        psY = [cx.ps(es, f"s6q{i}", [128, 1024]) for i in range(2)]
        q0, g0, o0, m0, a0, v0 = (cx.n[k] for k in ("qs", "qg", "qo", "mm", "ac", "ve"))
        MME = 320

        def mmh(e, tt, fc):
            return m0 + e * MME + tt * 32 + fc + 1

        def mmy(e, tc, oh):
            return m0 + e * MME + 256 + tc * 2 + oh + 1

        def qbase(e):
            return g0 + (16 + e * 80) * 16

        def qg_piece(e, tt, p4):
            return qbase(e) + (tt * 4 + p4 + 1) * 16

        def qg_w1(e):
            return g0 + 8 * 16 if e == 0 else qbase(e - 1) + 40 * 16

        def qg_w2(e):
            return g0 + 16 * 16 if e == 0 else qbase(e - 1) + 80 * 16

        def qg_acc(e, tc):
            return qbase(e) + (40 + tc + 1) * 16

        def vey(e, tc):
            return v0 + e * 32 + tc + 1

        def ach(e, tt, fc):
            return a0 + e * 256 + tt * 32 + fc + 1

        @blk.gpsimd
        def _(g):
            cx.wait_snap(g, bar)
            qoc = o0
            for dc in range(8):
                g.dma_start(out=w1sb[:, dc * FF:(dc + 1) * FF],
                            in_=cx.i["w1e"][0][dc * 128:(dc + 1) * 128, :]).then_inc(cx.sem["qg"], 16)
            for j in range(8):
                g.dma_start(out=w2sb.rearrange("p (c o) -> p c o", c=32)[:, j * 4:(j + 1) * 4, :],
                            in_=cx.i["w2e"][0][j * 512:(j + 1) * 512, :].rearrange(
                                "(c p) o -> p c o", c=4)).then_inc(cx.sem["qg"], 16)
            for e in range(NE):
                if e >= 2:
                    g.wait_ge(cx.sem["ac"], a0 + (e - 1) * 256)
                qoc = cx.sdma(g, b1sb[e % 2][:], cx.i["b1c"][e], qoc)
                if e >= 1:
                    g.wait_ge(cx.sem["mm"], m0 + (e - 1) * MME + 256 + 64)
                qoc = cx.sdma(g, b2b[:], cx.i["b2r"][e:e + 1, :], qoc)
                for tt in range(8):
                    for p4 in range(4):
                        g.wait_ge(cx.sem["ac"], ach(e, tt, p4 * 8 + 7))
                        g.dma_start(
                            out=cx.b["hbuf"][e].rearrange("(fc p) t -> p fc t", p=128)[
                                :, p4 * 8:(p4 + 1) * 8, tt * 512:(tt + 1) * 512],
                            in_=hst[(e * 32 + tt * 4 + p4) % 2].rearrange(
                                "p (fc t) -> p fc t", t=512)).then_inc(cx.sem["qg"], 16)
                if e + 1 < NE:
                    g.wait_ge(cx.sem["mm"], mmh(e, 7, 31))
                    for dc in range(8):
                        g.dma_start(out=w1sb[:, dc * FF:(dc + 1) * FF],
                                    in_=cx.i["w1e"][e + 1][dc * 128:(dc + 1) * 128, :]
                                    ).then_inc(cx.sem["qg"], 16)
                else:
                    for dc in range(8):
                        g.dma_start(out=cx.b["junk"][:, 0:64],
                                    in_=hst[0][:, 0:64]).then_inc(cx.sem["qg"], 16)
                for tc in range(32):
                    g.wait_ge(cx.sem["ve"], vey(e, tc))
                    g.dma_start(out=cx.b["selt"][tc * 128:(tc + 1) * 128, :],
                                in_=tmp[tc % 2][:],
                                accum_op=(ALU.bypass if e == 0 else ALU.add)
                                ).then_inc(cx.sem["qg"], 16)
                if e + 1 < NE:
                    g.wait_ge(cx.sem["mm"], mmy(e, 31, 1))
                    for j in range(8):
                        g.dma_start(out=w2sb.rearrange("p (c o) -> p c o", c=32)[:, j * 4:(j + 1) * 4, :],
                                    in_=cx.i["w2e"][e + 1][j * 512:(j + 1) * 512, :].rearrange(
                                        "(c p) o -> p c o", c=4)).then_inc(cx.sem["qg"], 16)
                else:
                    for j in range(8):
                        g.dma_start(out=cx.b["junk"][:, 64:128],
                                    in_=hst[0][:, 64:128]).then_inc(cx.sem["qg"], 16)
            g.wait_ge(cx.sem["qg"], qbase(NE - 1) + 80 * 16)

        @blk.sync
        def _(s):
            cx.wait_snap(s, bar)
            qoc = o0
            s.dma_start(out=msal.rearrange("p (c e) -> p c e", e=8),
                        in_=cx.b["msb"].rearrange("(c p) e -> p c e", p=128)
                        ).then_inc(cx.sem["qs"], 16)
            for e in range(NE):
                for tt in range(8):
                    ge = e * 8 + tt
                    if ge >= 2:
                        s.wait_ge(cx.sem["mm"], mmh((ge - 2) // 8, (ge - 2) % 8, 31))
                    s.dma_start(out=xnT[tt % 2].rearrange("p (c t) -> p c t", c=8),
                                in_=cx.b["xn0c_bf"].rearrange("(c p) t -> p c t", p=128)[
                                    :, :, tt * 512:(tt + 1) * 512]).then_inc(cx.sem["qs"], 16)
                for tc in range(32):
                    gh = e * 32 + tc
                    if gh >= 2:
                        s.wait_ge(cx.sem["mm"], mmy((gh - 2) // 32, (gh - 2) % 32, 1))
                    s.wait_ge(cx.sem["qg"], qg_piece(e, tc // 4, 3))
                    s.dma_start(out=hsb[gh % 2].rearrange("p (c t) -> p c t", c=32),
                                in_=cx.b["hbuf"][e].rearrange("(c p) t -> p c t", p=128)[
                                    :, :, tc * 128:(tc + 1) * 128]).then_inc(cx.sem["qs"], 16)

        @blk.tensor
        def _(t):
            cx.wait_snap(t, bar)
            for e in range(NE):
                t.wait_ge(cx.sem["qg"], qg_w1(e))
                t.wait_ge(cx.sem["qo"], o0 + (e * 2 + 1) * 16)
                for tt in range(8):
                    ge = e * 8 + tt
                    t.wait_ge(cx.sem["qs"], q0 + 16 + (e * 40 + tt + 1) * 16)
                    for fc in range(32):
                        gi = ge * 32 + fc
                        if gi >= 4:
                            pv = gi - 4
                            t.wait_ge(cx.sem["ac"], a0 + pv + 1)
                        inst = None
                        for dc in range(8):
                            inst = t.matmul(ps4[fc % 4][:],
                                            w1sb[:, dc * FF + fc * 128:dc * FF + (fc + 1) * 128],
                                            xnT[tt % 2][:, dc * 512:(dc + 1) * 512],
                                            start=(dc == 0), stop=(dc == 7))
                        inst.then_inc(cx.sem["mm"], 1)
                t.wait_ge(cx.sem["qg"], qg_w2(e))
                t.wait_ge(cx.sem["qo"], o0 + (e * 2 + 2) * 16)
                for tc in range(32):
                    gh = e * 32 + tc
                    t.wait_ge(cx.sem["qs"], q0 + 16 + (e * 40 + 8 + tc + 1) * 16)
                    if gh >= 2:
                        t.wait_ge(cx.sem["ve"], vey((gh - 2) // 32, (gh - 2) % 32))
                    for oh in range(2):
                        inst = None
                        for fc in range(32):
                            inst = t.matmul(psY[tc % 2][:, oh * 512:(oh + 1) * 512],
                                            hsb[gh % 2][:, fc * 128:(fc + 1) * 128],
                                            w2sb[:, fc * D + oh * 512:fc * D + (oh + 1) * 512],
                                            start=(fc == 0), stop=False)
                        inst = t.matmul(psY[tc % 2][:, oh * 512:(oh + 1) * 512],
                                        C["ones_lb"][:], b2b[0:1, oh * 512:(oh + 1) * 512],
                                        start=False, stop=True)
                        inst.then_inc(cx.sem["mm"], 1)

        @blk.scalar
        def _(a):
            cx.wait_snap(a, bar)
            for e in range(NE):
                for tt in range(8):
                    for fc in range(32):
                        gi = (e * 8 + tt) * 32 + fc
                        pp = (e * 32 + tt * 4 + fc // 8)
                        a.wait_ge(cx.sem["mm"], mmh(e, tt, fc))
                        if pp >= 2:
                            pe2 = (pp - 2) // 32
                            rem = (pp - 2) % 32
                            a.wait_ge(cx.sem["qg"], qg_piece(pe2, rem // 4, rem % 4))
                        a.activation(hst[pp % 2][:, (fc % 8) * 512:(fc % 8 + 1) * 512],
                                     ps4[fc % 4][:], AF.Gelu,
                                     bias=b1sb[e % 2][:, fc:fc + 1]).then_inc(cx.sem["ac"], 1)

        @blk.vector
        def _(v):
            cx.wait_snap(v, bar)
            v.wait_ge(cx.sem["qs"], q0 + 16)
            for e in range(NE):
                for tc in range(32):
                    v.wait_ge(cx.sem["mm"], mmy(e, tc, 1))
                    gh = e * 32 + tc
                    if gh >= 2:
                        v.wait_ge(cx.sem["qg"], qg_acc((gh - 2) // 32, (gh - 2) % 32))
                    v.tensor_scalar(tmp[tc % 2][:], psY[tc % 2][:],
                                    msal[:, tc * 8 + e:tc * 8 + e + 1], None, ALU.mult
                                    ).then_inc(cx.sem["ve"], 1)

        cx.n["qs"] = q0 + 16 + NE * 40 * 16
        cx.n["qg"] = g0 + (16 + NE * 80) * 16
        cx.n["qo"] = o0 + NE * 2 * 16
        cx.n["mm"] = m0 + NE * MME
        cx.n["ac"] = a0 + NE * 256
        cx.n["ve"] = v0 + NE * 32
        cx.marks["pre_s6_bar"] = bar
        cx.marks["selt_done"] = lambda tc: qg_acc(NE - 1, tc)


STAGE_FNS = ["s1_ln0", "s2_qkv", "s2b_norms", "s3_attn", "s4_attnout", "s5_route",
             "s6_moe", "s7_combine0", "s8_inproj", "s9_conv",
             "s10_ssd", "s11_gate", "s12_ffn1"]


def _build():
    nc = bass.Bass("TRN2", target_bir_lowering=False)
    es = ExitStack()
    with es:
        cx = Ctx(nc, es)
        declare(nc, cx, {"outt": "ExternalOutput"})
        s0_prelude(cx)
        for fn in STAGE_FNS:
            globals()[fn](cx)
    return nc


def kernel(**inputs):
    sh, xs = prep_inputs(inputs)
    nc = _build()
    in_maps = [{**sh, "x_tok": xs[c]} for c in range(8)]
    res = run_bass_kernel_spmd(nc, in_maps, core_ids=list(range(8)))
    out = np.stack([np.asarray(res.results[c]["outt"]) for c in range(8)], axis=0)
    return out.astype(np.float32)
